# version 02
# baseline (speedup 1.0000x reference)
import numpy as np
import ml_dtypes
from contextlib import ExitStack

import concourse.bass as bass
import concourse.mybir as mybir
from concourse.bass import IndirectOffsetOnAxis
from concourse.bass_utils import run_bass_kernel_spmd

F32 = mybir.dt.float32
BF16 = mybir.dt.bfloat16
I32 = mybir.dt.int32
U8 = mybir.dt.uint8
AF = mybir.ActivationFunctionType
ALU = mybir.AluOpType
AX = mybir.AxisListType
NPBF = ml_dtypes.bfloat16

NCORES = 8
T = 8192
D = 2048
PT = T // NCORES
NH = 16
HPC = NH // NCORES
HD = 128
GLA_SCALE = float(HD) ** -0.5
L1_SCALE = 1.0 / float(np.sqrt(np.float32(128.0)))
CHUNK = 32
LN_EPS = 1e-5
RMS0_EPS = 1e-5
RMS1_EPS = 1e-6
SEL_K = 1024
MINI_PT = SEL_K // NCORES
FFN = 8192
C1_GELU = 0.7978845608028654
C2_GELU = 0.7978845608028654 * 0.044715

_DT = {F32: np.float32, BF16: NPBF, I32: np.int32}


class Eng:
    _NOINC = {"wait_ge", "wait_op", "nop", "sem_inc", "drain"}
    _DMA = {"dma_start", "indirect_dma_start", "dma_start_transpose"}

    def __init__(self, raw, sem, dma_sems=()):
        self.raw = raw
        self.sem = sem
        self.n = 0
        self.d = [[h, 0] for h in dma_sems]

    def __getattr__(self, name):
        f = getattr(self.raw, name)
        if not callable(f):
            return f
        if name in Eng._NOINC:
            return f
        if name in Eng._DMA or name == "collective_compute":
            def g(*a, ch=0, **kw):
                instr = f(*a, **kw)
                instr.then_inc(self.d[ch][0], 16)
                self.d[ch][1] += 16
                return instr
            return g

        def g(*a, **kw):
            instr = f(*a, **kw)
            instr.then_inc(self.sem, 1)
            self.n += 1
            return instr
        return g

    def wait(self, *producers):
        for o in producers:
            if o.n:
                self.raw.wait_ge(o.sem, o.n)
            for h, cnt in o.d:
                if cnt:
                    self.raw.wait_ge(h, cnt)

    def wait_ev(self, o, n):
        if n:
            self.raw.wait_ge(o.sem, n)

    def wait_dch(self, o, ch, n=None):
        h, cnt = o.d[ch]
        n = cnt if n is None else n
        if n:
            self.raw.wait_ge(h, n)

    def dmark(self, ch=0):
        return self.d[ch][1]


class Ph:
    def __init__(self, name):
        self.nc = bass.Bass("TRN2", target_bir_lowering=False)
        self.stack = ExitStack()
        self.scopes = [self.stack]
        self.name = name
        self.consts = {}
        self._fconsts = {}
        self._uid = 0
        self.ext = None
        es = self.stack.enter_context
        nc = self.nc
        self.V = Eng(nc.vector, es(nc.semaphore("sem_v")))
        self.S = Eng(nc.scalar, es(nc.semaphore("sem_s")))
        self.P = Eng(nc.tensor, es(nc.semaphore("sem_p")))
        self.G = Eng(nc.gpsimd, es(nc.semaphore("sem_g")),
                     [es(nc.semaphore("sem_gd%d" % i)) for i in range(4)])
        self.Y = Eng(nc.sync, es(nc.semaphore("sem_y")),
                     [es(nc.semaphore("sem_yd%d" % i)) for i in range(4)])
        for v in (LN_EPS, RMS1_EPS):
            self.fconst(v)

    def scope(self):
        from contextlib import contextmanager

        @contextmanager
        def _s():
            es = ExitStack()
            self.scopes.append(es)
            try:
                yield
            finally:
                es.close()
                self.scopes.pop()
        return _s()

    def io(self, name, shape, dtype, kind):
        if self.ext is not None and name in self.ext:
            return self.ext[name]
        return self.nc.dram_tensor(name, list(shape), dtype, kind=kind).ap()

    def sbuf(self, name, shape, dtype):
        self._uid += 1
        return self.scopes[-1].enter_context(
            self.nc.sbuf_tensor("%s_u%d" % (name, self._uid), list(shape), dtype))

    def psum(self, name, shape, dtype=F32):
        self._uid += 1
        return self.scopes[-1].enter_context(
            self.nc.psum_tensor("%s_u%d" % (name, self._uid), list(shape), dtype))

    def fconst(self, val, dtype=F32):
        key = (dtype, float(val))
        if key in self._fconsts:
            return self._fconsts[key]
        self._uid += 1
        t = self.stack.enter_context(
            self.nc.sbuf_tensor("fc_u%d" % self._uid, [128, 1], dtype))
        self.nc.vector.memset(t[:], float(val))
        self._fconsts[key] = t[:]
        try:
            self.nc.const_aps.register(dtype, float(val), t[:])
        except Exception:
            pass
        return t[:]

    def const_in(self, name, arr):
        if name in self.consts:
            return self._const_aps[name]
        arr = np.ascontiguousarray(arr)
        t = self.nc.dram_tensor(name, list(arr.shape),
                                mybir.dt.from_np(arr.dtype), kind="ExternalInput")
        self.consts[name] = arr
        if not hasattr(self, "_const_aps"):
            self._const_aps = {}
        self._const_aps[name] = t.ap()
        return t.ap()

    def close(self):
        self.stack.close()


def _const_identity_bf16():
    return np.eye(128, dtype=NPBF)


def _const_identity_f32():
    return np.eye(128, dtype=np.float32)


def _const_neg_u32_blockdiag():
    m = np.zeros((128, 128), np.float32)
    for c in range(4):
        for t in range(32):
            for t2 in range(t, 32):
                m[c * 32 + t, c * 32 + t2] = -1.0
    return m.astype(NPBF)


def _const_mask32():
    return np.triu(np.ones((32, 32), np.float32))


def _ap(x):
    return x[:] if hasattr(x, "shape") and not isinstance(x, bass.AP) else x


def emit_ln_tile(ph, x_ap, out_ap, mean, rstd, scr, ssq, s1, eps_ap, n_feat,
                 rms=False):
    V, S = ph.V, ph.S
    mean, rstd, scr, ssq, s1 = _ap(mean), _ap(rstd), _ap(scr), _ap(ssq), _ap(s1)
    S.wait(V)
    S.activation(scr, x_ap, AF.Square, accum_out=ssq)
    if not rms:
        V.reduce_sum(s1, x_ap, axis=AX.X)
        V.wait(V)
        V.tensor_scalar(mean, s1, 1.0 / n_feat, None, op0=ALU.mult)
        V.tensor_tensor(s1, s1, s1, op=ALU.mult)
        V.wait(V); V.wait(S)
        V.scalar_tensor_tensor(ssq, s1, -1.0 / n_feat, ssq, op0=ALU.mult, op1=ALU.add)
        S.wait(V)
    S.wait(S)
    S.activation(ssq, ssq, AF.Ln, bias=eps_ap, scale=1.0 / n_feat)
    S.wait(S)
    S.activation(rstd, ssq, AF.Exp, scale=-0.5)
    V.wait(S)
    if rms:
        V.tensor_scalar(out_ap, x_ap, rstd, None, op0=ALU.mult)
    else:
        V.tensor_scalar(out_ap, x_ap, mean, rstd, op0=ALU.subtract, op1=ALU.mult)


def emit_transpose_set(ph, src_tm, out_cols, tps, id_sb, nchunk, evac_marks,
                       in_slices=None):
    P, V = ph.P, ph.V
    for c in range(nchunk):
        buf = c % 2
        if evac_marks.get(buf):
            P.wait_ev(V, evac_marks[buf])
        src = in_slices[c] if in_slices is not None else \
            src_tm[:, c * 128:(c + 1) * 128]
        P.transpose(tps[buf][:], src, id_sb[:])
        V.wait(P)
        V.tensor_copy(out_cols[c], tps[buf][:])
        evac_marks[buf] = V.n


def build_ln0(tiles=8, n_feat=D, name="ln0", ph=None):
    if ph is None:
        ph = Ph(name)
    nc = ph.nc
    x = ph.io("x", [tiles * 128, n_feat], F32, "ExternalInput")
    xn_tm = ph.io("xn_tm", [tiles * 128, n_feat], BF16, "ExternalOutput")
    xn_fm = ph.io("xn_fm", [n_feat, tiles * 128], BF16, "ExternalOutput")
    ident = ph.const_in("ident", _const_identity_bf16())
    nchunk = n_feat // 128

    with ph.scope():
        x_sb = ph.sbuf("x_sb", [128, 2 * n_feat], F32)
        xn_sb = ph.sbuf("xn_sb", [128, 2 * n_feat], BF16)
        fm_sb = ph.sbuf("fm_sb", [128, nchunk * tiles * 128], BF16)
        id_sb = ph.sbuf("id_sb", [128, 128], BF16)
        scr = ph.sbuf("scr", [128, n_feat], BF16)
        mean = ph.sbuf("mean", [128, 1], F32)
        rstd = ph.sbuf("rstd", [128, 1], F32)
        ssq = ph.sbuf("ssq", [128, 1], F32)
        s1 = ph.sbuf("s1", [128, 1], F32)
        tp0 = ph.psum("tp0", [128, 128], BF16)
        tp1 = ph.psum("tp1", [128, 128], BF16)
        eps_ap = ph.fconst(LN_EPS)
        V, S, P, G, Y = ph.V, ph.S, ph.P, ph.G, ph.Y

        with nc.Block() as blk:
            @blk.vector
            def _(_v):
                G.dma_start(id_sb[:], ident)
                P.wait(G)
                evac_marks = {}
                st_done = [0, 0]
                pe_done = [0, 0]
                for i in range(tiles):
                    b = i % 2
                    xb = x_sb[:, b * n_feat:(b + 1) * n_feat]
                    xnb = xn_sb[:, b * n_feat:(b + 1) * n_feat]
                    if i >= 2:
                        Y.wait_dch(Y, 2 + b, st_done[b])
                    Y.dma_start(xb, x[i * 128:(i + 1) * 128, :], ch=b)
                    V.wait(Y); S.wait(Y)
                    V.wait(V)
                    if i >= 2:
                        V.wait_ev(P, pe_done[b])
                    emit_ln_tile(ph, xb, xnb, mean, rstd, scr, ssq, s1,
                                 eps_ap, n_feat)
                    nc.sync.wait_ge(V.sem, V.n)
                    Y.dma_start(xn_tm[i * 128:(i + 1) * 128, :], xnb, ch=2 + b)
                    st_done[b] = Y.dmark(2 + b)
                    P.wait(V)
                    out_cols = [fm_sb[:, c * tiles * 128 + i * 128:
                                      c * tiles * 128 + (i + 1) * 128]
                                for c in range(nchunk)]
                    emit_transpose_set(ph, xnb, out_cols, [tp0, tp1],
                                       id_sb, nchunk, evac_marks)
                    pe_done[b] = P.n
                nc.sync.wait_ge(V.sem, V.n)
                Y.dma_start(xn_fm.rearrange("(c p) t -> p c t", p=128),
                            fm_sb[:].rearrange("p (c t) -> p c t", t=tiles * 128),
                            ch=2)
                Y.wait(Y); G.wait(G)
        return ph


def emit_weight_cast_load(ph, w_dram, w_sb, nrows, ncols, eng=None):
    G = eng or ph.G
    nch = nrows // 128
    G.dma_start(
        w_sb[:].rearrange("p (c n) -> p c n", n=ncols) if nch > 1 else w_sb[:],
        w_dram.rearrange("(c p) n -> p c n", p=128) if nch > 1 else w_dram)


def emit_bias_row_load(ph, b_dram, b_sb, n):
    ph.G.dma_start(b_sb[0:1, :], b_dram[None, :])


def emit_mm_tm(ph, psum_ap, act_fm_cols, w_sb_cols, kchunks, first=True,
               bias_row=None, ones_col=None):
    P = ph.P
    n = len(kchunks)
    for j, k in enumerate(kchunks):
        P.matmul(psum_ap, act_fm_cols(k), w_sb_cols(k),
                 start=(first and j == 0), stop=False)
    if bias_row is not None:
        P.matmul(psum_ap, ones_col, bias_row, start=False, stop=True)
    else:
        pass


def build_glaep(tiles=PT // 128, name="glaep", ph=None, nrk=1):
    if ph is None:
        ph = Ph(name)
    nc = ph.nc
    o = ph.io("o", [nrk * tiles * 128, D // nrk], BF16, "ExternalInput")
    wo = ph.io("wo", [D, D], F32, "ExternalInput")
    gw = ph.io("gw", [D], F32, "ExternalInput")
    s_tm = ph.io("s_tm", [tiles * 128, D], BF16, "ExternalOutput")
    ident = ph.const_in("ident", _const_identity_bf16())
    eps_ap = ph.fconst(RMS0_EPS)
    V, S, P, G, Y = ph.V, ph.S, ph.P, ph.G, ph.Y
    nch = D // 128

    with ph.scope():
        wo_sb = ph.sbuf("wo_sb", [128, nch * D], BF16)
        gw_sb = ph.sbuf("gw_sb", [128, nch], F32)
        id_sb = ph.sbuf("id_sb", [128, 128], BF16)
        o_sb = ph.sbuf("o_sb", [128, 2 * D], BF16)
        of_sb = ph.sbuf("of_sb", [128, D], BF16)
        st_sb = ph.sbuf("st_sb", [128, 2 * D], BF16)
        scr = ph.sbuf("scr", [128, D], BF16)
        rstd = ph.sbuf("rstd", [128, 1], F32)
        ssq = ph.sbuf("ssq", [128, 1], F32)
        tp0 = ph.psum("tp0", [128, 128], BF16)
        tp1 = ph.psum("tp1", [128, 128], BF16)
        mm0 = ph.psum("mm0", [128, 512], F32)
        mm1 = ph.psum("mm1", [128, 512], F32)

        with nc.Block() as blk:
            @blk.vector
            def _(_v):
                G.dma_start(id_sb[:], ident)
                emit_weight_cast_load(ph, wo, wo_sb, D, D)
                G.dma_start(gw_sb[:], gw.rearrange("(c p) -> p c", p=128), allow_slow_non_contiguous=True)
                V.wait(G)
                for c in range(nch):
                    V.tensor_scalar(wo_sb[:, c * D:(c + 1) * D],
                                    wo_sb[:, c * D:(c + 1) * D],
                                    gw_sb[:, c:c + 1], None, op0=ALU.mult)
                P.wait(V); S.wait(V)
                evac_marks = {}
                st_done = [0, 0]
                mmev = {}
                for i in range(tiles):
                    b = i % 2
                    ob = o_sb[:, b * D:(b + 1) * D]
                    stb = st_sb[:, b * D:(b + 1) * D]
                    if i >= 2:
                        Y.wait_dch(Y, 2 + b, st_done[b])
                    if nrk == 1:
                        Y.dma_start(ob, o[i * 128:(i + 1) * 128, :], ch=b)
                    else:
                        for r in range(nrk):
                            Y.dma_start(ob[:, r * (D // nrk):(r + 1) * (D // nrk)],
                                        o[r * tiles * 128 + i * 128:
                                          r * tiles * 128 + (i + 1) * 128, :],
                                        ch=b)
                    V.wait(Y); S.wait(Y)
                    V.wait(V); S.wait(S)
                    emit_ln_tile(ph, ob, ob, None, rstd, scr, ssq, None,
                                 eps_ap, D, rms=True)
                    P.wait(V)
                    out_cols = [of_sb[:, c * 128:(c + 1) * 128] for c in range(nch)]
                    emit_transpose_set(ph, ob, out_cols, [tp0, tp1], id_sb, nch,
                                       evac_marks)
                    for nt in range(4):
                        ps = (mm0, mm1)[nt % 2]
                        if nt >= 2 or i > 0:
                            P.wait_ev(V, mmev.get(nt % 2, 0))
                        P.wait(V)
                        for kc in range(nch):
                            P.matmul(ps[:], of_sb[:, kc * 128:(kc + 1) * 128],
                                     wo_sb[:, kc * D + nt * 512: kc * D + (nt + 1) * 512],
                                     start=(kc == 0), stop=(kc == nch - 1))
                        V.wait(P)
                        V.tensor_copy(stb[:, nt * 512:(nt + 1) * 512], ps[:])
                        mmev[nt % 2] = V.n
                    V.wait(V)
                    nc.sync.wait_ge(V.sem, V.n)
                    Y.dma_start(s_tm[i * 128:(i + 1) * 128, :], stb, ch=2 + b)
                    st_done[b] = Y.dmark(2 + b)
                Y.wait(Y); G.wait(G)
        return ph


def build_combine_ln(tiles=PT // 128, eps=LN_EPS, name="combine", ph=None):
    if ph is None:
        ph = Ph(name)
    nc = ph.nc
    n_feat = D
    x = ph.io("x", [tiles * 128, n_feat], F32, "ExternalInput")
    s_in = ph.io("s_in", [tiles * 128, n_feat], BF16, "ExternalInput")
    g_in = ph.io("g_in", [tiles * 128, n_feat], BF16, "ExternalInput")
    x1 = ph.io("x1", [tiles * 128, n_feat], F32, "ExternalOutput")
    xn_tm = ph.io("xn_tm", [tiles * 128, n_feat], BF16, "ExternalOutput")
    xn_fm = ph.io("xn_fm", [n_feat, tiles * 128], BF16, "ExternalOutput")
    ident = ph.const_in("ident", _const_identity_bf16())
    nchunk = n_feat // 128

    with ph.scope():
        x_sb = ph.sbuf("x_sb", [128, 2 * n_feat], F32)
        sg_sb = ph.sbuf("sg_sb", [128, 2 * n_feat], BF16)
        gg_sb = ph.sbuf("gg_sb", [128, 2 * n_feat], BF16)
        xn_sb = ph.sbuf("xn_sb", [128, 2 * n_feat], BF16)
        fm_sb = ph.sbuf("fm_sb", [128, nchunk * tiles * 128], BF16)
        id_sb = ph.sbuf("id_sb", [128, 128], BF16)
        scr = ph.sbuf("scr", [128, n_feat], BF16)
        sgf = ph.sbuf("sgf", [128, n_feat], F32)
        mean = ph.sbuf("mean", [128, 1], F32)
        rstd = ph.sbuf("rstd", [128, 1], F32)
        ssq = ph.sbuf("ssq", [128, 1], F32)
        s1 = ph.sbuf("s1", [128, 1], F32)
        tp0 = ph.psum("tp0", [128, 128], BF16)
        tp1 = ph.psum("tp1", [128, 128], BF16)
        eps_ap = ph.fconst(eps)
        V, S, P, G, Y = ph.V, ph.S, ph.P, ph.G, ph.Y

        with nc.Block() as blk:
            @blk.vector
            def _(_v):
                G.dma_start(id_sb[:], ident)
                P.wait(G)
                evac_marks = {}
                st_done = [0, 0]
                pe_done = [0, 0]
                for i in range(tiles):
                    b = i % 2
                    xb = x_sb[:, b * n_feat:(b + 1) * n_feat]
                    sb = sg_sb[:, b * n_feat:(b + 1) * n_feat]
                    gb = gg_sb[:, b * n_feat:(b + 1) * n_feat]
                    xnb = xn_sb[:, b * n_feat:(b + 1) * n_feat]
                    if i >= 2:
                        Y.wait_dch(Y, 2 + b, st_done[b])
                    Y.dma_start(xb, x[i * 128:(i + 1) * 128, :], ch=b)
                    Y.dma_start(sb, s_in[i * 128:(i + 1) * 128, :], ch=b)
                    Y.dma_start(gb, g_in[i * 128:(i + 1) * 128, :], ch=b)
                    V.wait(Y)
                    V.wait(V)
                    if i >= 2:
                        V.wait_ev(P, pe_done[b])
                    V.tensor_tensor(sgf[:], sb, gb, op=ALU.mult)
                    V.wait(V)
                    V.tensor_tensor(xb, xb, sgf[:], op=ALU.add)
                    V.wait(V)
                    nc.sync.wait_ge(V.sem, V.n)
                    Y.dma_start(x1[i * 128:(i + 1) * 128, :], xb, ch=2 + b)
                    S.wait(V)
                    emit_ln_tile(ph, xb, xnb, mean, rstd, scr, ssq, s1,
                                 eps_ap, n_feat)
                    nc.sync.wait_ge(V.sem, V.n)
                    Y.dma_start(xn_tm[i * 128:(i + 1) * 128, :], xnb, ch=2 + b)
                    st_done[b] = Y.dmark(2 + b)
                    P.wait(V)
                    out_cols = [fm_sb[:, c * tiles * 128 + i * 128:
                                      c * tiles * 128 + (i + 1) * 128]
                                for c in range(nchunk)]
                    emit_transpose_set(ph, xnb, out_cols, [tp0, tp1],
                                       id_sb, nchunk, evac_marks)
                    pe_done[b] = P.n
                nc.sync.wait_ge(V.sem, V.n)
                Y.dma_start(xn_fm.rearrange("(c p) t -> p c t", p=128),
                            fm_sb[:].rearrange("p (c t) -> p c t", t=tiles * 128),
                            ch=2)
                Y.wait(Y); G.wait(G)
        return ph


def build_moe(tiles=PT // 128, name="moe", ph=None):
    if ph is None:
        ph = Ph(name)
    nc = ph.nc
    E = 8
    xn_fm = ph.io("xn_fm", [D, tiles * 128], BF16, "ExternalInput")
    rw = ph.io("rw", [D, E], F32, "ExternalInput")
    rb = ph.io("rb", [E], F32, "ExternalInput")
    ew = ph.io("ew", [E, D, D], F32, "ExternalInput")
    eb = ph.io("eb", [E, D], F32, "ExternalInput")
    ow = ph.io("ow", [D, D], F32, "ExternalInput")
    ob = ph.io("ob", [D], F32, "ExternalInput")
    g_tm = ph.io("g_tm", [tiles * 128, D], BF16, "ExternalOutput")
    ident = ph.const_in("ident", _const_identity_bf16())
    onesc = ph.const_in("onesc", np.ones((1, 128), NPBF))
    nch = D // 128
    TL = tiles * 128

    with ph.scope():
        xnf = ph.sbuf("xnf", [128, nch * TL], BF16)
        rw_sb = ph.sbuf("rw_sb", [128, nch * E], BF16)
        rb_sb = ph.sbuf("rb_sb", [1, E], BF16)
        eb_sb = ph.sbuf("eb_sb", [1, 2 * 1024], BF16)
        ob_sb = ph.sbuf("ob_sb", [1, 2 * 1024], BF16)
        wx = [ph.sbuf("wx0", [128, nch * 1024], BF16),
              ph.sbuf("wx1", [128, nch * 1024], BF16)]
        ones_sb = ph.sbuf("ones_sb", [1, 128], BF16)
        id_sb = ph.sbuf("id_sb", [128, 128], BF16)
        z_sb = ph.sbuf("z_sb", [128, E], F32)
        t8 = ph.sbuf("t8", [128, E], F32)
        nmax = ph.sbuf("nmax", [128, 1], F32)
        den = ph.sbuf("den", [128, 1], F32)
        selm = ph.sbuf("selm", [128, E], F32)
        hg = ph.sbuf("hg", [128, tiles * E], F32)
        comb = ph.sbuf("comb", [128, tiles * D], BF16)
        cfm = ph.sbuf("cfm", [128, D], BF16)
        gt = ph.sbuf("gt", [128, 2 * D], BF16)
        gel = [[ph.sbuf("gel%d_%d" % (b, j), [128, 512], F32) for j in range(3)]
               for b in range(2)]
        tp0 = ph.psum("tp0", [128, 128], BF16)
        tp1 = ph.psum("tp1", [128, 128], BF16)
        pr = ph.psum("pr", [128, 8], F32)
        pm = [ph.psum("pm%d" % b, [128, 512], F32) for b in range(4)]
        V, S, P, G, Y = ph.V, ph.S, ph.P, ph.G, ph.Y
        c1 = 2.0 * C1_GELU
        c2 = 2.0 * C2_GELU

        with nc.Block() as blk:
            @blk.vector
            def _(_v):
                G.dma_start(id_sb[:], ident)
                G.dma_start(ones_sb[:], onesc)
                G.dma_start(xnf[:].rearrange("p (c t) -> p c t", t=TL),
                            xn_fm.rearrange("(c p) t -> p c t", p=128))
                G.dma_start(rw_sb[:].rearrange("p (c n) -> p c n", n=E),
                            rw.rearrange("(c p) n -> p c n", p=128))
                G.dma_start(rb_sb[:], rb[None, :])
                V.memset(comb[:], 0.0)
                P.wait(G); V.wait(G); S.wait(G)
                for i in range(tiles):
                    P.wait(V)
                    V.wait(S)
                    for c in range(nch):
                        P.matmul(pr[:], xnf[:, c * TL + i * 128: c * TL + (i + 1) * 128],
                                 rw_sb[:, c * E:(c + 1) * E],
                                 start=(c == 0), stop=False)
                    P.matmul(pr[:], ones_sb[:], rb_sb[:], start=False, stop=True)
                    V.wait(P)
                    V.reduce_max(nmax[:], pr[:], axis=AX.X, negate=True)
                    V.wait(V); S.wait(V)
                    S.activation(z_sb[:], pr[:], AF.Exp, bias=nmax[:])
                    V.wait(S)
                    V.max(out=t8[:], in_=z_sb[:])
                    V.wait(V)
                    V.tensor_tensor(den[:], t8[:, 0:1], t8[:, 1:2], op=ALU.add)
                    V.wait(V)
                    V.reciprocal(den[:], den[:])
                    V.wait(V)
                    V.tensor_scalar(selm[:], z_sb[:], t8[:, 1:2], None, op0=ALU.is_ge)
                    V.wait(V)
                    V.tensor_tensor(selm[:], z_sb[:], selm[:], op=ALU.mult)
                    V.wait(V)
                    V.tensor_scalar(hg[:, i * E:(i + 1) * E], selm[:], den[:], None,
                                    op0=ALU.mult)
                wx_rd = [0, 0]
                ld_done = [0, 0]
                for ei in range(2 * E + 1):
                    if ei < 2 * E:
                        e, half = divmod(ei, 2)
                        buf = ei % 2
                        if ei >= 2:
                            G.raw.wait_ge(P.sem, wx_rd[buf])
                        G.dma_start(wx[buf][:].rearrange("p (c n) -> p c n", n=1024),
                                    ew[e].rearrange("(c p) n -> p c n", p=128)
                                    [:, :, half * 1024:(half + 1) * 1024], ch=1 + buf)
                        G.dma_start(eb_sb[0:1, buf * 1024:(buf + 1) * 1024],
                                    eb[e][None, half * 1024:(half + 1) * 1024], ch=1 + buf)
                        ld_done[buf] = G.dmark(1 + buf)
                    if ei >= 1:
                        e, half = divmod(ei - 1, 2)
                        buf = (ei - 1) % 2
                        P.wait_dch(G, 1 + buf, ld_done[buf])
                        for i in range(tiles):
                            for nt in range(2):
                                pb = pm[2 * (buf) + nt]
                                gl = gel[buf]
                                P.wait(V)
                                for c in range(nch):
                                    P.matmul(pb[:],
                                             xnf[:, c * TL + i * 128: c * TL + (i + 1) * 128],
                                             wx[buf][:, c * 1024 + nt * 512:
                                                     c * 1024 + (nt + 1) * 512],
                                             start=(c == 0), stop=False)
                                P.matmul(pb[:], ones_sb[:],
                                         eb_sb[0:1, buf * 1024 + nt * 512:
                                               buf * 1024 + (nt + 1) * 512],
                                         start=False, stop=True)
                                S.wait(P); S.wait(V)
                                S.activation(gl[0][:], pb[:], AF.Square)
                                V.wait(S)
                                V.tensor_scalar(gl[1][:], gl[0][:], c2, c1,
                                                op0=ALU.mult, op1=ALU.add)
                                V.wait(V)
                                V.tensor_tensor(gl[1][:], gl[1][:], pb[:], op=ALU.mult)
                                S.wait(V)
                                S.activation(gl[2][:], gl[1][:], AF.Sigmoid)
                                V.wait(S)
                                V.scalar_tensor_tensor(gl[2][:], gl[2][:],
                                                       hg[:, i * E + e: i * E + e + 1],
                                                       pb[:], op0=ALU.mult, op1=ALU.mult)
                                V.wait(V)
                                cs = comb[:, i * D + half * 1024 + nt * 512:
                                          i * D + half * 1024 + (nt + 1) * 512]
                                V.tensor_tensor(cs, cs, gl[2][:], op=ALU.add)
                        wx_rd[buf] = P.n
                G.raw.wait_ge(P.sem, P.n)
                G.dma_start(wx[0][:].rearrange("p (c n) -> p c n", n=1024),
                            ow.rearrange("(c p) n -> p c n", p=128)[:, :, 0:1024], ch=1)
                G.dma_start(wx[1][:].rearrange("p (c n) -> p c n", n=1024),
                            ow.rearrange("(c p) n -> p c n", p=128)[:, :, 1024:2048], ch=2)
                G.dma_start(ob_sb[0:1, 0:1024], ob[None, 0:1024], ch=1)
                G.dma_start(ob_sb[0:1, 1024:2048], ob[None, 1024:2048], ch=2)
                P.wait(G); V.wait(G)
                evac_marks = {}
                st_done = [0, 0]
                mmev = {}
                for i in range(tiles):
                    b = i % 2
                    gb = gt[:, b * D:(b + 1) * D]
                    if i >= 2:
                        S.wait_dch(Y, b, st_done[b])
                    V.wait(V)
                    out_cols = [cfm[:, c * 128:(c + 1) * 128] for c in range(nch)]
                    if i > 0:
                        P.wait(V)
                    in_sl = [comb[:, i * D + c * 128: i * D + (c + 1) * 128]
                             for c in range(nch)]
                    emit_transpose_set(ph, None, out_cols, [tp0, tp1], id_sb, nch,
                                       evac_marks, in_slices=in_sl)
                    for nt in range(4):
                        pb = pm[nt % 2]
                        P.wait_ev(S, mmev.get(nt % 2, 0))
                        P.wait(V)
                        wxh = wx[nt // 2]
                        off = (nt % 2) * 512
                        for c in range(nch):
                            P.matmul(pb[:], cfm[:, c * 128:(c + 1) * 128],
                                     wxh[:, c * 1024 + off: c * 1024 + off + 512],
                                     start=(c == 0), stop=False)
                        P.matmul(pb[:], ones_sb[:],
                                 ob_sb[0:1, nt * 512:(nt + 1) * 512],
                                 start=False, stop=True)
                        S.wait(P)
                        S.activation(gb[:, nt * 512:(nt + 1) * 512], pb[:], AF.Silu)
                        mmev[nt % 2] = S.n
                    nc.sync.wait_ge(S.sem, S.n)
                    Y.dma_start(g_tm[i * 128:(i + 1) * 128, :], gb, ch=b)
                    st_done[b] = Y.dmark(b)
                Y.wait(Y); G.wait(G)
        return ph


def build_gla(TQ=2048, Ttot=T, name="gla", stage=3, ph=None, src_blk=None):
    if ph is None:
        ph = Ph(name)
    nc = ph.nc
    NQ = Ttot // TQ
    NG = TQ // 128
    if src_blk is None:
        src_blk = Ttot
    nsb = Ttot // src_blk
    xn_fm = ph.io("xn_fm", [nsb * D, src_blk], BF16, "ExternalInput")
    wq = ph.io("wq", [D, 256], F32, "ExternalInput")
    wk = ph.io("wk", [D, 256], F32, "ExternalInput")
    wi = ph.io("wi", [D, 256], F32, "ExternalInput")
    o_tm = ph.io("o_tm", [Ttot, 256], BF16, "ExternalOutput")
    ident = ph.const_in("ident", _const_identity_bf16())
    negu = ph.const_in("negu", _const_neg_u32_blockdiag())
    mask32 = ph.const_in("mask32", _const_mask32())
    nch = D // 128

    with ph.scope():
        xnf = ph.sbuf("xnf", [128, nch * TQ], BF16)
        wq_sb = ph.sbuf("wq_sb", [128, nch * 256], BF16)
        wk_sb = ph.sbuf("wk_sb", [128, nch * 256], BF16)
        wi_sb = ph.sbuf("wi_sb", [128, nch * 256], BF16)
        id_sb = ph.sbuf("id_sb", [128, 128], BF16)
        negu_sb = ph.sbuf("negu_sb", [128, 128], BF16)
        m32_sb = ph.sbuf("m32_sb", [32, 32], F32)
        qf = ph.sbuf("qf", [128, 2 * TQ], BF16)
        kf = ph.sbuf("kf", [128, 2 * TQ], BF16)
        spf = ph.sbuf("spf", [128, 2 * TQ], BF16)
        vt = ph.sbuf("vt", [128, NG * 256], BF16)
        sa1 = ph.sbuf("sa1", [128, 512], F32)
        sr1 = ph.sbuf("sr1", [128, 512], F32)
        sp_tm = ph.sbuf("sp_tm", [128, 256], BF16)
        eG = ph.sbuf("eG", [128, 2 * 128], F32)
        qt_sb = ph.sbuf("qt_sb", [128, 256], BF16)
        kc_sb = ph.sbuf("kc_sb", [128, 256], BF16)
        khf_sb = ph.sbuf("khf_sb", [128, 256], BF16)
        kht_sb = ph.sbuf("kht_sb", [128, 256], BF16)
        Pt_sb = ph.sbuf("Pt_sb", [128, 2 * 32], BF16)
        S32 = ph.sbuf("S32", [128, 2 * 128], F32)
        Sbf = ph.sbuf("Sbf", [128, 2 * 128], BF16)
        o_q = ph.sbuf("o_q", [128, NG * 256], BF16)
        pmA = ph.psum("pmA", [128, 512], F32)
        pmB = ph.psum("pmB", [128, 512], F32)
        tpA = ph.psum("tpA", [128, 128], BF16)
        tpB = ph.psum("tpB", [128, 128], BF16)
        pP = ph.psum("pP", [128, 32], F32)
        po = ph.psum("po", [128, 128], F32)
        pS = ph.psum("pS", [128, 128], F32)
        V, S, P, G, Y = ph.V, ph.S, ph.P, ph.G, ph.Y

        with nc.Block() as blk:
            @blk.vector
            def _(_v):
                G.dma_start(id_sb[:], ident)
                G.dma_start(negu_sb[:], negu)
                G.dma_start(m32_sb[:], mask32)
                emit_weight_cast_load(ph, wq, wq_sb, D, 256)
                emit_weight_cast_load(ph, wk, wk_sb, D, 256)
                emit_weight_cast_load(ph, wi, wi_sb, D, 256)
                V.memset(S32[:], 0.0)
                V.memset(Sbf[:], 0.0)
                P.wait(G); V.wait(G); S.wait(G)

                for qi in range(NQ):
                    V.wait(P)
                    G.raw.wait_ge(P.sem, P.n)
                    xnfv = xn_fm.rearrange("(r c p) t -> p r c t", p=128,
                                           c=D // 128)
                    for off in range(0, TQ, min(TQ, src_blk)):
                        gt = qi * TQ + off
                        r, tl = gt // src_blk, gt % src_blk
                        w = min(TQ, src_blk)
                        G.dma_start(xnf[:].rearrange("p (c t) -> p c t", t=TQ)
                                    [:, :, off:off + w],
                                    xnfv[:, r, :, tl:tl + w], ch=1)
                    P.wait(G); S.wait(G); V.wait(G)
                    S.wait(Y)
                    PW = min(512, TQ)
                    for h in range(2):
                        for tt in range(TQ // PW):
                            sl = slice(tt * PW, (tt + 1) * PW)
                            pb = (pmA, pmB)[tt % 2]
                            P.wait(S); P.wait(V)
                            for c in range(nch):
                                P.matmul(pb[:, 0:PW], wq_sb[:, c * 256 + h * 128: c * 256 + (h + 1) * 128],
                                         xnf[:, c * TQ + tt * PW: c * TQ + (tt + 1) * PW],
                                         start=(c == 0), stop=(c == nch - 1))
                            S.wait(P)
                            S.activation(qf[:, h * TQ:(h + 1) * TQ][:, sl], pb[:, 0:PW],
                                         AF.Relu, scale=GLA_SCALE)
                            P.wait(S)
                            for c in range(nch):
                                P.matmul(pb[:, 0:PW], wk_sb[:, c * 256 + h * 128: c * 256 + (h + 1) * 128],
                                         xnf[:, c * TQ + tt * PW: c * TQ + (tt + 1) * PW],
                                         start=(c == 0), stop=(c == nch - 1))
                            S.wait(P); V.wait(P)
                            S.activation(kf[:, h * TQ:(h + 1) * TQ][:, sl], pb[:, 0:PW], AF.Sigmoid)
                            S.wait(S)
                            S.activation(sa1[:, 0:PW], pb[:, 0:PW], AF.Abs)
                            S.activation(sr1[:, 0:PW], pb[:, 0:PW], AF.Relu)
                            S.wait(S)
                            S.activation(sa1[:, 0:PW], sa1[:, 0:PW], AF.Exp, scale=-1.0)
                            S.wait(S)
                            S.activation(sa1[:, 0:PW], sa1[:, 0:PW], AF.Ln, bias=1.0)
                            V.wait(S); V.wait(V)
                            V.tensor_tensor(spf[:, h * TQ:(h + 1) * TQ][:, sl],
                                            sr1[:, 0:PW], sa1[:, 0:PW], op=ALU.add)
                            V.wait(V)
                    vev = {}
                    for tt in range(NG):
                        pb, pbi = (pmA, 0) if (tt % 2 == 0) else (pmB, 1)
                        P.wait_ev(V, vev.get(pbi, 0))
                        P.wait(S); P.wait(V)
                        for c in range(nch):
                            P.matmul(pb[:, 0:256],
                                     xnf[:, c * TQ + tt * 128: c * TQ + (tt + 1) * 128],
                                     wi_sb[:, c * 256:(c + 1) * 256],
                                     start=(c == 0), stop=(c == nch - 1))
                        V.wait(P)
                        V.tensor_copy(vt[:, tt * 256:(tt + 1) * 256], pb[:, 0:256])
                        vev[pbi] = V.n
                    for gi in range(NG if stage >= 1 else 0):
                        gsl = slice(gi * 128, (gi + 1) * 128)
                        P.wait(V)
                        em = {}
                        emit_transpose_set(ph, None,
                                           [sp_tm[:, 0:128], sp_tm[:, 128:256]],
                                           [tpA, tpB], id_sb, 2, em,
                                           in_slices=[spf[:, h * TQ + gi * 128:
                                                          h * TQ + (gi + 1) * 128]
                                                      for h in range(2)])
                        for h in range(2 if stage >= 2 else 0):
                            pg = (pmA, pmB)[h]
                            P.wait(V); P.wait(S)
                            P.matmul(pg[:, 0:128], sp_tm[:, h * 128:(h + 1) * 128],
                                     negu_sb[:], start=True, stop=True)
                            S.wait(P); S.wait(V)
                            S.activation(eG[:, h * 128:(h + 1) * 128], pg[:, 0:128], AF.Exp)
                            S.wait(S)
                            S.activation(sr1[:, 0:128], pg[:, 0:128], AF.Exp, scale=-1.0)
                            V.wait(S)
                            V.tensor_tensor(qt_sb[:, h * 128:(h + 1) * 128],
                                            qf[:, h * TQ:(h + 1) * TQ][:, gsl],
                                            eG[:, h * 128:(h + 1) * 128], op=ALU.mult)
                            V.tensor_tensor(kc_sb[:, h * 128:(h + 1) * 128],
                                            kf[:, h * TQ:(h + 1) * TQ][:, gsl],
                                            sr1[:, 0:128], op=ALU.mult)
                            V.wait(V)
                            eslice = eG[:, h * 128:(h + 1) * 128][:, 31::32]
                            V.tensor_tensor(
                                khf_sb[:, h * 128:(h + 1) * 128]
                                .rearrange("p (c i) -> p c i", i=32),
                                kc_sb[:, h * 128:(h + 1) * 128]
                                .rearrange("p (c i) -> p c i", i=32),
                                eslice[:, :, None].broadcast_to([128, 4, 32]),
                                op=ALU.mult)
                            V.wait(V)
                            em2 = {}
                            P.wait(V)
                            emit_transpose_set(ph, None,
                                               [kht_sb[:, h * 128:(h + 1) * 128]],
                                               [tpA, tpB], id_sb, 1, em2,
                                               in_slices=[khf_sb[:, h * 128:(h + 1) * 128]])
                            for ci in range(4 if stage >= 3 else 0):
                                rs = slice(32 * ci, 32 * ci + 32)
                                P.wait(V); P.wait(S)
                                P.matmul(pP[rs, 0:32],
                                         kc_sb[:, h * 128 + 32 * ci: h * 128 + 32 * ci + 32],
                                         qt_sb[:, h * 128 + 32 * ci: h * 128 + 32 * ci + 32],
                                         start=True, stop=True,
                                         tile_position=(0, 32 * ci))
                                V.wait(P)
                                V.tensor_tensor(Pt_sb[rs, h * 32:(h + 1) * 32],
                                                pP[rs, 0:32], m32_sb[:], op=ALU.mult)
                                P.wait(V)
                                P.matmul(po[rs, :],
                                         qt_sb[:, h * 128 + 32 * ci: h * 128 + 32 * ci + 32],
                                         Sbf[:, h * 128:(h + 1) * 128],
                                         start=True, stop=False, tile_position=(0, 32 * ci))
                                P.matmul(po[rs, :],
                                         Pt_sb[rs, h * 32:(h + 1) * 32],
                                         vt[rs, gi * 256 + h * 128: gi * 256 + (h + 1) * 128],
                                         start=False, stop=True,
                                         tile_position=(32 * ci, 32 * ci))
                                P.matmul(pS[:],
                                         kht_sb[rs, h * 128:(h + 1) * 128],
                                         vt[rs, gi * 256 + h * 128: gi * 256 + (h + 1) * 128],
                                         start=True, stop=True,
                                         tile_position=(32 * ci, 0))
                                S.wait(P)
                                S.activation(o_q[rs, gi * 256 + h * 128:
                                                 gi * 256 + (h + 1) * 128],
                                             po[rs, :], AF.Copy)
                                V.wait(P)
                                V.scalar_tensor_tensor(
                                    S32[:, h * 128:(h + 1) * 128],
                                    S32[:, h * 128:(h + 1) * 128],
                                    eG[:, h * 128 + 32 * ci + 31: h * 128 + 32 * ci + 32],
                                    pS[:], op0=ALU.mult, op1=ALU.add)
                                V.wait(V)
                                V.tensor_copy(Sbf[:, h * 128:(h + 1) * 128],
                                              S32[:, h * 128:(h + 1) * 128])
                                V.wait(V)
                    nc.sync.wait_ge(S.sem, S.n)
                    Y.dma_start(o_tm.rearrange("(tt p) n -> p tt n", p=128)
                                [:, qi * NG:(qi + 1) * NG, :],
                                o_q[:].rearrange("p (tt n) -> p tt n", n=256))
                Y.wait(Y); G.wait(G)
        return ph


def _const_l1_rope():
    half = 16
    inv_freq = 1.0 / 10000.0 ** (np.arange(0, half, 2, dtype=np.float32) / half)
    t = np.arange(T, dtype=np.float32) / 40.0
    freqs = t[:, None] * inv_freq[None]
    emb = np.concatenate([freqs, freqs], -1)
    return np.cos(emb).astype(np.float32), np.sin(emb).astype(np.float32)


def _const_maskT():
    m = np.zeros((128, 128), np.float32)
    tk = np.arange(128)[:, None]
    tq = np.arange(128)[None, :]
    m[tk > tq] = -1e30
    return m


def build_l1proj(tiles=PT // 128, name="l1proj", ph=None):
    if ph is None:
        ph = Ph(name)
    nc = ph.nc
    TL = tiles * 128
    nch = D // 128
    xn_fm = ph.io("xn_fm", [D, TL], BF16, "ExternalInput")
    wdkv = ph.io("wdkv", [D, 512], F32, "ExternalInput")
    bdkv = ph.io("bdkv", [512], F32, "ExternalInput")
    wdq = ph.io("wdq", [D, 512], F32, "ExternalInput")
    bdq = ph.io("bdq", [512], F32, "ExternalInput")
    wr = ph.io("wr", [D], F32, "ExternalInput")
    ckv = ph.io("ckv", [512, TL], BF16, "ExternalOutput")
    cq = ph.io("cq", [512, TL], BF16, "ExternalOutput")
    identf_l1p = ph.const_in("identf", _const_identity_f32())
    logits = ph.io("logits", [tiles * 128], F32, "ExternalOutput")

    with ph.scope():
        xnf = ph.sbuf("xnf", [128, nch * TL], BF16)
        wkv_sb = ph.sbuf("wkv_sb", [128, nch * 512], BF16)
        wq_sb = ph.sbuf("wq_sb", [128, nch * 512], BF16)
        bkv_sb = ph.sbuf("bkv_sb", [128, 4], F32)
        bq_sb = ph.sbuf("bq_sb", [128, 4], F32)
        wr_sb = ph.sbuf("wr_sb", [128, nch], BF16)
        ck_sb = ph.sbuf("ck_sb", [128, 4 * TL], BF16)
        cq_sb = ph.sbuf("cq_sb", [128, 4 * TL], BF16)
        lg_sb = ph.sbuf("lg_sb", [128, tiles], F32)
        lgT_sb = ph.sbuf("lgT_sb", [tiles, 128], F32)
        idf2_sb = ph.sbuf("idf2_sb", [128, 128], F32)
        pm = [ph.psum("pm%d" % b, [128, 512], F32) for b in range(2)]
        pl = ph.psum("pl", [128, 1], F32)
        plT = ph.psum("plT", [128, 128], F32)
        V, S, P, G, Y = ph.V, ph.S, ph.P, ph.G, ph.Y

        with nc.Block() as blk:
            @blk.vector
            def _(_v):
                G.dma_start(xnf[:].rearrange("p (c t) -> p c t", t=TL),
                            xn_fm.rearrange("(c p) t -> p c t", p=128))
                emit_weight_cast_load(ph, wdkv, wkv_sb, D, 512)
                emit_weight_cast_load(ph, wdq, wq_sb, D, 512)
                G.dma_start(bkv_sb[:], bdkv.rearrange("(c p) -> p c", p=128),
                            allow_slow_non_contiguous=True)
                G.dma_start(bq_sb[:], bdq.rearrange("(c p) -> p c", p=128),
                            allow_slow_non_contiguous=True)
                G.dma_start(wr_sb[:], wr.rearrange("(c p) -> p c", p=128),
                            allow_slow_non_contiguous=True)
                P.wait(G); S.wait(G); V.wait(G)
                for which, wsb, bsb, csb in ((0, wkv_sb, bkv_sb, ck_sb),
                                             (1, wq_sb, bq_sb, cq_sb)):
                    for cc in range(4):
                        for ts in range(TL // 512):
                            pb = pm[ts % 2]
                            P.wait(V)
                            for c in range(nch):
                                P.matmul(pb[:],
                                         wsb[:, c * 512 + cc * 128: c * 512 + (cc + 1) * 128],
                                         xnf[:, c * TL + ts * 512: c * TL + (ts + 1) * 512],
                                         start=(c == 0), stop=(c == nch - 1))
                            V.wait(P)
                            V.tensor_tensor(csb[:, cc * TL + ts * 512: cc * TL + (ts + 1) * 512],
                                            pb[:], bsb[:, cc:cc + 1].broadcast_to([128, 512]),
                                            op=ALU.add)
                for i in range(tiles):
                    P.wait(V)
                    for c in range(nch):
                        P.matmul(pl[:], xnf[:, c * TL + i * 128: c * TL + (i + 1) * 128],
                                 wr_sb[:, c:c + 1], start=(c == 0), stop=(c == nch - 1))
                    V.wait(P)
                    V.tensor_copy(lg_sb[:, i:i + 1], pl[:])
                nc.sync.wait_ge(S.sem, S.n)
                nc.sync.wait_ge(V.sem, V.n)
                Y.dma_start(ckv.rearrange("(c p) t -> p c t", p=128),
                            ck_sb[:].rearrange("p (c t) -> p c t", t=TL))
                Y.dma_start(cq.rearrange("(c p) t -> p c t", p=128),
                            cq_sb[:].rearrange("p (c t) -> p c t", t=TL))
                Y.dma_start(logits.rearrange("(f p) -> f p", p=128), lgT_sb[:])
                Y.wait(Y); G.wait(G)
        return ph


def build_l1flash(Ttot=T, name="l1flash", ph=None, src_blk=None):
    if ph is None:
        ph = Ph(name)
    nc = ph.nc
    NT = Ttot // 128
    if src_blk is None:
        src_blk = Ttot
    nsb = Ttot // src_blk
    ckv = ph.io("ckv", [nsb * 512, src_blk], BF16, "ExternalInput")
    cqf = ph.io("cqf", [nsb * 512, src_blk], BF16, "ExternalInput")
    xn_fm = ph.io("xn_fm", [nsb * D, src_blk], BF16, "ExternalInput")
    wuk = ph.io("wuk", [512, 192], F32, "ExternalInput")
    buk = ph.io("buk", [192], F32, "ExternalInput")
    wuv = ph.io("wuv", [512, 256], F32, "ExternalInput")
    buv = ph.io("buv", [256], F32, "ExternalInput")
    wuq = ph.io("wuq", [512, 192], F32, "ExternalInput")
    buq = ph.io("buq", [192], F32, "ExternalInput")
    wqr = ph.io("wqr", [512, 64], F32, "ExternalInput")
    bqr = ph.io("bqr", [64], F32, "ExternalInput")
    wkr = ph.io("wkr", [D, 64], F32, "ExternalInput")
    bkr = ph.io("bkr", [64], F32, "ExternalInput")
    o_tm = ph.io("o_tm", [Ttot, 256], BF16, "ExternalOutput")
    ident = ph.const_in("ident", _const_identity_bf16())
    identf = ph.const_in("identf", _const_identity_f32())
    cos_c, sin_c = _const_l1_rope()
    cosc = ph.const_in("cosc", cos_c[:Ttot])
    sinc = ph.const_in("sinc", sin_c[:Ttot])
    maskT = ph.const_in("maskT", _const_maskT())
    onesb = ph.const_in("onesb", np.ones((1, 128), NPBF))
    onescol = ph.const_in("onescol", np.ones((128, 1), NPBF))
    onesf = ph.const_in("onesf", np.ones((1, 128), np.float32))
    nch = D // 128
    SC = float(L1_SCALE)

    with ph.scope():
        kfull = [ph.sbuf("kfull%d" % h, [128, Ttot], BF16) for h in range(2)]
        qfull = [ph.sbuf("qfull%d" % h, [128, Ttot], BF16) for h in range(2)]
        vsb = [ph.sbuf("vsb%d" % h, [128, Ttot], BF16) for h in range(2)]
        ot_sb = ph.sbuf("ot_sb", [128, (Ttot // 128) * 256], BF16)
        cstr = ph.sbuf("cstr", [128, 2 * 8 * 128], BF16)
        xstr = ph.sbuf("xstr", [128, 2 * nch * 128], BF16)
        wuk_sb = ph.sbuf("wuk_sb", [128, 4 * 192], BF16)
        wuq_sb = ph.sbuf("wuq_sb", [128, 4 * 192], BF16)
        wuv_sb = ph.sbuf("wuv_sb", [128, 4 * 256], BF16)
        wqr_sb = ph.sbuf("wqr_sb", [128, 4 * 64], BF16)
        wkr_sb = ph.sbuf("wkr_sb", [128, nch * 64], BF16)
        buk_sb = ph.sbuf("buk_sb", [96, 2], F32)
        buq_sb = ph.sbuf("buq_sb", [96, 2], F32)
        buv_sb = ph.sbuf("buv_sb", [1, 256], BF16)
        bqr_sb = ph.sbuf("bqr_sb", [1, 64], BF16)
        bkr_sb = ph.sbuf("bkr_sb", [1, 64], BF16)
        cos_sb = ph.sbuf("cos_sb", [128, NT * 16], F32)
        sin_sb = ph.sbuf("sin_sb", [128, NT * 16], F32)
        rot_sb = ph.sbuf("rot_sb", [128, 64], BF16)
        rA = ph.sbuf("rA", [128, 16], F32)
        rB = ph.sbuf("rB", [128, 16], F32)
        id_sb = ph.sbuf("id_sb", [128, 128], BF16)
        idf_sb = ph.sbuf("idf_sb", [128, 128], F32)
        mT_sb = ph.sbuf("mT_sb", [128, 128], F32)
        on_sb = ph.sbuf("on_sb", [1, 128], BF16)
        onc_sb = ph.sbuf("onc_sb", [128, 1], BF16)
        onf_sb = ph.sbuf("onf_sb", [1, 128], F32)
        p_sb = [ph.sbuf("p_sb%d" % b, [128, 512], BF16) for b in range(2)]
        mrow = ph.sbuf("mrow", [128, 1], F32)
        m1 = ph.sbuf("m1", [1, 1], F32)
        mbc = ph.sbuf("mbc", [128, 1], F32)
        nbias = ph.sbuf("nbias", [128, 1], F32)
        lsb = ph.sbuf("lsb", [128, 1], F32)
        linv = ph.sbuf("linv", [128, 1], F32)
        otmp = ph.sbuf("otmp", [128, 128], BF16)
        pB = ph.psum("pB", [128, 512], F32)
        po = ph.psum("po", [128, 128], F32)
        pl = ph.psum("pl", [128, 1], F32)
        pmax = ph.psum("pmax", [128, 128], F32)
        tpA = ph.psum("tpA", [128, 128], BF16)
        V, S, P, G, Y = ph.V, ph.S, ph.P, ph.G, ph.Y

        with nc.Block() as blk:
            @blk.vector
            def _(_v):
                G.dma_start(id_sb[:], ident)
                G.dma_start(idf_sb[:], identf)
                G.dma_start(mT_sb[:], maskT)
                G.dma_start(on_sb[:], onesb)
                G.dma_start(onc_sb[:], onescol)
                G.dma_start(onf_sb[:], onesf)
                G.dma_start(cos_sb[:].rearrange("p (t f) -> p t f", f=16),
                            cosc.rearrange("(t p) f -> p t f", p=128))
                G.dma_start(sin_sb[:].rearrange("p (t f) -> p t f", f=16),
                            sinc.rearrange("(t p) f -> p t f", p=128))
                emit_weight_cast_load(ph, wuk, wuk_sb, 512, 192)
                emit_weight_cast_load(ph, wuq, wuq_sb, 512, 192)
                emit_weight_cast_load(ph, wuv, wuv_sb, 512, 256)
                emit_weight_cast_load(ph, wqr, wqr_sb, 512, 64)
                emit_weight_cast_load(ph, wkr, wkr_sb, D, 64)
                G.dma_start(buk_sb[:], buk.rearrange("(h p) -> p h", p=96),
                            allow_slow_non_contiguous=True)
                G.dma_start(buq_sb[:], buq.rearrange("(h p) -> p h", p=96),
                            allow_slow_non_contiguous=True)
                G.dma_start(buv_sb[:], buv[None, :])
                G.dma_start(bqr_sb[:], bqr[None, :])
                G.dma_start(bkr_sb[:], bkr[None, :])
                P.wait(G); S.wait(G); V.wait(G)

                def rotary_tm(src_ap, dst_ap, ti):
                    cs = cos_sb[:, ti * 16:(ti + 1) * 16]
                    sn = sin_sb[:, ti * 16:(ti + 1) * 16]
                    V.wait(V); V.wait(S); V.wait(P)
                    V.tensor_tensor(rA[:], src_ap[:, 0:16], cs, op=ALU.mult)
                    V.tensor_tensor(rB[:, 0:8], src_ap[:, 8:16], sn[:, 0:8], op=ALU.mult)
                    V.tensor_tensor(rB[:, 8:16], src_ap[:, 0:8], sn[:, 8:16], op=ALU.mult)
                    V.wait(V)
                    V.tensor_tensor(dst_ap[:, 0:8], rA[:, 0:8], rB[:, 0:8], op=ALU.subtract)
                    V.tensor_tensor(dst_ap[:, 8:16], rA[:, 8:16], rB[:, 8:16], op=ALU.add)
                    V.wait(V)

                for ti in range(NT):
                    sl = ti % 2
                    xb = xstr[:, sl * nch * 128:(sl + 1) * nch * 128]
                    if ti >= 2:
                        G.raw.wait_ge(P.sem, P.n)
                    rk, tl = (ti * 128) // src_blk, (ti * 128) % src_blk
                    G.dma_start(xb.rearrange("p (c t) -> p c t", t=128),
                                xn_fm.rearrange("(r c p) t -> p r c t", p=128,
                                                c=D // 128)
                                [:, rk, :, tl:tl + 128], ch=1 + sl)
                    P.wait_dch(G, 1 + sl)
                    P.wait(S); P.wait(V)
                    for c in range(nch):
                        P.matmul(pB[:, 0:64], xb[:, c * 128:(c + 1) * 128],
                                 wkr_sb[:, c * 64:(c + 1) * 64],
                                 start=(c == 0), stop=False)
                    P.matmul(pB[:, 0:64], on_sb[:], bkr_sb[:], start=False, stop=True)
                    V.wait(P)
                    V.tensor_copy(rot_sb[:], pB[:, 0:64])
                    for h in range(2):
                        rotary_tm(rot_sb[:, h * 32:(h + 1) * 32],
                                  rot_sb[:, h * 32:(h + 1) * 32], ti)
                    for h in range(2):
                        P.wait(V)
                        P.transpose(tpA[96:128, :], rot_sb[:, h * 32:(h + 1) * 32],
                                    id_sb[:], tile_position=(0, 96))
                        V.wait(P)
                        V.tensor_copy(kfull[h][96:128, ti * 128:(ti + 1) * 128],
                                      tpA[96:128, :])
                G.raw.wait_ge(P.sem, P.n)
                for ti in range(NT):
                    sl = ti % 2
                    cb = cstr[:, sl * 8 * 128:(sl + 1) * 8 * 128]
                    ck_t = cb[:, 0:4 * 128]
                    cq_t = cb[:, 4 * 128:8 * 128]
                    if ti >= 2:
                        G.raw.wait_ge(P.sem, P.n)
                    rk, tl = (ti * 128) // src_blk, (ti * 128) % src_blk
                    G.dma_start(ck_t.rearrange("p (c t) -> p c t", t=128),
                                ckv.rearrange("(r c p) t -> p r c t", p=128, c=4)
                                [:, rk, :, tl:tl + 128], ch=1 + sl)
                    G.dma_start(cq_t.rearrange("p (c t) -> p c t", t=128),
                                cqf.rearrange("(r c p) t -> p r c t", p=128, c=4)
                                [:, rk, :, tl:tl + 128], ch=1 + sl)
                    P.wait_dch(G, 1 + sl)
                    for h in range(2):
                        P.wait(S); P.wait(V)
                        for c in range(4):
                            P.matmul(pB[0:96, 0:128],
                                     wuk_sb[:, c * 192 + h * 96: c * 192 + (h + 1) * 96],
                                     ck_t[:, c * 128:(c + 1) * 128],
                                     start=(c == 0), stop=(c == 3))
                        V.wait(P)
                        V.tensor_tensor(kfull[h][0:96, ti * 128:(ti + 1) * 128],
                                        pB[0:96, 0:128],
                                        buk_sb[:, h:h + 1].broadcast_to([96, 128]),
                                        op=ALU.add)
                        for c in range(4):
                            P.matmul(pmax[0:96, 0:128],
                                     wuq_sb[:, c * 192 + h * 96: c * 192 + (h + 1) * 96],
                                     cq_t[:, c * 128:(c + 1) * 128],
                                     start=(c == 0), stop=(c == 3))
                        V.wait(P)
                        V.tensor_tensor(qfull[h][0:96, ti * 128:(ti + 1) * 128],
                                        pmax[0:96, 0:128],
                                        buq_sb[:, h:h + 1].broadcast_to([96, 128]),
                                        op=ALU.add)
                        P.wait(S); P.wait(V)
                        for c in range(4):
                            P.matmul(po[:], ck_t[:, c * 128:(c + 1) * 128],
                                     wuv_sb[:, c * 256 + h * 128: c * 256 + (h + 1) * 128],
                                     start=(c == 0), stop=False)
                        P.matmul(po[:], on_sb[:],
                                 buv_sb[0:1, h * 128:(h + 1) * 128],
                                 start=False, stop=True)
                        V.wait(P)
                        V.tensor_copy(vsb[h][:, ti * 128:(ti + 1) * 128], po[:])
                        for c in range(4):
                            P.matmul(pl[:] if False else pB[:, 256:288],
                                     cq_t[:, c * 128:(c + 1) * 128],
                                     wqr_sb[:, c * 64 + h * 32: c * 64 + (h + 1) * 32],
                                     start=(c == 0), stop=False)
                        P.matmul(pB[:, 256:288], on_sb[:],
                                 bqr_sb[0:1, h * 32:(h + 1) * 32],
                                 start=False, stop=True)
                        V.wait(P)
                        V.tensor_copy(rot_sb[:, 0:32], pB[:, 256:288])
                        rotary_tm(rot_sb[:, 0:32], rot_sb[:, 0:32], ti)
                        P.wait(V)
                        P.transpose(tpA[96:128, :], rot_sb[:, 0:32],
                                    id_sb[:], tile_position=(0, 96))
                        V.wait(P)
                        V.tensor_copy(qfull[h][96:128, ti * 128:(ti + 1) * 128],
                                      tpA[96:128, :])
                for h in range(2):
                    for qt in range(NT):
                        nf = qt // 4
                        dc = qt - 4 * nf + 1
                        qsl = slice(qt * 128, (qt + 1) * 128)
                        P.wait(S); P.wait(V)
                        for jj in range(dc):
                            tk = 4 * nf + jj
                            P.matmul(pB[:, jj * 128:(jj + 1) * 128],
                                     kfull[h][:, tk * 128:(tk + 1) * 128],
                                     qfull[h][:, qsl], start=True, stop=True)
                        V.wait(P)
                        V.tensor_tensor(pB[:, (dc - 1) * 128: dc * 128],
                                        pB[:, (dc - 1) * 128: dc * 128],
                                        mT_sb[:], op=ALU.add)
                        V.wait(V)
                        V.reduce_max(mrow[:], pB[:, 0:dc * 128], axis=AX.X)
                        P.wait(V)
                        P.transpose(pmax[0:1, :], mrow[:], idf_sb[:])
                        V.wait(P)
                        V.reduce_max(m1[:], pmax[0:1, 0:128], axis=AX.X)
                        V.wait(V)
                        P.wait(V)
                        P.matmul(pmax[:, 0:1], onf_sb[:], m1[:], start=True, stop=True)
                        V.wait(P)
                        V.tensor_copy(mbc[:], pmax[:, 0:1])
                        V.wait(V)
                        V.tensor_scalar(nbias[:], mbc[:], -SC, None, op0=ALU.mult)
                        S.wait(V)
                        S.activation(p_sb[0][:, 0:dc * 128], pB[:, 0:dc * 128],
                                     AF.Exp, bias=nbias[:], scale=SC)
                        P.wait(S)
                        for jj in range(dc):
                            tk = 4 * nf + jj
                            last = (nf == 0 and jj == dc - 1)
                            P.matmul(pl[:], p_sb[0][:, jj * 128:(jj + 1) * 128],
                                     onc_sb[:], start=(jj == 0), stop=last)
                            P.matmul(po[:], p_sb[0][:, jj * 128:(jj + 1) * 128],
                                     vsb[h][:, tk * 128:(tk + 1) * 128],
                                     start=(jj == 0), stop=last)
                        pev = [S.n, 0]
                        for kb in range(nf):
                            buf = kb % 2
                            P.wait_ev(S, pev[buf])
                            P.wait(S)
                            for jj in range(4):
                                tk = 4 * kb + jj
                                P.matmul(pB[:, jj * 128:(jj + 1) * 128],
                                         kfull[h][:, tk * 128:(tk + 1) * 128],
                                         qfull[h][:, qsl], start=True, stop=True)
                            S.wait(P)
                            S.activation(p_sb[buf][:], pB[:], AF.Exp,
                                         bias=nbias[:], scale=SC)
                            pev[buf] = S.n
                            P.wait(S)
                            for jj in range(4):
                                tk = 4 * kb + jj
                                last = (kb == nf - 1 and jj == 3)
                                P.matmul(pl[:], p_sb[buf][:, jj * 128:(jj + 1) * 128],
                                         onc_sb[:], start=False, stop=last)
                                P.matmul(po[:], p_sb[buf][:, jj * 128:(jj + 1) * 128],
                                         vsb[h][:, tk * 128:(tk + 1) * 128],
                                         start=False, stop=last)
                        V.wait(P)
                        V.tensor_copy(lsb[:], pl[:])
                        V.wait(V)
                        V.reciprocal(linv[:], lsb[:])
                        V.wait(V)
                        V.tensor_tensor(ot_sb[:, qt * 256 + h * 128:
                                              qt * 256 + (h + 1) * 128],
                                        po[:],
                                        linv[:].broadcast_to([128, 128]),
                                        op=ALU.mult)
                nc.sync.wait_ge(V.sem, V.n)
                Y.dma_start(o_tm.rearrange("(tt p) n -> p tt n", p=128),
                            ot_sb[:].rearrange("p (tt n) -> p tt n", n=256))
                Y.wait(Y); G.wait(G)
        return ph


def _const_su_ones():
    m = np.zeros((128, 128), np.float32)
    p = np.arange(128)[:, None]
    i = np.arange(128)[None, :]
    m[p < i] = 1.0
    return m.astype(NPBF)


def build_l1sel(name="l1sel", ph=None):
    if ph is None:
        ph = Ph(name)
    nc = ph.nc
    NF = T // 128
    lgi = ph.io("logits", [T], F32, "ExternalInput")
    xn1 = ph.io("xn1_tm", [T, D], BF16, "ExternalInput")
    iotaj = ph.io("iotaj", [128, 128], F32, "ExternalInput")
    fx = ph.io("fx", [128, D], BF16, "ExternalOutput")
    selidx = ph.io("selidx", [128], I32, "ExternalOutput")
    rw = ph.io("rw", [128], F32, "ExternalOutput")
    iota_c = ph.const_in("iota_c",
                         (np.arange(128)[:, None] * NF +
                          np.arange(NF)[None, :]).astype(np.float32))
    suones = ph.const_in("suones", _const_su_ones())
    onesf = ph.const_in("onesf", np.ones((1, 128), np.float32))
    onescf = ph.const_in("onescf", np.ones((128, 1), np.float32))

    with ph.scope():
        lg = ph.sbuf("lg", [128, NF], F32)
        iota_sb = ph.sbuf("iota_sb", [128, NF], F32)
        su_sb = ph.sbuf("su_sb", [128, 128], BF16)
        onf_sb = ph.sbuf("onf_sb", [1, 128], F32)
        oncf_sb = ph.sbuf("oncf_sb", [128, 1], F32)
        idf_sb = ph.sbuf("idf_sb", [128, 128], F32)
        identf = ph.const_in("identf", _const_identity_f32())
        lo = ph.sbuf("lo", [128, 1], F32)
        hi = ph.sbuf("hi", [128, 1], F32)
        mid = ph.sbuf("mid", [128, 1], F32)
        selv = ph.sbuf("selv", [128, 1], F32)
        d1 = ph.sbuf("d1", [128, 1], F32)
        rowc = ph.sbuf("rowc", [128, 1], F32)
        rowcb = ph.sbuf("rowcb", [128, 1], BF16)
        cnt1 = ph.sbuf("cnt1", [1, 1], F32)
        cntb = ph.sbuf("cntb", [128, 1], F32)
        gmx = ph.sbuf("gmx", [128, 1], F32)
        ngmx = ph.sbuf("ngmx", [128, 1], F32)
        cmp = ph.sbuf("cmp", [128, NF], F32)
        mask = ph.sbuf("mask", [128, NF], F32)
        incl = ph.sbuf("incl", [128, NF], F32)
        slots = ph.sbuf("slots", [128, NF], F32)
        sltmp = ph.sbuf("sltmp", [128, NF], F32)
        slot_i = ph.sbuf("slot_i", [128, NF], I32)
        rwe = ph.sbuf("rwe", [128, NF], F32)
        pair_sb = ph.sbuf("pair_sb", [128, 2 * NF], F32)
        base_sb = ph.sbuf("base_sb", [128, 1], F32)
        mypair = ph.sbuf("mypair", [128, 2], F32)
        myidx = ph.sbuf("myidx", [128, 1], I32)
        iotaj_sb = ph.sbuf("iotaj_sb", [128, 128], F32)
        Mf = ph.sbuf("Mf", [128, 128], F32)
        myrw = ph.sbuf("myrw", [128, 1], F32)
        fx_sb = ph.sbuf("fx_sb", [128, D], BF16)
        selp = ph.sbuf("selp", [128, SEL_K // 128], F32)
        seli = ph.sbuf("seli", [128, SEL_K // 128], I32)
        rwn = ph.sbuf("rwn", [128, SEL_K // 128], F32)
        den = ph.sbuf("den", [128, 1], F32)
        pq = ph.psum("pq", [128, 128], F32)
        V, S, P, G, Y = ph.V, ph.S, ph.P, ph.G, ph.Y

        def cross_part_bcast(src_col, op):
            P.wait(V)
            P.transpose(pq[0:1, :], src_col, idf_sb[:])
            V.wait(P)
            V.tensor_reduce(cnt1[:], pq[0:1, 0:128], axis=AX.X, op=op)
            V.wait(V)
            P.wait(V)
            P.matmul(pq[:, 0:1], onf_sb[:], cnt1[:], start=True, stop=True)
            V.wait(P)
            V.tensor_copy(cntb[:], pq[:, 0:1])
            V.wait(V)

        with nc.Block() as blk:
            @blk.vector
            def _(_v):
                G.dma_start(lg[:], lgi.rearrange("(p f) -> p f", f=NF))
                G.dma_start(iota_sb[:], iota_c)
                G.dma_start(su_sb[:], suones)
                G.dma_start(onf_sb[:], onesf)
                G.dma_start(oncf_sb[:], onescf)
                G.dma_start(idf_sb[:], identf)
                G.dma_start(iotaj_sb[:], iotaj)
                P.wait(G); S.wait(G); V.wait(G)
                V.reduce_max(rowc[:], lg[:], axis=AX.X)
                cross_part_bcast(rowc[:], ALU.max)
                V.tensor_copy(gmx[:], cntb[:])
                V.tensor_scalar(hi[:], cntb[:], 1.0, None, op0=ALU.add)
                V.tensor_scalar(ngmx[:], cntb[:], -1.0, None, op0=ALU.mult)
                V.wait(V)
                V.tensor_reduce(rowc[:], lg[:], axis=AX.X, op=ALU.min)
                cross_part_bcast(rowc[:], ALU.min)
                V.tensor_scalar(lo[:], cntb[:], -1.0, None, op0=ALU.add)
                V.wait(V)
                for _it in range(64):
                    V.tensor_tensor(mid[:], lo[:], hi[:], op=ALU.add)
                    V.wait(V)
                    V.tensor_scalar(mid[:], mid[:], 0.5, None, op0=ALU.mult)
                    V.wait(V)
                    V.tensor_scalar(cmp[:], lg[:], mid[:], None, op0=ALU.is_ge)
                    V.wait(V)
                    V.reduce_sum(rowc[:], cmp[:], axis=AX.X)
                    cross_part_bcast(rowc[:], ALU.add)
                    V.tensor_scalar(selv[:], cntb[:], float(SEL_K), None, op0=ALU.is_ge)
                    V.wait(V)
                    V.tensor_tensor(d1[:], mid[:], lo[:], op=ALU.subtract)
                    V.wait(V)
                    V.scalar_tensor_tensor(lo[:], d1[:], selv[:], lo[:],
                                           op0=ALU.mult, op1=ALU.add)
                    V.tensor_tensor(d1[:], hi[:], mid[:], op=ALU.subtract)
                    V.wait(V)
                    V.scalar_tensor_tensor(d1[:], d1[:], selv[:], mid[:],
                                           op0=ALU.mult, op1=ALU.add)
                    V.wait(V)
                    V.tensor_copy(hi[:], d1[:])
                    V.wait(V)
                V.tensor_scalar(mask[:], lg[:], lo[:], None, op0=ALU.is_ge)
                V.wait(V)
                V.tensor_tensor_scan(incl[:], mask[:], mask[:], 0.0,
                                     op0=ALU.add, op1=ALU.bypass)
                V.wait(V)
                V.tensor_tensor(slots[:], incl[:], mask[:], op=ALU.subtract)
                V.tensor_copy(rowcb[:], incl[:, NF - 1:NF])
                V.wait(V)
                P.wait(V)
                P.matmul(pq[:, 0:1], su_sb[:], rowcb[:], start=True, stop=True)
                V.wait(P)
                V.tensor_copy(base_sb[:], pq[:, 0:1])
                V.wait(V)
                V.tensor_scalar(slots[:], slots[:], base_sb[:], None, op0=ALU.add)
                V.wait(V)
                V.tensor_tensor(sltmp[:], slots[:], mask[:], op=ALU.mult)
                V.tensor_scalar(cmp[:], mask[:], -60000.0, 60000.0,
                                op0=ALU.mult, op1=ALU.add)
                V.wait(V)
                V.tensor_tensor(sltmp[:], sltmp[:], cmp[:], op=ALU.add)
                V.wait(V)
                V.tensor_copy(slot_i[:], sltmp[:])
                S.wait(V)
                S.activation(rwe[:], lg[:], AF.Exp, bias=ngmx[:])
                V.wait(S); V.wait(V)
                V.tensor_copy(pair_sb[:].rearrange("p (f c) -> p f c", c=2)[:, :, 0],
                              iota_sb[:])
                V.tensor_copy(pair_sb[:].rearrange("p (f c) -> p f c", c=2)[:, :, 1],
                              rwe[:])
                V.wait(V)
                for f in range(NF):
                    V.wait(P)
                    V.tensor_tensor(Mf[:], sltmp[:, f:f + 1].broadcast_to([128, 128]),
                                    iotaj_sb[:], op=ALU.is_equal)
                    P.wait(V)
                    P.matmul(pq[:, 2:4], Mf[:], pair_sb[:, 2 * f:2 * f + 2],
                             start=(f == 0), stop=(f == NF - 1))
                V.wait(P)
                V.tensor_copy(mypair[:], pq[:, 2:4])
                V.wait(V)
                V.tensor_copy(myidx[:], mypair[:, 0:1])
                V.tensor_tensor(cmp[:], rwe[:], mask[:], op=ALU.mult)
                V.wait(V)
                V.reduce_sum(rowc[:], cmp[:], axis=AX.X)
                cross_part_bcast(rowc[:], ALU.add)
                V.reciprocal(den[:], cntb[:])
                V.wait(V)
                V.tensor_scalar(myrw[:], mypair[:, 1:2], den[:], None, op0=ALU.mult)
                nc.gpsimd.wait_ge(V.sem, V.n)
                G.indirect_dma_start(
                    fx_sb[:], None,
                    xn1, IndirectOffsetOnAxis(myidx[:], 0),
                    bounds_check=T - 1, oob_is_err=False, ch=3)
                Y.wait_dch(G, 3)
                nc.sync.wait_ge(V.sem, V.n)
                Y.dma_start(fx, fx_sb[:], ch=0)
                Y.dma_start(selidx[:, None], myidx[:], ch=1)
                Y.dma_start(rw[:, None], myrw[:], ch=1)
                Y.wait(Y); G.wait(G)
        return ph


def build_mini1(name="mini1", ph=None):
    if ph is None:
        ph = Ph(name)
    nc = ph.nc
    MT = SEL_K // 128
    nch = D // 128
    fxd = ph.io("fx_full", [SEL_K, D], BF16, "ExternalInput")
    g1d = ph.io("g1", [D], F32, "ExternalInput")
    wqd = ph.io("wq", [D, 256], F32, "ExternalInput")
    wkd = ph.io("wk", [D, 256], F32, "ExternalInput")
    wvd = ph.io("wv", [D, 256], F32, "ExternalInput")
    wod = ph.io("wo_r", [256, D], F32, "ExternalInput")
    frd = ph.io("freqs", [SEL_K, 64, 2], F32, "ExternalInput")
    mkd = ph.io("mask", [SEL_K, SEL_K], F32, "ExternalInput")
    part1 = ph.io("part1", [SEL_K, D], F32, "ExternalOutput")
    ident = ph.const_in("ident", _const_identity_bf16())
    eps_ap = ph.fconst(RMS1_EPS)

    with ph.scope():
        fx_sb = ph.sbuf("fx_sb", [128, MT * D], BF16)
        h1f = ph.sbuf("h1f", [128, nch * SEL_K], BF16)
        g1_sb = ph.sbuf("g1_sb", [128, nch], F32)
        wq_sb = ph.sbuf("wq_sb", [128, nch * 256], BF16)
        wk_sb = ph.sbuf("wk_sb", [128, nch * 256], BF16)
        wv_sb = ph.sbuf("wv_sb", [128, nch * 256], BF16)
        wo_sb = ph.sbuf("wo_sb", [128, 2 * D], BF16)
        fr_sb = ph.sbuf("fr_sb", [128, MT * 128], F32)
        q_sb = ph.sbuf("q_sb", [128, MT * 256], BF16)
        k_sb = ph.sbuf("k_sb", [128, MT * 256], BF16)
        v_sb = ph.sbuf("v_sb", [128, MT * 256], BF16)
        qf_sb = ph.sbuf("qf_sb", [128, SEL_K], BF16)
        qf2 = [ph.sbuf("qf2_%d" % h, [128, SEL_K], BF16) for h in range(2)]
        kf2 = [ph.sbuf("kf2_%d" % h, [128, SEL_K], BF16) for h in range(2)]
        ptr_sb = ph.sbuf("ptr_sb", [128, 256], BF16)
        of_sb = ph.sbuf("of_sb", [128, 2 * SEL_K], BF16)
        mk_sb = ph.sbuf("mk_sb", [128, SEL_K], F32)
        p1_sb = ph.sbuf("p1_sb", [128, SEL_K], BF16)
        otm_sb = ph.sbuf("otm_sb", [128, 128], BF16)
        pt_sb = ph.sbuf("pt_sb", [128, D], F32)
        scr = ph.sbuf("scr", [128, D], BF16)
        rstd = ph.sbuf("rstd", [128, 1], F32)
        ssq = ph.sbuf("ssq", [128, 1], F32)
        mrow = ph.sbuf("mrow", [128, 1], F32)
        nbias = ph.sbuf("nbias", [128, 1], F32)
        lsum = ph.sbuf("lsum", [128, 1], F32)
        l2 = ph.sbuf("l2", [128, 1], F32)
        linv = ph.sbuf("linv", [128, 1], F32)
        id_sb = ph.sbuf("id_sb", [128, 128], BF16)
        rA = ph.sbuf("rA", [128, 64], F32)
        rB = ph.sbuf("rB", [128, 64], F32)
        pB = ph.psum("pB", [128, 512], F32)
        pC = ph.psum("pC", [128, 512], F32)
        po = ph.psum("po", [128, 128], F32)
        tpA = ph.psum("tpA", [128, 128], BF16)
        tpB = ph.psum("tpB", [128, 128], BF16)
        V, S, P, G, Y = ph.V, ph.S, ph.P, ph.G, ph.Y

        with nc.Block() as blk:
            @blk.vector
            def _(_v):
                G.dma_start(id_sb[:], ident)
                G.dma_start(fx_sb[:].rearrange("p (t d) -> p t d", d=D),
                            fxd.rearrange("(t p) d -> p t d", p=128))
                G.dma_start(g1_sb[:], g1d.rearrange("(c p) -> p c", p=128),
                            allow_slow_non_contiguous=True)
                emit_weight_cast_load(ph, wqd, wq_sb, D, 256)
                emit_weight_cast_load(ph, wkd, wk_sb, D, 256)
                emit_weight_cast_load(ph, wvd, wv_sb, D, 256)
                G.dma_start(wo_sb[:].rearrange("p (c n) -> p c n", n=D),
                            wod.rearrange("(c p) n -> p c n", p=128))
                G.dma_start(fr_sb[:].rearrange("p (t f) -> p t f", f=128),
                            frd.rearrange("(t p) f c -> p t (f c)", p=128))
                P.wait(G); S.wait(G); V.wait(G)
                for wsb in (wq_sb, wk_sb, wv_sb):
                    for c in range(nch):
                        V.tensor_scalar(wsb[:, c * 256:(c + 1) * 256],
                                        wsb[:, c * 256:(c + 1) * 256],
                                        g1_sb[:, c:c + 1], None, op0=ALU.mult)
                V.wait(V)
                em = {}
                for i in range(MT):
                    fxt = fx_sb[:, i * D:(i + 1) * D]
                    V.wait(V); S.wait(S); S.wait(V); S.wait(P); V.wait(P)
                    emit_ln_tile(ph, fxt, scr[:], None, rstd, scr, ssq, None,
                                 eps_ap, D, rms=True)
                    P.wait(V)
                    out_cols = [h1f[:, c * SEL_K + i * 128: c * SEL_K + (i + 1) * 128]
                                for c in range(nch)]
                    emit_transpose_set(ph, scr, out_cols, [tpA, tpB], id_sb, nch, em)
                for i in range(MT):
                    for wsb, dst in ((wq_sb, q_sb), (wk_sb, k_sb), (wv_sb, v_sb)):
                        P.wait(V)
                        for c in range(nch):
                            P.matmul(pB[:, 0:256],
                                     h1f[:, c * SEL_K + i * 128: c * SEL_K + (i + 1) * 128],
                                     wsb[:, c * 256:(c + 1) * 256],
                                     start=(c == 0), stop=(c == nch - 1))
                        V.wait(P)
                        V.tensor_copy(dst[:, i * 256:(i + 1) * 256], pB[:, 0:256])
                    fr = fr_sb[:, i * 128:(i + 1) * 128].rearrange(
                        "p (f c) -> p f c", c=2)
                    cs, sn = fr[:, :, 0], fr[:, :, 1]
                    for dst in (q_sb, k_sb):
                        for h in range(2):
                            b0 = i * 256 + h * 128
                            x1 = dst[:, b0:b0 + 64]
                            x2 = dst[:, b0 + 64:b0 + 128]
                            V.wait(V); V.wait(P)
                            V.tensor_tensor(rA[:], x1, cs, op=ALU.mult)
                            V.tensor_tensor(rB[:], x2, sn, op=ALU.mult)
                            V.wait(V)
                            V.tensor_tensor(rA[:], rA[:], rB[:], op=ALU.subtract)
                            V.wait(V)
                            V.tensor_tensor(rB[:], x1, sn, op=ALU.mult)
                            V.wait(V)
                            V.tensor_tensor(x2, x2, cs, op=ALU.mult)
                            V.wait(V)
                            V.tensor_tensor(x2, x2, rB[:], op=ALU.add)
                            V.tensor_copy(x1, rA[:])
                            V.wait(V)
                SC = float(L1_SCALE)
                for h in range(2):
                    em2 = {}
                    for i in range(MT):
                        P.wait(V)
                        emit_transpose_set(
                            ph, None,
                            [qf_sb[:, (h * MT + i) * 128:(h * MT + i + 1) * 128]
                             if False else qf2[h][:, i * 128:(i + 1) * 128]],
                            [tpA, tpB], id_sb, 1, em2,
                            in_slices=[q_sb[:, i * 256 + h * 128: i * 256 + (h + 1) * 128]])
                        emit_transpose_set(
                            ph, None,
                            [kf2[h][:, i * 128:(i + 1) * 128]], [tpA, tpB],
                            id_sb, 1, em2,
                            in_slices=[k_sb[:, i * 256 + h * 128: i * 256 + (h + 1) * 128]])
                for qt in range(MT):
                    G.raw.wait_ge(V.sem, V.n)
                    G.raw.wait_ge(S.sem, S.n)
                    G.dma_start(mk_sb[:], mkd[qt * 128:(qt + 1) * 128, :], ch=1)
                    V.wait_dch(G, 1)
                    for h in range(2):
                        P.wait(S); P.wait(V)
                        for half, ps in ((0, pB), (1, pC)):
                            P.matmul(ps[:], qf2[h][:, qt * 128:(qt + 1) * 128],
                                     kf2[h][:, half * 512:(half + 1) * 512],
                                     start=True, stop=True)
                        V.wait(P)
                        for half, ps in ((0, pB), (1, pC)):
                            V.scalar_tensor_tensor(
                                ps[:], ps[:], SC, mk_sb[:, half * 512:(half + 1) * 512],
                                op0=ALU.mult, op1=ALU.add)
                        V.wait(V)
                        V.reduce_max(mrow[:], pB[:], axis=AX.X)
                        V.reduce_max(l2[:], pC[:], axis=AX.X)
                        V.wait(V)
                        V.tensor_tensor(mrow[:], mrow[:], l2[:], op=ALU.max)
                        V.wait(V)
                        V.tensor_scalar(nbias[:], mrow[:], -1.0, None, op0=ALU.mult)
                        S.wait(V)
                        S.activation(p1_sb[:, 0:512], pB[:], AF.Exp,
                                     bias=nbias[:], accum_out=lsum[:])
                        S.wait(S)
                        S.activation(p1_sb[:, 512:1024], pC[:], AF.Exp,
                                     bias=nbias[:], accum_out=l2[:])
                        V.wait(S)
                        V.tensor_tensor(lsum[:], lsum[:], l2[:], op=ALU.add)
                        V.wait(V)
                        V.reciprocal(linv[:], lsum[:])
                        P.wait(S); P.wait(V)
                        emA = {}
                        for tk in range(MT):
                            buf = tk % 2
                            tp = (tpA, tpB)[buf]
                            if emA.get(buf):
                                P.wait_ev(V, emA[buf])
                            P.transpose(tp[:], p1_sb[:, tk * 128:(tk + 1) * 128],
                                        id_sb[:])
                            V.wait(P)
                            V.tensor_copy(ptr_sb[:, buf * 128:(buf + 1) * 128], tp[:])
                            emA[buf] = V.n
                            P.wait_ev(V, V.n)
                            P.matmul(po[:], ptr_sb[:, buf * 128:(buf + 1) * 128],
                                     v_sb[:, tk * 256 + h * 128: tk * 256 + (h + 1) * 128],
                                     start=(tk == 0), stop=(tk == MT - 1))
                        V.wait(P)
                        V.tensor_tensor(otm_sb[:], po[:],
                                        linv[:].broadcast_to([128, 128]), op=ALU.mult)
                        P.wait(V)
                        P.transpose(tpA[:], otm_sb[:], id_sb[:])
                        V.wait(P)
                        V.tensor_copy(of_sb[:, h * SEL_K + qt * 128:
                                           h * SEL_K + (qt + 1) * 128], tpA[:])
                        emA[0] = V.n
                for i in range(MT):
                    for nt in range(4):
                        ps = (pB, pC)[nt % 2]
                        P.wait(V)
                        for c in range(2):
                            P.matmul(ps[:],
                                     of_sb[:, c * SEL_K + i * 128: c * SEL_K + (i + 1) * 128],
                                     wo_sb[:, c * D + nt * 512: c * D + (nt + 1) * 512],
                                     start=(c == 0), stop=(c == 1))
                        V.wait(P)
                        V.tensor_copy(pt_sb[:, nt * 512:(nt + 1) * 512], ps[:])
                    V.wait(V)
                    nc.sync.wait_ge(V.sem, V.n)
                    Y.dma_start(part1[i * 128:(i + 1) * 128, :], pt_sb[:], ch=i % 2)
                Y.wait(Y); G.wait(G)
        return ph


def build_mini2(name="mini2", ph=None, pre_summed=False):
    if ph is None:
        ph = Ph(name)
    nc = ph.nc
    parts = ph.io("parts", [1, 128, D] if pre_summed else [NCORES, 128, D],
                  F32, "ExternalInput")
    fxr = ph.io("fx_rows", [128, D], BF16, "ExternalInput")
    xb = ph.io("xb", [128, D], F32, "ExternalOutput")
    h2 = ph.io("h2", [128, D], BF16, "ExternalOutput")
    eps_ap = ph.fconst(RMS1_EPS)
    with ph.scope():
        acc = ph.sbuf("acc", [128, D], F32)
        pin = ph.sbuf("pin", [128, 2 * D], F32)
        fx_sb = ph.sbuf("fx_sb", [128, D], BF16)
        h2_sb = ph.sbuf("h2_sb", [128, D], BF16)
        scr = ph.sbuf("scr", [128, D], BF16)
        rstd = ph.sbuf("rstd", [128, 1], F32)
        ssq = ph.sbuf("ssq", [128, 1], F32)
        V, S, P, G, Y = ph.V, ph.S, ph.P, ph.G, ph.Y
        with nc.Block() as blk:
            @blk.vector
            def _(_v):
                Y.dma_start(fx_sb[:], fxr, ch=2)
                Y.dma_start(pin[:, 0:D], parts[0], ch=0)
                V.wait_dch(Y, 2); V.wait_dch(Y, 0)
                V.tensor_tensor(acc[:], fx_sb[:], pin[:, 0:D], op=ALU.add)
                for j in range(1, 1 if pre_summed else NCORES):
                    b = j % 2
                    if j >= 2:
                        nc.sync.wait_ge(V.sem, V.n)
                    Y.dma_start(pin[:, b * D:(b + 1) * D], parts[j], ch=b)
                    V.wait_dch(Y, b)
                    V.wait(V)
                    V.tensor_tensor(acc[:], acc[:], pin[:, b * D:(b + 1) * D],
                                    op=ALU.add)
                V.wait(V)
                emit_ln_tile(ph, acc[:], h2_sb[:], None, rstd, scr, ssq, None,
                             eps_ap, D, rms=True)
                nc.sync.wait_ge(V.sem, V.n)
                Y.dma_start(xb, acc[:], ch=2)
                Y.dma_start(h2, h2_sb[:], ch=2)
                Y.wait(Y); G.wait(G)
        return ph


def build_mini3(name="mini3", ph=None):
    if ph is None:
        ph = Ph(name)
    nc = ph.nc
    MT = SEL_K // 128
    nch = D // 128
    FS = FFN // NCORES
    h2d = ph.io("h2_full", [SEL_K, D], BF16, "ExternalInput")
    g2d = ph.io("g2", [D], F32, "ExternalInput")
    w1d = ph.io("w1c", [D, FS], F32, "ExternalInput")
    w3d = ph.io("w3c", [D, FS], F32, "ExternalInput")
    w2d = ph.io("w2r", [FS, D], F32, "ExternalInput")
    part2 = ph.io("part2", [SEL_K, D], F32, "ExternalOutput")
    ident = ph.const_in("ident", _const_identity_bf16())

    with ph.scope():
        h2_sb = ph.sbuf("h2_sb", [128, MT * D], BF16)
        h2f = ph.sbuf("h2f", [128, nch * SEL_K], BF16)
        g2_sb = ph.sbuf("g2_sb", [128, nch], F32)
        w1_sb = ph.sbuf("w1_sb", [128, nch * FS], BF16)
        w3_sb = ph.sbuf("w3_sb", [128, nch * FS], BF16)
        w2_sb = ph.sbuf("w2_sb", [128, (FS // 128) * D], BF16)
        u_sb = ph.sbuf("u_sb", [128, MT * FS], BF16)
        uf_sb = ph.sbuf("uf_sb", [128, (FS // 128) * SEL_K], BF16)
        u1_sb = ph.sbuf("u1_sb", [128, 512], BF16)
        pt_sb = ph.sbuf("pt_sb", [128, D], F32)
        id_sb = ph.sbuf("id_sb", [128, 128], BF16)
        pB = ph.psum("pB", [128, 512], F32)
        pC = ph.psum("pC", [128, 512], F32)
        tpA = ph.psum("tpA", [128, 128], BF16)
        tpB = ph.psum("tpB", [128, 128], BF16)
        V, S, P, G, Y = ph.V, ph.S, ph.P, ph.G, ph.Y
        with nc.Block() as blk:
            @blk.vector
            def _(_v):
                G.dma_start(id_sb[:], ident)
                G.dma_start(h2_sb[:].rearrange("p (t d) -> p t d", d=D),
                            h2d.rearrange("(t p) d -> p t d", p=128))
                G.dma_start(g2_sb[:], g2d.rearrange("(c p) -> p c", p=128),
                            allow_slow_non_contiguous=True)
                emit_weight_cast_load(ph, w1d, w1_sb, D, FS)
                emit_weight_cast_load(ph, w3d, w3_sb, D, FS)
                emit_weight_cast_load(ph, w2d, w2_sb, FS, D)
                P.wait(G); S.wait(G); V.wait(G)
                for wsb in (w1_sb, w3_sb):
                    for c in range(nch):
                        V.tensor_scalar(wsb[:, c * FS:(c + 1) * FS],
                                        wsb[:, c * FS:(c + 1) * FS],
                                        g2_sb[:, c:c + 1], None, op0=ALU.mult)
                V.wait(V)
                em = {}
                for i in range(MT):
                    P.wait(V)
                    out_cols = [h2f[:, c * SEL_K + i * 128: c * SEL_K + (i + 1) * 128]
                                for c in range(nch)]
                    emit_transpose_set(ph, h2_sb[:, i * D:(i + 1) * D], out_cols,
                                       [tpA, tpB], id_sb, nch, em)
                for i in range(MT):
                    for st in range(FS // 512):
                        P.wait(S); P.wait(V)
                        for c in range(nch):
                            lt = h2f[:, c * SEL_K + i * 128: c * SEL_K + (i + 1) * 128]
                            P.matmul(pB[:], lt, w1_sb[:, c * FS + st * 512:
                                                      c * FS + (st + 1) * 512],
                                     start=(c == 0), stop=(c == nch - 1))
                        for c in range(nch):
                            lt = h2f[:, c * SEL_K + i * 128: c * SEL_K + (i + 1) * 128]
                            P.matmul(pC[:], lt, w3_sb[:, c * FS + st * 512:
                                                      c * FS + (st + 1) * 512],
                                     start=(c == 0), stop=(c == nch - 1))
                        S.wait(P)
                        S.activation(u1_sb[:], pB[:], AF.Silu)
                        V.wait(S)
                        V.tensor_tensor(u_sb[:, i * FS + st * 512:
                                             i * FS + (st + 1) * 512],
                                        u1_sb[:], pC[:], op=ALU.mult)
                V.wait(V)
                em2 = {}
                for i in range(MT):
                    P.wait(V)
                    out_cols = [uf_sb[:, c * SEL_K + i * 128: c * SEL_K + (i + 1) * 128]
                                for c in range(FS // 128)]
                    emit_transpose_set(ph, u_sb[:, i * FS:(i + 1) * FS], out_cols,
                                       [tpA, tpB], id_sb, FS // 128, em2)
                for i in range(MT):
                    for nt in range(4):
                        ps = (pB, pC)[nt % 2]
                        P.wait(V)
                        for c in range(FS // 128):
                            P.matmul(ps[:],
                                     uf_sb[:, c * SEL_K + i * 128: c * SEL_K + (i + 1) * 128],
                                     w2_sb[:, c * D + nt * 512: c * D + (nt + 1) * 512],
                                     start=(c == 0), stop=(c == FS // 128 - 1))
                        V.wait(P)
                        V.tensor_copy(pt_sb[:, nt * 512:(nt + 1) * 512], ps[:])
                    V.wait(V)
                    nc.sync.wait_ge(V.sem, V.n)
                    Y.dma_start(part2[i * 128:(i + 1) * 128, :], pt_sb[:], ch=i % 2)
                Y.wait(Y); G.wait(G)
        return ph


def build_mini4(name="mini4", ph=None, pre_summed=False):
    if ph is None:
        ph = Ph(name)
    nc = ph.nc
    parts = ph.io("parts2", [1, 128, D] if pre_summed else [NCORES, 128, D],
                  F32, "ExternalInput")
    xbr = ph.io("xb_rows", [128, D], F32, "ExternalInput")
    rwr = ph.io("rw_rows", [128], F32, "ExternalInput")
    xw = ph.io("xw", [128, D], BF16, "ExternalOutput")
    with ph.scope():
        acc = ph.sbuf("acc", [128, D], F32)
        pin = ph.sbuf("pin", [128, 2 * D], F32)
        rw_sb = ph.sbuf("rw_sb", [128, 1], F32)
        xw_sb = ph.sbuf("xw_sb", [128, D], BF16)
        V, S, P, G, Y = ph.V, ph.S, ph.P, ph.G, ph.Y
        with nc.Block() as blk:
            @blk.vector
            def _(_v):
                Y.dma_start(acc[:], xbr, ch=2)
                Y.dma_start(rw_sb[:], rwr[:, None], ch=2)
                V.wait_dch(Y, 2)
                for j in range(1 if pre_summed else NCORES):
                    b = j % 2
                    if j >= 2:
                        nc.sync.wait_ge(V.sem, V.n)
                    Y.dma_start(pin[:, b * D:(b + 1) * D], parts[j], ch=b)
                    V.wait_dch(Y, b)
                    V.wait(V)
                    V.tensor_tensor(acc[:], acc[:], pin[:, b * D:(b + 1) * D],
                                    op=ALU.add)
                V.wait(V)
                V.tensor_scalar(xw_sb[:], acc[:], rw_sb[:], None, op0=ALU.mult)
                nc.sync.wait_ge(V.sem, V.n)
                Y.dma_start(xw, xw_sb[:], ch=2)
                Y.wait(Y); G.wait(G)
        return ph


def build_final(tiles=PT // 128, name="final", ph=None):
    if ph is None:
        ph = Ph(name)
    nc = ph.nc
    TL = tiles * 128
    nch = D // 128
    x1d = ph.io("x1", [TL, D], F32, "ExternalInput")
    xn1d = ph.io("xn1_tm", [TL, D], BF16, "ExternalInput")
    ofd = ph.io("o_a2a", [NCORES * TL, 256], BF16, "ExternalInput")
    wod = ph.io("wout", [D, D], F32, "ExternalInput")
    bod = ph.io("bout", [D], F32, "ExternalInput")
    xwd = ph.io("xw_full", [SEL_K, D], BF16, "ExternalInput")
    seld = ph.io("selidx", [SEL_K], I32, "ExternalInput")
    cbased = ph.io("cbase", [1], F32, "ExternalInput")
    outd = ph.io("out", [TL, D], F32, "ExternalOutput")
    xadd = nc.dram_tensor("xadd", [TL, D], BF16)
    onesc = ph.const_in("onesc", np.ones((1, 128), NPBF))
    ident_fin = ph.const_in("ident", _const_identity_bf16())

    with ph.scope():
        oa_sb = ph.sbuf("oa_sb", [128, D], BF16)
        of_sb = ph.sbuf("of_sb", [128, D], BF16)
        tpA = ph.psum("tpA", [128, 128], BF16)
        tpB = ph.psum("tpB", [128, 128], BF16)
        id_sb = ph.sbuf("id_sb", [128, 128], BF16)
        wo_sb = ph.sbuf("wo_sb", [128, nch * D], BF16)
        bo_sb = ph.sbuf("bo_sb", [1, D], BF16)
        on_sb = ph.sbuf("on_sb", [1, 128], BF16)
        sel_sb = ph.sbuf("sel_sb", [128, SEL_K // 128], F32)
        seli_sb = ph.sbuf("seli_sb", [128, SEL_K // 128], I32)
        cb_sb = ph.sbuf("cb_sb", [1, 1], F32)
        cbb = ph.sbuf("cbb", [128, 1], F32)
        negm = ph.sbuf("negm", [128, SEL_K // 128], F32)
        onesf = ph.const_in("onesf", np.ones((1, 128), np.float32))
        onf_sb = ph.sbuf("onf_sb", [1, 128], F32)
        xw_sb = ph.sbuf("xw_sb", [128, D], BF16)
        zz_sb = ph.sbuf("zz_sb", [128, D], BF16)
        x1_sb = ph.sbuf("x1_sb", [128, 2 * D], F32)
        xn_sb = ph.sbuf("xn_sb", [128, 2 * D], BF16)
        xa_sb = ph.sbuf("xa_sb", [128, 2 * D], BF16)
        gv_sb = ph.sbuf("gv_sb", [128, D], BF16)
        t1_sb = ph.sbuf("t1_sb", [128, 512], F32)
        out_sb = ph.sbuf("out_sb", [128, 2 * D], F32)
        pB = ph.psum("pB", [128, 512], F32)
        pC = ph.psum("pC", [128, 512], F32)
        pq = ph.psum("pq", [128, 1], F32)
        V, S, P, G, Y = ph.V, ph.S, ph.P, ph.G, ph.Y
        with nc.Block() as blk:
            @blk.vector
            def _(_v):
                G.dma_start(id_sb[:], ident_fin)
                emit_weight_cast_load(ph, wod, wo_sb, D, D)
                G.dma_start(bo_sb[:], bod[None, :])
                G.dma_start(on_sb[:], onesc)
                G.dma_start(onf_sb[:], onesf)
                G.dma_start(seli_sb[:], seld.rearrange("(p f) -> p f", f=SEL_K // 128))
                G.dma_start(cb_sb[:], cbased[None, :])
                V.memset(zz_sb[:], 0.0)
                P.wait(G); V.wait(G); S.wait(G)
                for i in range(tiles):
                    Y.dma_start(xadd.ap()[i * 128:(i + 1) * 128, :], zz_sb[:], ch=3)
                P.matmul(pq[:], onf_sb[:], cb_sb[:], start=True, stop=True)
                V.wait(P)
                V.tensor_copy(cbb[:], pq[:])
                V.wait(V)
                V.tensor_copy(sel_sb[:], seli_sb[:])
                V.wait(V)
                V.tensor_scalar(sel_sb[:], sel_sb[:], cbb[:], None, op0=ALU.add)
                V.wait(V)
                V.tensor_scalar(negm[:], sel_sb[:], 0.0, 70000.0,
                                op0=ALU.is_lt, op1=ALU.mult)
                V.wait(V)
                V.tensor_tensor(sel_sb[:], sel_sb[:], negm[:], op=ALU.add)
                V.wait(V)
                V.tensor_copy(seli_sb[:], sel_sb[:])
                V.wait(V)
                G.raw.wait_ge(V.sem, V.n)
                G.wait_dch(Y, 3)
                for r in range(SEL_K // 128):
                    G.dma_start(xw_sb[:],
                                xwd.rearrange("(p r) d -> p r d", r=SEL_K // 128)
                                [:, r, :], ch=1)
                    G.wait_dch(G, 1)
                    G.indirect_dma_start(
                        xadd.ap(), IndirectOffsetOnAxis(seli_sb[:, r:r + 1], 0),
                        xw_sb[:], None,
                        bounds_check=TL - 1, oob_is_err=False, ch=1)
                    G.wait_dch(G, 1)
                st_done = [0, 0]
                for i in range(tiles):
                    b = i % 2
                    if i >= 2:
                        Y.wait_dch(Y, b, st_done[b])
                    Y.dma_start(x1_sb[:, b * D:(b + 1) * D],
                                x1d[i * 128:(i + 1) * 128, :], ch=b)
                    Y.dma_start(xn_sb[:, b * D:(b + 1) * D],
                                xn1d[i * 128:(i + 1) * 128, :], ch=b)
                    Y.dma_start(xa_sb[:, b * D:(b + 1) * D],
                                xadd.ap()[i * 128:(i + 1) * 128, :], ch=b)
                    V.wait_dch(Y, b); V.wait(V); V.wait(S)
                    V.tensor_tensor(gv_sb[:], xn_sb[:, b * D:(b + 1) * D],
                                    xa_sb[:, b * D:(b + 1) * D], op=ALU.add)
                    S.wait(V)
                    S.activation(gv_sb[:], gv_sb[:], AF.Silu)
                    G.raw.wait_ge(P.sem, P.n)
                    for r in range(NCORES):
                        G.dma_start(oa_sb[:, r * 256:(r + 1) * 256],
                                    ofd[r * TL + i * 128: r * TL + (i + 1) * 128, :],
                                    ch=2)
                    P.wait(G); V.wait(G)
                    emt = {}
                    emit_transpose_set(ph, oa_sb[:],
                                       [of_sb[:, c * 128:(c + 1) * 128]
                                        for c in range(nch)],
                                       [tpA, tpB], id_sb, nch, emt)
                    for nt in range(4):
                        ps = (pB, pC)[nt % 2]
                        P.wait(V); P.wait(S)
                        for c in range(nch):
                            P.matmul(ps[:],
                                     of_sb[:, c * 128:(c + 1) * 128],
                                     wo_sb[:, c * D + nt * 512: c * D + (nt + 1) * 512],
                                     start=(c == 0), stop=False)
                        P.matmul(ps[:], on_sb[:], bo_sb[0:1, nt * 512:(nt + 1) * 512],
                                 start=False, stop=True)
                        V.wait(P)
                        V.tensor_tensor(t1_sb[:], ps[:],
                                        gv_sb[:, nt * 512:(nt + 1) * 512], op=ALU.mult)
                        V.wait(V)
                        V.tensor_tensor(out_sb[:, b * D + nt * 512: b * D + (nt + 1) * 512],
                                        t1_sb[:],
                                        x1_sb[:, b * D + nt * 512: b * D + (nt + 1) * 512],
                                        op=ALU.add)
                    V.wait(V)
                    nc.sync.wait_ge(V.sem, V.n)
                    Y.dma_start(outd[i * 128:(i + 1) * 128, :],
                                out_sb[:, b * D:(b + 1) * D], ch=b)
                    st_done[b] = Y.dmark(b)
                Y.wait(Y); G.wait(G)
        return ph



def _shard_rows(a, n=NCORES):
    a = np.ascontiguousarray(a)
    return np.split(a, n, axis=0)


def _run(ph, in_maps, **kw):
    for m in in_maps:
        for k, v in ph.consts.items():
            m[k] = v
    res = run_bass_kernel_spmd(ph.nc, in_maps, core_ids=list(range(len(in_maps))), **kw)
    ph.close()
    return res.results


def emit_coll(ph, kind, op, in_ap, out_ap, ch=3):
    nc = ph.nc
    from concourse.aot_env import allow_ici_all_to_all
    with nc.Block() as blk:
        @blk.vector
        def _(_v):
            with allow_ici_all_to_all():
                ph.G.collective_compute(kind, op,
                                        replica_groups=[list(range(NCORES))],
                                        ins=[in_ap], outs=[out_ap], ch=ch)
            ph.G.wait_dch(ph.G, ch)


def build_fused():
    ph = Ph("fused")
    nc = ph.nc

    def dram(name, shape, dtype, kind="Internal", shared=False):
        return nc.dram_tensor(name, list(shape), dtype, kind=kind,
                              addr_space="Shared" if shared else "Local").ap()

    def ein(name, shape, dtype):
        return dram(name, shape, dtype, "ExternalInput")

    x = ein("x", [PT, D], F32)
    wq0 = ein("wq0", [D, 256], F32)
    wk0 = ein("wk0", [D, 256], F32)
    wi0 = ein("wi0", [D, 256], F32)
    wo0 = ein("wo0", [D, D], F32)
    gw0 = ein("gw0", [D], F32)
    mrw = ein("mrw", [D, 8], F32)
    mrb = ein("mrb", [8], F32)
    mew = ein("mew", [8, D, D], F32)
    meb = ein("meb", [8, D], F32)
    mow = ein("mow", [D, D], F32)
    mob = ein("mob", [D], F32)
    wdkv = ein("wdkv", [D, 512], F32)
    bdkv = ein("bdkv", [512], F32)
    wdq = ein("wdq", [D, 512], F32)
    bdq = ein("bdq", [512], F32)
    wrl1 = ein("wrl1", [D], F32)
    wuk = ein("wuk", [512, 192], F32)
    buk = ein("buk", [192], F32)
    wuv = ein("wuv", [512, 256], F32)
    buv = ein("buv", [256], F32)
    wuq = ein("wuq", [512, 192], F32)
    buq = ein("buq", [192], F32)
    wqr = ein("wqr", [512, 64], F32)
    bqr = ein("bqr", [64], F32)
    wkr = ein("wkr", [D, 64], F32)
    bkr = ein("bkr", [64], F32)
    iotaj = ein("iotaj", [128, 128], F32)
    g1m = ein("g1m", [D], F32)
    wqm = ein("wqm", [D, 256], F32)
    wkm = ein("wkm", [D, 256], F32)
    wvm = ein("wvm", [D, 256], F32)
    wom = ein("wom", [256, D], F32)
    freqs = ein("freqs", [SEL_K, 64, 2], F32)
    maskm = ein("maskm", [SEL_K, SEL_K], F32)
    g2m = ein("g2m", [D], F32)
    w1c = ein("w1c", [D, FFN // NCORES], F32)
    w3c = ein("w3c", [D, FFN // NCORES], F32)
    w2r = ein("w2r", [FFN // NCORES, D], F32)
    wout = ein("wout", [D, D], F32)
    bout = ein("bout", [D], F32)
    cbase = ein("cbase", [1], F32)
    out = dram("out", [PT, D], F32, "ExternalOutput")

    xn_tm_l = dram("xn_tm_l", [PT, D], BF16)
    xn_fm_l = dram("xn_fm_l", [D, PT], BF16)
    ag_xnfm = dram("ag_xnfm", [NCORES * D, PT], BF16)
    o_tm_l = dram("o_tm_l", [T, 256], BF16)
    a2a_o = dram("a2a_o", [T, 256], BF16)
    s_tm_l = dram("s_tm_l", [PT, D], BF16)
    g_tm_l = dram("g_tm_l", [PT, D], BF16)
    x1_l = dram("x1_l", [PT, D], F32)
    xn1_tm_l = dram("xn1_tm_l", [PT, D], BF16)
    xn1_fm_l = dram("xn1_fm_l", [D, PT], BF16)
    ag_xn1fm = dram("ag_xn1fm", [NCORES * D, PT], BF16)
    ag_xn1tm = dram("ag_xn1tm", [T, D], BF16)
    ckv_l = dram("ckv_l", [512, PT], BF16)
    cq_l = dram("cq_l", [512, PT], BF16)
    ag_ckv = dram("ag_ckv", [NCORES * 512, PT], BF16)
    ag_cq = dram("ag_cq", [NCORES * 512, PT], BF16)
    lg_l = dram("lg_l", [PT], F32)
    ag_lg = dram("ag_lg", [T], F32)
    o1_tm_l = dram("o1_tm_l", [T, 256], BF16)
    a2a_o1 = dram("a2a_o1", [T, 256], BF16)
    fx_l = dram("fx_l", [128, D], BF16)
    ag_fx = dram("ag_fx", [SEL_K, D], BF16)
    sel_l = dram("sel_l", [128], I32)
    ag_sel = dram("ag_sel", [SEL_K], I32)
    rw_l = dram("rw_l", [128], F32)
    part1_l = dram("part1_l", [SEL_K, D], F32)
    p1s_l = dram("p1s_l", [1, 128, D], F32)
    xb_l = dram("xb_l", [128, D], F32)
    h2_l = dram("h2_l", [128, D], BF16)
    ag_h2 = dram("ag_h2", [SEL_K, D], BF16)
    part2_l = dram("part2_l", [SEL_K, D], F32)
    p2s_l = dram("p2s_l", [1, 128, D], F32)
    xw_l = dram("xw_l", [128, D], BF16)
    ag_xw = dram("ag_xw", [SEL_K, D], BF16)

    ph.fconst(LN_EPS)
    ph.fconst(RMS0_EPS)
    ph.fconst(RMS1_EPS)

    def phase(builder, ext, **kw):
        ph.ext = ext
        builder(ph=ph, **kw)
        ph.ext = None

    AG, RS, A2A = "AllGather", "ReduceScatter", "AllToAll"
    phase(build_ln0, {"x": x, "xn_tm": xn_tm_l, "xn_fm": xn_fm_l})
    emit_coll(ph, AG, ALU.bypass, xn_fm_l, ag_xnfm)
    phase(build_gla, {"xn_fm": ag_xnfm, "wq": wq0, "wk": wk0, "wi": wi0,
                      "o_tm": o_tm_l}, src_blk=PT)
    emit_coll(ph, A2A, ALU.bypass, o_tm_l, a2a_o)
    phase(build_glaep, {"o": a2a_o, "wo": wo0, "gw": gw0, "s_tm": s_tm_l},
          nrk=NCORES)
    phase(build_moe, {"xn_fm": xn_fm_l, "rw": mrw, "rb": mrb, "ew": mew,
                      "eb": meb, "ow": mow, "ob": mob, "g_tm": g_tm_l})
    phase(build_combine_ln, {"x": x, "s_in": s_tm_l, "g_in": g_tm_l,
                             "x1": x1_l, "xn_tm": xn1_tm_l, "xn_fm": xn1_fm_l})
    emit_coll(ph, AG, ALU.bypass, xn1_fm_l, ag_xn1fm)
    emit_coll(ph, AG, ALU.bypass, xn1_tm_l, ag_xn1tm)
    phase(build_l1proj, {"xn_fm": xn1_fm_l, "wdkv": wdkv, "bdkv": bdkv,
                         "wdq": wdq, "bdq": bdq, "wr": wrl1,
                         "ckv": ckv_l, "cq": cq_l, "logits": lg_l})
    emit_coll(ph, AG, ALU.bypass, ckv_l, ag_ckv)
    emit_coll(ph, AG, ALU.bypass, cq_l, ag_cq)
    emit_coll(ph, AG, ALU.bypass, lg_l, ag_lg)
    phase(build_l1flash, {"ckv": ag_ckv, "cqf": ag_cq, "xn_fm": ag_xn1fm,
                          "wuk": wuk, "buk": buk, "wuv": wuv, "buv": buv,
                          "wuq": wuq, "buq": buq, "wqr": wqr, "bqr": bqr,
                          "wkr": wkr, "bkr": bkr, "o_tm": o1_tm_l},
          src_blk=PT)
    emit_coll(ph, A2A, ALU.bypass, o1_tm_l, a2a_o1)
    phase(build_l1sel, {"logits": ag_lg, "xn1_tm": ag_xn1tm, "iotaj": iotaj,
                        "fx": fx_l, "selidx": sel_l, "rw": rw_l})
    emit_coll(ph, AG, ALU.bypass, fx_l, ag_fx)
    emit_coll(ph, AG, ALU.bypass, sel_l, ag_sel)
    phase(build_mini1, {"fx_full": ag_fx, "g1": g1m, "wq": wqm, "wk": wkm,
                        "wv": wvm, "wo_r": wom, "freqs": freqs, "mask": maskm,
                        "part1": part1_l})
    emit_coll(ph, RS, ALU.add, part1_l, p1s_l.rearrange("o p d -> (o p) d"))
    phase(build_mini2, {"parts": p1s_l, "fx_rows": fx_l, "xb": xb_l,
                        "h2": h2_l}, pre_summed=True)
    emit_coll(ph, AG, ALU.bypass, h2_l, ag_h2)
    phase(build_mini3, {"h2_full": ag_h2, "g2": g2m, "w1c": w1c, "w3c": w3c,
                        "w2r": w2r, "part2": part2_l})
    emit_coll(ph, RS, ALU.add, part2_l, p2s_l.rearrange("o p d -> (o p) d"))
    phase(build_mini4, {"parts2": p2s_l, "xb_rows": xb_l, "rw_rows": rw_l,
                        "xw": xw_l}, pre_summed=True)
    emit_coll(ph, AG, ALU.bypass, xw_l, ag_xw)
    phase(build_final, {"x1": x1_l, "xn1_tm": xn1_tm_l, "o_a2a": a2a_o1,
                        "wout": wout, "bout": bout, "xw_full": ag_xw,
                        "selidx": ag_sel, "cbase": cbase, "out": out})
    return ph, out


FUSED = False


def kernel(**inputs):
    if FUSED:
        return _kernel_fused(**inputs)
    return _kernel_unfused(**inputs)


def _kernel_fused(**inputs):
    x = np.ascontiguousarray(np.asarray(inputs["L0_attn_hidden_states"])[0])
    ph, _ = build_fused()
    A = lambda k: np.asarray(inputs[k])
    C = np.ascontiguousarray
    FS = FFN // NCORES
    in_maps = []
    for c in range(NCORES):
        h256 = slice(c * 256, (c + 1) * 256)
        h192 = slice(c * 192, (c + 1) * 192)
        h64 = slice(c * 64, (c + 1) * 64)
        m = {
            "x": C(x[c * PT:(c + 1) * PT]),
            "wq0": C(A("L0_attn_Wq")[:, h256]),
            "wk0": C(A("L0_attn_Wk")[:, h256]),
            "wi0": C(A("L0_attn_Wi")[:, h256]),
            "wo0": A("L0_attn_Wo"), "gw0": A("L0_attn_g_weight"),
            "mrw": A("L0_ffn_router_w"), "mrb": A("L0_ffn_router_b"),
            "mew": A("L0_ffn_expert_w"), "meb": A("L0_ffn_expert_b"),
            "mow": A("L0_ffn_out_w"), "mob": A("L0_ffn_out_b"),
            "wdkv": A("L1_attn_W_dkv"), "bdkv": A("L1_attn_b_dkv"),
            "wdq": A("L1_attn_W_dq"), "bdq": A("L1_attn_b_dq"),
            "wrl1": A("L1_ffn_w_router"),
            "wuk": C(A("L1_attn_W_uk")[:, h192]), "buk": C(A("L1_attn_b_uk")[h192]),
            "wuv": C(A("L1_attn_W_uv")[:, h256]), "buv": C(A("L1_attn_b_uv")[h256]),
            "wuq": C(A("L1_attn_W_uq")[:, h192]), "buq": C(A("L1_attn_b_uq")[h192]),
            "wqr": C(A("L1_attn_W_qr")[:, h64]), "bqr": C(A("L1_attn_b_qr")[h64]),
            "wkr": C(A("L1_attn_W_kr")[:, h64]), "bkr": C(A("L1_attn_b_kr")[h64]),
            "iotaj": (np.arange(128)[None, :] +
                      128.0 * c * np.ones((128, 1))).astype(np.float32),
            "g1m": A("L1_ffn_g1"),
            "wqm": C(A("L1_ffn_wq")[:, h256]), "wkm": C(A("L1_ffn_wk")[:, h256]),
            "wvm": C(A("L1_ffn_wv")[:, h256]), "wom": C(A("L1_ffn_wo")[h256, :]),
            "freqs": A("L1_ffn_freqs_cis"), "maskm": A("L1_ffn_mask"),
            "g2m": A("L1_ffn_g2"),
            "w1c": C(A("L1_ffn_w1")[:, c * FS:(c + 1) * FS]),
            "w3c": C(A("L1_ffn_w3")[:, c * FS:(c + 1) * FS]),
            "w2r": C(A("L1_ffn_w2")[c * FS:(c + 1) * FS, :]),
            "wout": A("L1_attn_W_out"), "bout": A("L1_attn_b_out"),
            "cbase": np.array([-PT * c], np.float32),
        }
        in_maps.append(m)
    res = _run(ph, in_maps)
    out = np.concatenate([np.asarray(r["out"], np.float32) for r in res], 0)
    return out[None]


def _kernel_unfused(**inputs):
    x = np.ascontiguousarray(np.asarray(inputs["L0_attn_hidden_states"])[0])
    f32 = np.float32

    def rows(a):
        return [np.ascontiguousarray(s) for s in np.split(a, NCORES, 0)]

    def cols(a, n=NCORES):
        return [np.ascontiguousarray(s) for s in np.split(a, n, 1)]

    ph = build_ln0()
    res = _run(ph, [{"x": s} for s in rows(x)])
    xn_tm = [r["xn_tm"] for r in res]
    xn_fm_loc = [r["xn_fm"] for r in res]
    ag_xn_fm = np.concatenate(xn_fm_loc, 0)

    ph = build_gla(src_blk=PT)
    Wq, Wk, Wi = (np.asarray(inputs["L0_attn_" + n]) for n in ("Wq", "Wk", "Wi"))
    in_maps = []
    for c in range(NCORES):
        hs = slice(c * 256, (c + 1) * 256)
        in_maps.append({"xn_fm": ag_xn_fm,
                        "wq": np.ascontiguousarray(Wq[:, hs]),
                        "wk": np.ascontiguousarray(Wk[:, hs]),
                        "wi": np.ascontiguousarray(Wi[:, hs])})
    res = _run(ph, in_maps)
    o_percore = [r["o_tm"] for r in res]

    ph = build_glaep(nrk=NCORES)
    res = _run(ph, [{"o": np.concatenate(
                        [o_percore[r][c * PT:(c + 1) * PT, :] for r in range(NCORES)], 0),
                     "wo": np.asarray(inputs["L0_attn_Wo"]),
                     "gw": np.asarray(inputs["L0_attn_g_weight"])}
                    for c in range(NCORES)])
    s_tm = [r["s_tm"] for r in res]

    ph = build_moe()
    res = _run(ph, [{"xn_fm": xn_fm_loc[c],
                     "rw": np.asarray(inputs["L0_ffn_router_w"]),
                     "rb": np.asarray(inputs["L0_ffn_router_b"]),
                     "ew": np.asarray(inputs["L0_ffn_expert_w"]),
                     "eb": np.asarray(inputs["L0_ffn_expert_b"]),
                     "ow": np.asarray(inputs["L0_ffn_out_w"]),
                     "ob": np.asarray(inputs["L0_ffn_out_b"])}
                    for c in range(NCORES)])
    g_tm = [r["g_tm"] for r in res]

    ph = build_combine_ln()
    res = _run(ph, [{"x": xs, "s_in": ss, "g_in": gs}
                    for xs, ss, gs in zip(rows(x), s_tm, g_tm)])
    x1 = [r["x1"] for r in res]
    xn1_tm = [r["xn_tm"] for r in res]
    xn1_fm_loc = [r["xn_fm"] for r in res]
    ag_xn1_fm = np.concatenate(xn1_fm_loc, 0)
    xn1_tm_full = np.concatenate(xn1_tm, 0)

    ph = build_l1proj()
    res = _run(ph, [{"xn_fm": xn1_fm_loc[c],
                     "wdkv": np.asarray(inputs["L1_attn_W_dkv"]),
                     "bdkv": np.asarray(inputs["L1_attn_b_dkv"]),
                     "wdq": np.asarray(inputs["L1_attn_W_dq"]),
                     "bdq": np.asarray(inputs["L1_attn_b_dq"]),
                     "wr": np.asarray(inputs["L1_ffn_w_router"])}
                    for c in range(NCORES)])
    ag_ckv = np.concatenate([r["ckv"] for r in res], 0)
    ag_cq = np.concatenate([r["cq"] for r in res], 0)
    logits_full = np.concatenate([np.asarray(r["logits"]) for r in res])

    ph = build_l1flash(src_blk=PT)
    in_maps = []
    for c in range(NCORES):
        h2 = slice(c * 192, (c + 1) * 192)
        hv = slice(c * 256, (c + 1) * 256)
        hr = slice(c * 64, (c + 1) * 64)
        in_maps.append({
            "ckv": ag_ckv, "cqf": ag_cq, "xn_fm": ag_xn1_fm,
            "wuk": np.ascontiguousarray(np.asarray(inputs["L1_attn_W_uk"])[:, h2]),
            "buk": np.ascontiguousarray(np.asarray(inputs["L1_attn_b_uk"])[h2]),
            "wuv": np.ascontiguousarray(np.asarray(inputs["L1_attn_W_uv"])[:, hv]),
            "buv": np.ascontiguousarray(np.asarray(inputs["L1_attn_b_uv"])[hv]),
            "wuq": np.ascontiguousarray(np.asarray(inputs["L1_attn_W_uq"])[:, h2]),
            "buq": np.ascontiguousarray(np.asarray(inputs["L1_attn_b_uq"])[h2]),
            "wqr": np.ascontiguousarray(np.asarray(inputs["L1_attn_W_qr"])[:, hr]),
            "bqr": np.ascontiguousarray(np.asarray(inputs["L1_attn_b_qr"])[hr]),
            "wkr": np.ascontiguousarray(np.asarray(inputs["L1_attn_W_kr"])[:, hr]),
            "bkr": np.ascontiguousarray(np.asarray(inputs["L1_attn_b_kr"])[hr]),
        })
    res = _run(ph, in_maps)
    o1_percore = [r["o_tm"] for r in res]

    ph = build_l1sel()
    res = _run(ph, [{"logits": logits_full, "xn1_tm": xn1_tm_full,
                     "iotaj": (np.arange(128)[None, :] +
                               128 * c * np.ones((128, 1))).astype(np.float32)
                     + 0.0}
                    for c in range(NCORES)])
    fx_full = np.concatenate([r["fx"] for r in res], 0)
    selidx = np.concatenate([r["selidx"] for r in res])
    rw_full = np.concatenate([r["rw"] for r in res])

    ph = build_mini1()
    in_maps = []
    for c in range(NCORES):
        hv = slice(c * 256, (c + 1) * 256)
        in_maps.append({
            "fx_full": fx_full, "g1": np.asarray(inputs["L1_ffn_g1"]),
            "wq": np.ascontiguousarray(np.asarray(inputs["L1_ffn_wq"])[:, hv]),
            "wk": np.ascontiguousarray(np.asarray(inputs["L1_ffn_wk"])[:, hv]),
            "wv": np.ascontiguousarray(np.asarray(inputs["L1_ffn_wv"])[:, hv]),
            "wo_r": np.ascontiguousarray(np.asarray(inputs["L1_ffn_wo"])[hv, :]),
            "freqs": np.asarray(inputs["L1_ffn_freqs_cis"]),
            "mask": np.asarray(inputs["L1_ffn_mask"]),
        })
    res = _run(ph, in_maps)
    part1 = np.stack([r["part1"] for r in res], 0)

    ph = build_mini2()
    res = _run(ph, [{"parts": np.ascontiguousarray(
                        part1[:, c * 128:(c + 1) * 128, :]),
                     "fx_rows": np.ascontiguousarray(
                        fx_full[c * 128:(c + 1) * 128])}
                    for c in range(NCORES)])
    xb_rows = [r["xb"] for r in res]
    h2_full = np.concatenate([r["h2"] for r in res], 0)

    ph = build_mini3()
    FS = FFN // NCORES
    res = _run(ph, [{"h2_full": h2_full, "g2": np.asarray(inputs["L1_ffn_g2"]),
                     "w1c": np.ascontiguousarray(
                        np.asarray(inputs["L1_ffn_w1"])[:, c * FS:(c + 1) * FS]),
                     "w3c": np.ascontiguousarray(
                        np.asarray(inputs["L1_ffn_w3"])[:, c * FS:(c + 1) * FS]),
                     "w2r": np.ascontiguousarray(
                        np.asarray(inputs["L1_ffn_w2"])[c * FS:(c + 1) * FS, :])}
                    for c in range(NCORES)])
    part2 = np.stack([r["part2"] for r in res], 0)

    ph = build_mini4()
    res = _run(ph, [{"parts2": np.ascontiguousarray(
                        part2[:, c * 128:(c + 1) * 128, :]),
                     "xb_rows": xb_rows[c],
                     "rw_rows": np.ascontiguousarray(
                        np.asarray(rw_full)[c * 128:(c + 1) * 128])}
                    for c in range(NCORES)])
    xw_full = np.concatenate([r["xw"] for r in res], 0)

    ph = build_final()
    res = _run(ph, [{"x1": x1[c], "xn1_tm": xn1_tm[c],
                     "o_a2a": np.concatenate(
                        [o1_percore[r][c * PT:(c + 1) * PT, :]
                         for r in range(NCORES)], 0),
                     "wout": np.asarray(inputs["L1_attn_W_out"]),
                     "bout": np.asarray(inputs["L1_attn_b_out"]),
                     "xw_full": xw_full, "selidx": selidx,
                     "cbase": np.array([-PT * c], np.float32)}
                    for c in range(NCORES)])
    out = np.concatenate([np.asarray(r["out"], f32) for r in res], 0)
    return out[None]


# version 03
# speedup vs baseline: 1.1338x; 1.1338x over previous
import numpy as np
import ml_dtypes
from contextlib import ExitStack

import concourse.bass as bass
import concourse.mybir as mybir
from concourse.bass import IndirectOffsetOnAxis
from concourse.bass_utils import run_bass_kernel_spmd

F32 = mybir.dt.float32
BF16 = mybir.dt.bfloat16
I32 = mybir.dt.int32
U8 = mybir.dt.uint8
AF = mybir.ActivationFunctionType
ALU = mybir.AluOpType
AX = mybir.AxisListType
NPBF = ml_dtypes.bfloat16

NCORES = 8
T = 8192
D = 2048
PT = T // NCORES
NH = 16
HPC = NH // NCORES
HD = 128
GLA_SCALE = float(HD) ** -0.5
L1_SCALE = 1.0 / float(np.sqrt(np.float32(128.0)))
CHUNK = 32
LN_EPS = 1e-5
RMS0_EPS = 1e-5
RMS1_EPS = 1e-6
SEL_K = 1024
MINI_PT = SEL_K // NCORES
FFN = 8192
C1_GELU = 0.7978845608028654
C2_GELU = 0.7978845608028654 * 0.044715

_DT = {F32: np.float32, BF16: NPBF, I32: np.int32}


class Eng:
    _NOINC = {"wait_ge", "wait_op", "nop", "sem_inc", "drain"}
    _DMA = {"dma_start", "indirect_dma_start", "dma_start_transpose"}

    def __init__(self, raw, sem, dma_sems=()):
        self.raw = raw
        self.sem = sem
        self.n = 0
        self.d = [[h, 0] for h in dma_sems]

    def __getattr__(self, name):
        f = getattr(self.raw, name)
        if not callable(f):
            return f
        if name in Eng._NOINC:
            return f
        if name in Eng._DMA or name == "collective_compute":
            def g(*a, ch=0, **kw):
                instr = f(*a, **kw)
                instr.then_inc(self.d[ch][0], 16)
                self.d[ch][1] += 16
                return instr
            return g

        def g(*a, **kw):
            instr = f(*a, **kw)
            instr.then_inc(self.sem, 1)
            self.n += 1
            return instr
        return g

    def wait(self, *producers):
        for o in producers:
            if o.n:
                self.raw.wait_ge(o.sem, o.n)
            for h, cnt in o.d:
                if cnt:
                    self.raw.wait_ge(h, cnt)

    def wait_ev(self, o, n):
        if n:
            self.raw.wait_ge(o.sem, n)

    def wait_dch(self, o, ch, n=None):
        h, cnt = o.d[ch]
        n = cnt if n is None else n
        if n:
            self.raw.wait_ge(h, n)

    def dmark(self, ch=0):
        return self.d[ch][1]


class Ph:
    def __init__(self, name):
        self.nc = bass.Bass("TRN2", target_bir_lowering=False)
        self.stack = ExitStack()
        self.scopes = [self.stack]
        self.name = name
        self.consts = {}
        self._fconsts = {}
        self._uid = 0
        self.ext = None
        es = self.stack.enter_context
        nc = self.nc
        self.V = Eng(nc.vector, es(nc.semaphore("sem_v")))
        self.S = Eng(nc.scalar, es(nc.semaphore("sem_s")))
        self.P = Eng(nc.tensor, es(nc.semaphore("sem_p")))
        self.G = Eng(nc.gpsimd, es(nc.semaphore("sem_g")),
                     [es(nc.semaphore("sem_gd%d" % i)) for i in range(4)])
        self.Y = Eng(nc.sync, es(nc.semaphore("sem_y")),
                     [es(nc.semaphore("sem_yd%d" % i)) for i in range(4)])
        for v in (LN_EPS, RMS1_EPS):
            self.fconst(v)

    def scope(self):
        from contextlib import contextmanager

        @contextmanager
        def _s():
            es = ExitStack()
            self.scopes.append(es)
            try:
                yield
            finally:
                es.close()
                self.scopes.pop()
        return _s()

    def io(self, name, shape, dtype, kind):
        if self.ext is not None and name in self.ext:
            return self.ext[name]
        return self.nc.dram_tensor(name, list(shape), dtype, kind=kind).ap()

    def sbuf(self, name, shape, dtype):
        self._uid += 1
        return self.scopes[-1].enter_context(
            self.nc.sbuf_tensor("%s_u%d" % (name, self._uid), list(shape), dtype))

    def psum(self, name, shape, dtype=F32):
        self._uid += 1
        return self.scopes[-1].enter_context(
            self.nc.psum_tensor("%s_u%d" % (name, self._uid), list(shape), dtype))

    def fconst(self, val, dtype=F32):
        key = (dtype, float(val))
        if key in self._fconsts:
            return self._fconsts[key]
        self._uid += 1
        t = self.stack.enter_context(
            self.nc.sbuf_tensor("fc_u%d" % self._uid, [128, 1], dtype))
        self.nc.vector.memset(t[:], float(val))
        self._fconsts[key] = t[:]
        try:
            self.nc.const_aps.register(dtype, float(val), t[:])
        except Exception:
            pass
        return t[:]

    def const_in(self, name, arr):
        if name in self.consts:
            return self._const_aps[name]
        arr = np.ascontiguousarray(arr)
        t = self.nc.dram_tensor(name, list(arr.shape),
                                mybir.dt.from_np(arr.dtype), kind="ExternalInput")
        self.consts[name] = arr
        if not hasattr(self, "_const_aps"):
            self._const_aps = {}
        self._const_aps[name] = t.ap()
        return t.ap()

    def close(self):
        self.stack.close()


def _const_identity_bf16():
    return np.eye(128, dtype=NPBF)


def _const_identity_f32():
    return np.eye(128, dtype=np.float32)


def _const_neg_u32_blockdiag():
    m = np.zeros((128, 128), np.float32)
    for c in range(4):
        for t in range(32):
            for t2 in range(t, 32):
                m[c * 32 + t, c * 32 + t2] = -1.0
    return m.astype(NPBF)


def _const_mask32():
    return np.triu(np.ones((32, 32), np.float32))


def _ap(x):
    return x[:] if hasattr(x, "shape") and not isinstance(x, bass.AP) else x


def emit_ln_tile(ph, x_ap, out_ap, mean, rstd, scr, ssq, s1, eps_ap, n_feat,
                 rms=False):
    V, S = ph.V, ph.S
    mean, rstd, scr, ssq, s1 = _ap(mean), _ap(rstd), _ap(scr), _ap(ssq), _ap(s1)
    S.wait(V)
    S.activation(scr, x_ap, AF.Square, accum_out=ssq)
    if not rms:
        V.reduce_sum(s1, x_ap, axis=AX.X)
        V.wait(V)
        V.tensor_scalar(mean, s1, 1.0 / n_feat, None, op0=ALU.mult)
        V.tensor_tensor(s1, s1, s1, op=ALU.mult)
        V.wait(V); V.wait(S)
        V.scalar_tensor_tensor(ssq, s1, -1.0 / n_feat, ssq, op0=ALU.mult, op1=ALU.add)
        S.wait(V)
    S.wait(S)
    S.activation(ssq, ssq, AF.Ln, bias=eps_ap, scale=1.0 / n_feat)
    S.wait(S)
    S.activation(rstd, ssq, AF.Exp, scale=-0.5)
    V.wait(S)
    if rms:
        V.tensor_scalar(out_ap, x_ap, rstd, None, op0=ALU.mult)
    else:
        V.tensor_scalar(out_ap, x_ap, mean, rstd, op0=ALU.subtract, op1=ALU.mult)


def emit_transpose_set(ph, src_tm, out_cols, tps, id_sb, nchunk, evac_marks,
                       in_slices=None):
    P, V = ph.P, ph.V
    for c in range(nchunk):
        buf = c % 2
        if evac_marks.get(buf):
            P.wait_ev(V, evac_marks[buf])
        src = in_slices[c] if in_slices is not None else \
            src_tm[:, c * 128:(c + 1) * 128]
        P.transpose(tps[buf][:], src, id_sb[:])
        V.wait(P)
        V.tensor_copy(out_cols[c], tps[buf][:])
        evac_marks[buf] = V.n


def build_ln0(tiles=8, n_feat=D, name="ln0", ph=None):
    if ph is None:
        ph = Ph(name)
    nc = ph.nc
    x = ph.io("x", [tiles * 128, n_feat], F32, "ExternalInput")
    xn_tm = ph.io("xn_tm", [tiles * 128, n_feat], BF16, "ExternalOutput")
    xn_fm = ph.io("xn_fm", [n_feat, tiles * 128], BF16, "ExternalOutput")
    ident = ph.const_in("ident", _const_identity_bf16())
    nchunk = n_feat // 128

    with ph.scope():
        x_sb = ph.sbuf("x_sb", [128, 2 * n_feat], F32)
        xn_sb = ph.sbuf("xn_sb", [128, 2 * n_feat], BF16)
        fm_sb = ph.sbuf("fm_sb", [128, nchunk * tiles * 128], BF16)
        id_sb = ph.sbuf("id_sb", [128, 128], BF16)
        scr = ph.sbuf("scr", [128, n_feat], BF16)
        mean = ph.sbuf("mean", [128, 1], F32)
        rstd = ph.sbuf("rstd", [128, 1], F32)
        ssq = ph.sbuf("ssq", [128, 1], F32)
        s1 = ph.sbuf("s1", [128, 1], F32)
        tp0 = ph.psum("tp0", [128, 128], BF16)
        tp1 = ph.psum("tp1", [128, 128], BF16)
        eps_ap = ph.fconst(LN_EPS)
        V, S, P, G, Y = ph.V, ph.S, ph.P, ph.G, ph.Y

        with nc.Block() as blk:
            @blk.vector
            def _(_v):
                G.dma_start(id_sb[:], ident)
                P.wait(G)
                evac_marks = {}
                st_done = [0, 0]
                pe_done = [0, 0]
                for i in range(tiles):
                    b = i % 2
                    xb = x_sb[:, b * n_feat:(b + 1) * n_feat]
                    xnb = xn_sb[:, b * n_feat:(b + 1) * n_feat]
                    if i >= 2:
                        Y.wait_dch(Y, 2 + b, st_done[b])
                    Y.dma_start(xb, x[i * 128:(i + 1) * 128, :], ch=b)
                    V.wait(Y); S.wait(Y)
                    V.wait(V)
                    if i >= 2:
                        V.wait_ev(P, pe_done[b])
                    emit_ln_tile(ph, xb, xnb, mean, rstd, scr, ssq, s1,
                                 eps_ap, n_feat)
                    nc.sync.wait_ge(V.sem, V.n)
                    Y.dma_start(xn_tm[i * 128:(i + 1) * 128, :], xnb, ch=2 + b)
                    st_done[b] = Y.dmark(2 + b)
                    P.wait(V)
                    out_cols = [fm_sb[:, c * tiles * 128 + i * 128:
                                      c * tiles * 128 + (i + 1) * 128]
                                for c in range(nchunk)]
                    emit_transpose_set(ph, xnb, out_cols, [tp0, tp1],
                                       id_sb, nchunk, evac_marks)
                    pe_done[b] = P.n
                nc.sync.wait_ge(V.sem, V.n)
                Y.dma_start(xn_fm.rearrange("(c p) t -> p c t", p=128),
                            fm_sb[:].rearrange("p (c t) -> p c t", t=tiles * 128),
                            ch=2)
                Y.wait(Y); G.wait(G)
        return ph


def emit_weight_cast_load(ph, w_dram, w_sb, nrows, ncols, eng=None):
    G = eng or ph.G
    nch = nrows // 128
    G.dma_start(
        w_sb[:].rearrange("p (c n) -> p c n", n=ncols) if nch > 1 else w_sb[:],
        w_dram.rearrange("(c p) n -> p c n", p=128) if nch > 1 else w_dram)


def emit_bias_row_load(ph, b_dram, b_sb, n):
    ph.G.dma_start(b_sb[0:1, :], b_dram[None, :])


def emit_mm_tm(ph, psum_ap, act_fm_cols, w_sb_cols, kchunks, first=True,
               bias_row=None, ones_col=None):
    P = ph.P
    n = len(kchunks)
    for j, k in enumerate(kchunks):
        P.matmul(psum_ap, act_fm_cols(k), w_sb_cols(k),
                 start=(first and j == 0), stop=False)
    if bias_row is not None:
        P.matmul(psum_ap, ones_col, bias_row, start=False, stop=True)
    else:
        pass


def build_glaep(tiles=PT // 128, name="glaep", ph=None, nrk=1):
    if ph is None:
        ph = Ph(name)
    nc = ph.nc
    o = ph.io("o", [nrk * tiles * 128, D // nrk], BF16, "ExternalInput")
    wo = ph.io("wo", [D, D], F32, "ExternalInput")
    gw = ph.io("gw", [D], F32, "ExternalInput")
    s_tm = ph.io("s_tm", [tiles * 128, D], BF16, "ExternalOutput")
    ident = ph.const_in("ident", _const_identity_bf16())
    eps_ap = ph.fconst(RMS0_EPS)
    V, S, P, G, Y = ph.V, ph.S, ph.P, ph.G, ph.Y
    nch = D // 128

    with ph.scope():
        wo_sb = ph.sbuf("wo_sb", [128, nch * D], BF16)
        gw_sb = ph.sbuf("gw_sb", [128, nch], F32)
        id_sb = ph.sbuf("id_sb", [128, 128], BF16)
        o_sb = ph.sbuf("o_sb", [128, 2 * D], BF16)
        of_sb = ph.sbuf("of_sb", [128, D], BF16)
        st_sb = ph.sbuf("st_sb", [128, 2 * D], BF16)
        scr = ph.sbuf("scr", [128, D], BF16)
        rstd = ph.sbuf("rstd", [128, 1], F32)
        ssq = ph.sbuf("ssq", [128, 1], F32)
        tp0 = ph.psum("tp0", [128, 128], BF16)
        tp1 = ph.psum("tp1", [128, 128], BF16)
        mm0 = ph.psum("mm0", [128, 512], F32)
        mm1 = ph.psum("mm1", [128, 512], F32)

        with nc.Block() as blk:
            @blk.vector
            def _(_v):
                G.dma_start(id_sb[:], ident)
                emit_weight_cast_load(ph, wo, wo_sb, D, D)
                G.dma_start(gw_sb[:], gw.rearrange("(c p) -> p c", p=128), allow_slow_non_contiguous=True)
                V.wait(G)
                for c in range(nch):
                    V.tensor_scalar(wo_sb[:, c * D:(c + 1) * D],
                                    wo_sb[:, c * D:(c + 1) * D],
                                    gw_sb[:, c:c + 1], None, op0=ALU.mult)
                P.wait(V); S.wait(V)
                evac_marks = {}
                st_done = [0, 0]
                mmev = {}
                for i in range(tiles):
                    b = i % 2
                    ob = o_sb[:, b * D:(b + 1) * D]
                    stb = st_sb[:, b * D:(b + 1) * D]
                    if i >= 2:
                        Y.wait_dch(Y, 2 + b, st_done[b])
                    if nrk == 1:
                        Y.dma_start(ob, o[i * 128:(i + 1) * 128, :], ch=b)
                    else:
                        for r in range(nrk):
                            Y.dma_start(ob[:, r * (D // nrk):(r + 1) * (D // nrk)],
                                        o[r * tiles * 128 + i * 128:
                                          r * tiles * 128 + (i + 1) * 128, :],
                                        ch=b)
                    V.wait(Y); S.wait(Y)
                    V.wait(V); S.wait(S)
                    emit_ln_tile(ph, ob, ob, None, rstd, scr, ssq, None,
                                 eps_ap, D, rms=True)
                    P.wait(V)
                    out_cols = [of_sb[:, c * 128:(c + 1) * 128] for c in range(nch)]
                    emit_transpose_set(ph, ob, out_cols, [tp0, tp1], id_sb, nch,
                                       evac_marks)
                    for nt in range(4):
                        ps = (mm0, mm1)[nt % 2]
                        if nt >= 2 or i > 0:
                            P.wait_ev(V, mmev.get(nt % 2, 0))
                        P.wait(V)
                        for kc in range(nch):
                            P.matmul(ps[:], of_sb[:, kc * 128:(kc + 1) * 128],
                                     wo_sb[:, kc * D + nt * 512: kc * D + (nt + 1) * 512],
                                     start=(kc == 0), stop=(kc == nch - 1))
                        V.wait(P)
                        V.tensor_copy(stb[:, nt * 512:(nt + 1) * 512], ps[:])
                        mmev[nt % 2] = V.n
                    V.wait(V)
                    nc.sync.wait_ge(V.sem, V.n)
                    Y.dma_start(s_tm[i * 128:(i + 1) * 128, :], stb, ch=2 + b)
                    st_done[b] = Y.dmark(2 + b)
                Y.wait(Y); G.wait(G)
        return ph


def build_combine_ln(tiles=PT // 128, eps=LN_EPS, name="combine", ph=None):
    if ph is None:
        ph = Ph(name)
    nc = ph.nc
    n_feat = D
    x = ph.io("x", [tiles * 128, n_feat], F32, "ExternalInput")
    s_in = ph.io("s_in", [tiles * 128, n_feat], BF16, "ExternalInput")
    g_in = ph.io("g_in", [tiles * 128, n_feat], BF16, "ExternalInput")
    x1 = ph.io("x1", [tiles * 128, n_feat], F32, "ExternalOutput")
    xn_tm = ph.io("xn_tm", [tiles * 128, n_feat], BF16, "ExternalOutput")
    xn_fm = ph.io("xn_fm", [n_feat, tiles * 128], BF16, "ExternalOutput")
    ident = ph.const_in("ident", _const_identity_bf16())
    nchunk = n_feat // 128

    with ph.scope():
        x_sb = ph.sbuf("x_sb", [128, 2 * n_feat], F32)
        sg_sb = ph.sbuf("sg_sb", [128, 2 * n_feat], BF16)
        gg_sb = ph.sbuf("gg_sb", [128, 2 * n_feat], BF16)
        xn_sb = ph.sbuf("xn_sb", [128, 2 * n_feat], BF16)
        fm_sb = ph.sbuf("fm_sb", [128, nchunk * tiles * 128], BF16)
        id_sb = ph.sbuf("id_sb", [128, 128], BF16)
        scr = ph.sbuf("scr", [128, n_feat], BF16)
        sgf = ph.sbuf("sgf", [128, n_feat], F32)
        mean = ph.sbuf("mean", [128, 1], F32)
        rstd = ph.sbuf("rstd", [128, 1], F32)
        ssq = ph.sbuf("ssq", [128, 1], F32)
        s1 = ph.sbuf("s1", [128, 1], F32)
        tp0 = ph.psum("tp0", [128, 128], BF16)
        tp1 = ph.psum("tp1", [128, 128], BF16)
        eps_ap = ph.fconst(eps)
        V, S, P, G, Y = ph.V, ph.S, ph.P, ph.G, ph.Y

        with nc.Block() as blk:
            @blk.vector
            def _(_v):
                G.dma_start(id_sb[:], ident)
                P.wait(G)
                evac_marks = {}
                st_done = [0, 0]
                pe_done = [0, 0]
                for i in range(tiles):
                    b = i % 2
                    xb = x_sb[:, b * n_feat:(b + 1) * n_feat]
                    sb = sg_sb[:, b * n_feat:(b + 1) * n_feat]
                    gb = gg_sb[:, b * n_feat:(b + 1) * n_feat]
                    xnb = xn_sb[:, b * n_feat:(b + 1) * n_feat]
                    if i >= 2:
                        Y.wait_dch(Y, 2 + b, st_done[b])
                    Y.dma_start(xb, x[i * 128:(i + 1) * 128, :], ch=b)
                    Y.dma_start(sb, s_in[i * 128:(i + 1) * 128, :], ch=b)
                    Y.dma_start(gb, g_in[i * 128:(i + 1) * 128, :], ch=b)
                    V.wait(Y)
                    V.wait(V)
                    if i >= 2:
                        V.wait_ev(P, pe_done[b])
                    V.tensor_tensor(sgf[:], sb, gb, op=ALU.mult)
                    V.wait(V)
                    V.tensor_tensor(xb, xb, sgf[:], op=ALU.add)
                    V.wait(V)
                    nc.sync.wait_ge(V.sem, V.n)
                    Y.dma_start(x1[i * 128:(i + 1) * 128, :], xb, ch=2 + b)
                    S.wait(V)
                    emit_ln_tile(ph, xb, xnb, mean, rstd, scr, ssq, s1,
                                 eps_ap, n_feat)
                    nc.sync.wait_ge(V.sem, V.n)
                    Y.dma_start(xn_tm[i * 128:(i + 1) * 128, :], xnb, ch=2 + b)
                    st_done[b] = Y.dmark(2 + b)
                    P.wait(V)
                    out_cols = [fm_sb[:, c * tiles * 128 + i * 128:
                                      c * tiles * 128 + (i + 1) * 128]
                                for c in range(nchunk)]
                    emit_transpose_set(ph, xnb, out_cols, [tp0, tp1],
                                       id_sb, nchunk, evac_marks)
                    pe_done[b] = P.n
                nc.sync.wait_ge(V.sem, V.n)
                Y.dma_start(xn_fm.rearrange("(c p) t -> p c t", p=128),
                            fm_sb[:].rearrange("p (c t) -> p c t", t=tiles * 128),
                            ch=2)
                Y.wait(Y); G.wait(G)
        return ph


def build_moe(tiles=PT // 128, name="moe", ph=None):
    if ph is None:
        ph = Ph(name)
    nc = ph.nc
    E = 8
    xn_fm = ph.io("xn_fm", [D, tiles * 128], BF16, "ExternalInput")
    rw = ph.io("rw", [D, E], F32, "ExternalInput")
    rb = ph.io("rb", [E], F32, "ExternalInput")
    ew = ph.io("ew", [E, D, D], F32, "ExternalInput")
    eb = ph.io("eb", [E, D], F32, "ExternalInput")
    ow = ph.io("ow", [D, D], F32, "ExternalInput")
    ob = ph.io("ob", [D], F32, "ExternalInput")
    g_tm = ph.io("g_tm", [tiles * 128, D], BF16, "ExternalOutput")
    ident = ph.const_in("ident", _const_identity_bf16())
    onesc = ph.const_in("onesc", np.ones((1, 128), NPBF))
    nch = D // 128
    TL = tiles * 128

    with ph.scope():
        xnf = ph.sbuf("xnf", [128, nch * TL], BF16)
        rw_sb = ph.sbuf("rw_sb", [128, nch * E], BF16)
        rb_sb = ph.sbuf("rb_sb", [1, E], BF16)
        eb_sb = ph.sbuf("eb_sb", [1, 2 * 1024], BF16)
        ob_sb = ph.sbuf("ob_sb", [1, 2 * 1024], BF16)
        wx = [ph.sbuf("wx0", [128, nch * 1024], BF16),
              ph.sbuf("wx1", [128, nch * 1024], BF16)]
        ones_sb = ph.sbuf("ones_sb", [1, 128], BF16)
        id_sb = ph.sbuf("id_sb", [128, 128], BF16)
        z_sb = ph.sbuf("z_sb", [128, E], F32)
        t8 = ph.sbuf("t8", [128, E], F32)
        nmax = ph.sbuf("nmax", [128, 1], F32)
        den = ph.sbuf("den", [128, 1], F32)
        selm = ph.sbuf("selm", [128, E], F32)
        hg = ph.sbuf("hg", [128, tiles * E], F32)
        comb = ph.sbuf("comb", [128, tiles * D], BF16)
        cfm = ph.sbuf("cfm", [128, D], BF16)
        gt = ph.sbuf("gt", [128, 2 * D], BF16)
        gel = [[ph.sbuf("gel%d_%d" % (b, j), [128, 512], F32) for j in range(3)]
               for b in range(4)]
        tp0 = ph.psum("tp0", [128, 128], BF16)
        tp1 = ph.psum("tp1", [128, 128], BF16)
        pr = ph.psum("pr", [128, 8], F32)
        pm = [ph.psum("pm%d" % b, [128, 512], F32) for b in range(4)]
        V, S, P, G, Y = ph.V, ph.S, ph.P, ph.G, ph.Y
        c1 = 2.0 * C1_GELU
        c2 = 2.0 * C2_GELU

        with nc.Block() as blk:
            @blk.vector
            def _(_v):
                G.dma_start(id_sb[:], ident)
                G.dma_start(ones_sb[:], onesc)
                G.dma_start(xnf[:].rearrange("p (c t) -> p c t", t=TL),
                            xn_fm.rearrange("(c p) t -> p c t", p=128))
                G.dma_start(rw_sb[:].rearrange("p (c n) -> p c n", n=E),
                            rw.rearrange("(c p) n -> p c n", p=128))
                G.dma_start(rb_sb[:], rb[None, :])
                V.memset(comb[:], 0.0)
                P.wait(G); V.wait(G); S.wait(G)
                for i in range(tiles):
                    P.wait(V)
                    V.wait(S)
                    for c in range(nch):
                        P.matmul(pr[:], xnf[:, c * TL + i * 128: c * TL + (i + 1) * 128],
                                 rw_sb[:, c * E:(c + 1) * E],
                                 start=(c == 0), stop=False)
                    P.matmul(pr[:], ones_sb[:], rb_sb[:], start=False, stop=True)
                    V.wait(P)
                    V.reduce_max(nmax[:], pr[:], axis=AX.X, negate=True)
                    V.wait(V); S.wait(V)
                    S.activation(z_sb[:], pr[:], AF.Exp, bias=nmax[:])
                    V.wait(S)
                    V.max(out=t8[:], in_=z_sb[:])
                    V.wait(V)
                    V.tensor_tensor(den[:], t8[:, 0:1], t8[:, 1:2], op=ALU.add)
                    V.wait(V)
                    V.reciprocal(den[:], den[:])
                    V.wait(V)
                    V.tensor_scalar(selm[:], z_sb[:], t8[:, 1:2], None, op0=ALU.is_ge)
                    V.wait(V)
                    V.tensor_tensor(selm[:], z_sb[:], selm[:], op=ALU.mult)
                    V.wait(V)
                    V.tensor_scalar(hg[:, i * E:(i + 1) * E], selm[:], den[:], None,
                                    op0=ALU.mult)
                wx_rd = [0, 0]
                ld_done = [0, 0]
                vmark = {}
                for ei in range(2 * E + 1):
                    if ei < 2 * E:
                        e, half = divmod(ei, 2)
                        buf = ei % 2
                        if ei >= 2:
                            G.raw.wait_ge(P.sem, wx_rd[buf])
                        G.dma_start(wx[buf][:].rearrange("p (c n) -> p c n", n=1024),
                                    ew[e].rearrange("(c p) n -> p c n", p=128)
                                    [:, :, half * 1024:(half + 1) * 1024], ch=1 + buf)
                        G.dma_start(eb_sb[0:1, buf * 1024:(buf + 1) * 1024],
                                    eb[e][None, half * 1024:(half + 1) * 1024], ch=1 + buf)
                        ld_done[buf] = G.dmark(1 + buf)
                    if ei >= 1:
                        e, half = divmod(ei - 1, 2)
                        buf = (ei - 1) % 2
                        P.wait_dch(G, 1 + buf, ld_done[buf])
                        for i in range(tiles):
                            for nt in range(2):
                                pb = pm[2 * (buf) + nt]
                                gl = gel[2 * buf + nt]
                                P.wait_ev(V, vmark.get(2 * buf + nt, 0))
                                for c in range(nch):
                                    P.matmul(pb[:],
                                             xnf[:, c * TL + i * 128: c * TL + (i + 1) * 128],
                                             wx[buf][:, c * 1024 + nt * 512:
                                                     c * 1024 + (nt + 1) * 512],
                                             start=(c == 0), stop=False)
                                P.matmul(pb[:], ones_sb[:],
                                         eb_sb[0:1, buf * 1024 + nt * 512:
                                               buf * 1024 + (nt + 1) * 512],
                                         start=False, stop=True)
                                S.wait(P); S.wait(V)
                                S.activation(gl[0][:], pb[:], AF.Square)
                                V.wait(S)
                                V.tensor_scalar(gl[1][:], gl[0][:], c2, c1,
                                                op0=ALU.mult, op1=ALU.add)
                                V.wait(V)
                                V.tensor_tensor(gl[1][:], gl[1][:], pb[:], op=ALU.mult)
                                S.wait(V)
                                S.activation(gl[2][:], gl[1][:], AF.Sigmoid)
                                V.wait(S)
                                V.scalar_tensor_tensor(gl[2][:], gl[2][:],
                                                       hg[:, i * E + e: i * E + e + 1],
                                                       pb[:], op0=ALU.mult, op1=ALU.mult)
                                vmark[2 * buf + nt] = V.n
                                V.wait(V)
                                cs = comb[:, i * D + half * 1024 + nt * 512:
                                          i * D + half * 1024 + (nt + 1) * 512]
                                V.tensor_tensor(cs, cs, gl[2][:], op=ALU.add)
                        wx_rd[buf] = P.n
                G.raw.wait_ge(P.sem, P.n)
                G.dma_start(wx[0][:].rearrange("p (c n) -> p c n", n=1024),
                            ow.rearrange("(c p) n -> p c n", p=128)[:, :, 0:1024], ch=1)
                G.dma_start(wx[1][:].rearrange("p (c n) -> p c n", n=1024),
                            ow.rearrange("(c p) n -> p c n", p=128)[:, :, 1024:2048], ch=2)
                G.dma_start(ob_sb[0:1, 0:1024], ob[None, 0:1024], ch=1)
                G.dma_start(ob_sb[0:1, 1024:2048], ob[None, 1024:2048], ch=2)
                P.wait(G); V.wait(G)
                evac_marks = {}
                st_done = [0, 0]
                mmev = {}
                for i in range(tiles):
                    b = i % 2
                    gb = gt[:, b * D:(b + 1) * D]
                    if i >= 2:
                        S.wait_dch(Y, b, st_done[b])
                    V.wait(V)
                    out_cols = [cfm[:, c * 128:(c + 1) * 128] for c in range(nch)]
                    if i > 0:
                        P.wait(V)
                    in_sl = [comb[:, i * D + c * 128: i * D + (c + 1) * 128]
                             for c in range(nch)]
                    emit_transpose_set(ph, None, out_cols, [tp0, tp1], id_sb, nch,
                                       evac_marks, in_slices=in_sl)
                    for nt in range(4):
                        pb = pm[nt % 2]
                        P.wait_ev(S, mmev.get(nt % 2, 0))
                        P.wait(V)
                        wxh = wx[nt // 2]
                        off = (nt % 2) * 512
                        for c in range(nch):
                            P.matmul(pb[:], cfm[:, c * 128:(c + 1) * 128],
                                     wxh[:, c * 1024 + off: c * 1024 + off + 512],
                                     start=(c == 0), stop=False)
                        P.matmul(pb[:], ones_sb[:],
                                 ob_sb[0:1, nt * 512:(nt + 1) * 512],
                                 start=False, stop=True)
                        S.wait(P)
                        S.activation(gb[:, nt * 512:(nt + 1) * 512], pb[:], AF.Silu)
                        mmev[nt % 2] = S.n
                    nc.sync.wait_ge(S.sem, S.n)
                    Y.dma_start(g_tm[i * 128:(i + 1) * 128, :], gb, ch=b)
                    st_done[b] = Y.dmark(b)
                Y.wait(Y); G.wait(G)
        return ph


def build_gla(TQ=2048, Ttot=T, name="gla", stage=3, ph=None, src_blk=None):
    if ph is None:
        ph = Ph(name)
    nc = ph.nc
    NQ = Ttot // TQ
    NG = TQ // 128
    if src_blk is None:
        src_blk = Ttot
    nsb = Ttot // src_blk
    xn_fm = ph.io("xn_fm", [nsb * D, src_blk], BF16, "ExternalInput")
    wq = ph.io("wq", [D, 256], F32, "ExternalInput")
    wk = ph.io("wk", [D, 256], F32, "ExternalInput")
    wi = ph.io("wi", [D, 256], F32, "ExternalInput")
    o_tm = ph.io("o_tm", [Ttot, 256], BF16, "ExternalOutput")
    ident = ph.const_in("ident", _const_identity_bf16())
    negu = ph.const_in("negu", _const_neg_u32_blockdiag())
    mask32 = ph.const_in("mask32", _const_mask32())
    nch = D // 128

    with ph.scope():
        xnf = ph.sbuf("xnf", [128, nch * TQ], BF16)
        wq_sb = ph.sbuf("wq_sb", [128, nch * 256], BF16)
        wk_sb = ph.sbuf("wk_sb", [128, nch * 256], BF16)
        wi_sb = ph.sbuf("wi_sb", [128, nch * 256], BF16)
        id_sb = ph.sbuf("id_sb", [128, 128], BF16)
        negu_sb = ph.sbuf("negu_sb", [128, 128], BF16)
        m32_sb = ph.sbuf("m32_sb", [32, 32], F32)
        qf = ph.sbuf("qf", [128, 2 * TQ], BF16)
        kf = ph.sbuf("kf", [128, 2 * TQ], BF16)
        spf = ph.sbuf("spf", [128, 2 * TQ], BF16)
        vt = ph.sbuf("vt", [128, NG * 256], BF16)
        sa1 = ph.sbuf("sa1", [128, 512], F32)
        sr1 = ph.sbuf("sr1", [128, 512], F32)
        sp_tm = ph.sbuf("sp_tm", [128, 256], BF16)
        eG = ph.sbuf("eG", [128, 2 * 128], F32)
        qt_sb = ph.sbuf("qt_sb", [128, 256], BF16)
        kc_sb = ph.sbuf("kc_sb", [128, 256], BF16)
        khf_sb = ph.sbuf("khf_sb", [128, 256], BF16)
        kht_sb = ph.sbuf("kht_sb", [128, 256], BF16)
        Pt_sb = ph.sbuf("Pt_sb", [128, 2 * 32], BF16)
        S32 = ph.sbuf("S32", [128, 2 * 128], F32)
        Sbf = ph.sbuf("Sbf", [128, 2 * 128], BF16)
        o_q = ph.sbuf("o_q", [128, NG * 256], BF16)
        pmA = ph.psum("pmA", [128, 512], F32)
        pmB = ph.psum("pmB", [128, 512], F32)
        tpA = ph.psum("tpA", [128, 128], BF16)
        tpB = ph.psum("tpB", [128, 128], BF16)
        pP = ph.psum("pP", [128, 32], F32)
        po = ph.psum("po", [128, 128], F32)
        pS = ph.psum("pS", [128, 128], F32)
        V, S, P, G, Y = ph.V, ph.S, ph.P, ph.G, ph.Y

        with nc.Block() as blk:
            @blk.vector
            def _(_v):
                G.dma_start(id_sb[:], ident)
                G.dma_start(negu_sb[:], negu)
                G.dma_start(m32_sb[:], mask32)
                emit_weight_cast_load(ph, wq, wq_sb, D, 256)
                emit_weight_cast_load(ph, wk, wk_sb, D, 256)
                emit_weight_cast_load(ph, wi, wi_sb, D, 256)
                V.memset(S32[:], 0.0)
                V.memset(Sbf[:], 0.0)
                P.wait(G); V.wait(G); S.wait(G)

                for qi in range(NQ):
                    V.wait(P)
                    G.raw.wait_ge(P.sem, P.n)
                    xnfv = xn_fm.rearrange("(r c p) t -> p r c t", p=128,
                                           c=D // 128)
                    for off in range(0, TQ, min(TQ, src_blk)):
                        gt = qi * TQ + off
                        r, tl = gt // src_blk, gt % src_blk
                        w = min(TQ, src_blk)
                        G.dma_start(xnf[:].rearrange("p (c t) -> p c t", t=TQ)
                                    [:, :, off:off + w],
                                    xnfv[:, r, :, tl:tl + w], ch=1)
                    P.wait(G); S.wait(G); V.wait(G)
                    S.wait(Y)
                    PW = min(512, TQ)
                    for h in range(2):
                        for tt in range(TQ // PW):
                            sl = slice(tt * PW, (tt + 1) * PW)
                            pb = (pmA, pmB)[tt % 2]
                            P.wait(S); P.wait(V)
                            for c in range(nch):
                                P.matmul(pb[:, 0:PW], wq_sb[:, c * 256 + h * 128: c * 256 + (h + 1) * 128],
                                         xnf[:, c * TQ + tt * PW: c * TQ + (tt + 1) * PW],
                                         start=(c == 0), stop=(c == nch - 1))
                            S.wait(P)
                            S.activation(qf[:, h * TQ:(h + 1) * TQ][:, sl], pb[:, 0:PW],
                                         AF.Relu, scale=GLA_SCALE)
                            P.wait(S)
                            for c in range(nch):
                                P.matmul(pb[:, 0:PW], wk_sb[:, c * 256 + h * 128: c * 256 + (h + 1) * 128],
                                         xnf[:, c * TQ + tt * PW: c * TQ + (tt + 1) * PW],
                                         start=(c == 0), stop=(c == nch - 1))
                            S.wait(P); V.wait(P)
                            S.activation(kf[:, h * TQ:(h + 1) * TQ][:, sl], pb[:, 0:PW], AF.Sigmoid)
                            S.wait(S)
                            S.activation(sa1[:, 0:PW], pb[:, 0:PW], AF.Abs)
                            S.activation(sr1[:, 0:PW], pb[:, 0:PW], AF.Relu)
                            S.wait(S)
                            S.activation(sa1[:, 0:PW], sa1[:, 0:PW], AF.Exp, scale=-1.0)
                            S.wait(S)
                            S.activation(sa1[:, 0:PW], sa1[:, 0:PW], AF.Ln, bias=1.0)
                            V.wait(S); V.wait(V)
                            V.tensor_tensor(spf[:, h * TQ:(h + 1) * TQ][:, sl],
                                            sr1[:, 0:PW], sa1[:, 0:PW], op=ALU.add)
                            V.wait(V)
                    vev = {}
                    for tt in range(NG):
                        pb, pbi = (pmA, 0) if (tt % 2 == 0) else (pmB, 1)
                        P.wait_ev(V, vev.get(pbi, 0))
                        P.wait(S); P.wait(V)
                        for c in range(nch):
                            P.matmul(pb[:, 0:256],
                                     xnf[:, c * TQ + tt * 128: c * TQ + (tt + 1) * 128],
                                     wi_sb[:, c * 256:(c + 1) * 256],
                                     start=(c == 0), stop=(c == nch - 1))
                        V.wait(P)
                        V.tensor_copy(vt[:, tt * 256:(tt + 1) * 256], pb[:, 0:256])
                        vev[pbi] = V.n
                    for gi in range(NG if stage >= 1 else 0):
                        gsl = slice(gi * 128, (gi + 1) * 128)
                        P.wait(V)
                        em = {}
                        emit_transpose_set(ph, None,
                                           [sp_tm[:, 0:128], sp_tm[:, 128:256]],
                                           [tpA, tpB], id_sb, 2, em,
                                           in_slices=[spf[:, h * TQ + gi * 128:
                                                          h * TQ + (gi + 1) * 128]
                                                      for h in range(2)])
                        for h in range(2 if stage >= 2 else 0):
                            pg = (pmA, pmB)[h]
                            P.wait(V); P.wait(S)
                            P.matmul(pg[:, 0:128], sp_tm[:, h * 128:(h + 1) * 128],
                                     negu_sb[:], start=True, stop=True)
                            S.wait(P); S.wait(V)
                            S.activation(eG[:, h * 128:(h + 1) * 128], pg[:, 0:128], AF.Exp)
                            S.wait(S)
                            S.activation(sr1[:, 0:128], pg[:, 0:128], AF.Exp, scale=-1.0)
                            V.wait(S)
                            V.tensor_tensor(qt_sb[:, h * 128:(h + 1) * 128],
                                            qf[:, h * TQ:(h + 1) * TQ][:, gsl],
                                            eG[:, h * 128:(h + 1) * 128], op=ALU.mult)
                            V.tensor_tensor(kc_sb[:, h * 128:(h + 1) * 128],
                                            kf[:, h * TQ:(h + 1) * TQ][:, gsl],
                                            sr1[:, 0:128], op=ALU.mult)
                            V.wait(V)
                            eslice = eG[:, h * 128:(h + 1) * 128][:, 31::32]
                            V.tensor_tensor(
                                khf_sb[:, h * 128:(h + 1) * 128]
                                .rearrange("p (c i) -> p c i", i=32),
                                kc_sb[:, h * 128:(h + 1) * 128]
                                .rearrange("p (c i) -> p c i", i=32),
                                eslice[:, :, None].broadcast_to([128, 4, 32]),
                                op=ALU.mult)
                            V.wait(V)
                            em2 = {}
                            P.wait(V)
                            emit_transpose_set(ph, None,
                                               [kht_sb[:, h * 128:(h + 1) * 128]],
                                               [tpA, tpB], id_sb, 1, em2,
                                               in_slices=[khf_sb[:, h * 128:(h + 1) * 128]])
                            for ci in range(4 if stage >= 3 else 0):
                                rs = slice(32 * ci, 32 * ci + 32)
                                P.wait(V); P.wait(S)
                                P.matmul(pP[rs, 0:32],
                                         kc_sb[:, h * 128 + 32 * ci: h * 128 + 32 * ci + 32],
                                         qt_sb[:, h * 128 + 32 * ci: h * 128 + 32 * ci + 32],
                                         start=True, stop=True,
                                         tile_position=(0, 32 * ci))
                                V.wait(P)
                                V.tensor_tensor(Pt_sb[rs, h * 32:(h + 1) * 32],
                                                pP[rs, 0:32], m32_sb[:], op=ALU.mult)
                                P.wait(V)
                                P.matmul(po[rs, :],
                                         qt_sb[:, h * 128 + 32 * ci: h * 128 + 32 * ci + 32],
                                         Sbf[:, h * 128:(h + 1) * 128],
                                         start=True, stop=False, tile_position=(0, 32 * ci))
                                P.matmul(po[rs, :],
                                         Pt_sb[rs, h * 32:(h + 1) * 32],
                                         vt[rs, gi * 256 + h * 128: gi * 256 + (h + 1) * 128],
                                         start=False, stop=True,
                                         tile_position=(32 * ci, 32 * ci))
                                P.matmul(pS[:],
                                         kht_sb[rs, h * 128:(h + 1) * 128],
                                         vt[rs, gi * 256 + h * 128: gi * 256 + (h + 1) * 128],
                                         start=True, stop=True,
                                         tile_position=(32 * ci, 0))
                                S.wait(P)
                                S.activation(o_q[rs, gi * 256 + h * 128:
                                                 gi * 256 + (h + 1) * 128],
                                             po[rs, :], AF.Copy)
                                V.wait(P)
                                V.scalar_tensor_tensor(
                                    S32[:, h * 128:(h + 1) * 128],
                                    S32[:, h * 128:(h + 1) * 128],
                                    eG[:, h * 128 + 32 * ci + 31: h * 128 + 32 * ci + 32],
                                    pS[:], op0=ALU.mult, op1=ALU.add)
                                V.wait(V)
                                V.tensor_copy(Sbf[:, h * 128:(h + 1) * 128],
                                              S32[:, h * 128:(h + 1) * 128])
                                V.wait(V)
                    nc.sync.wait_ge(S.sem, S.n)
                    Y.dma_start(o_tm.rearrange("(tt p) n -> p tt n", p=128)
                                [:, qi * NG:(qi + 1) * NG, :],
                                o_q[:].rearrange("p (tt n) -> p tt n", n=256))
                Y.wait(Y); G.wait(G)
        return ph


def _const_l1_rope():
    half = 16
    inv_freq = 1.0 / 10000.0 ** (np.arange(0, half, 2, dtype=np.float32) / half)
    t = np.arange(T, dtype=np.float32) / 40.0
    freqs = t[:, None] * inv_freq[None]
    emb = np.concatenate([freqs, freqs], -1)
    return np.cos(emb).astype(np.float32), np.sin(emb).astype(np.float32)


def _const_maskT():
    m = np.zeros((128, 128), np.float32)
    tk = np.arange(128)[:, None]
    tq = np.arange(128)[None, :]
    m[tk > tq] = -1e30
    return m


def build_l1proj(tiles=PT // 128, name="l1proj", ph=None):
    if ph is None:
        ph = Ph(name)
    nc = ph.nc
    TL = tiles * 128
    nch = D // 128
    xn_fm = ph.io("xn_fm", [D, TL], BF16, "ExternalInput")
    wdkv = ph.io("wdkv", [D, 512], F32, "ExternalInput")
    bdkv = ph.io("bdkv", [512], F32, "ExternalInput")
    wdq = ph.io("wdq", [D, 512], F32, "ExternalInput")
    bdq = ph.io("bdq", [512], F32, "ExternalInput")
    wr = ph.io("wr", [D], F32, "ExternalInput")
    ckv = ph.io("ckv", [512, TL], BF16, "ExternalOutput")
    cq = ph.io("cq", [512, TL], BF16, "ExternalOutput")
    identf_l1p = ph.const_in("identf", _const_identity_f32())
    logits = ph.io("logits", [tiles * 128], F32, "ExternalOutput")

    with ph.scope():
        xnf = ph.sbuf("xnf", [128, nch * TL], BF16)
        wkv_sb = ph.sbuf("wkv_sb", [128, nch * 512], BF16)
        wq_sb = ph.sbuf("wq_sb", [128, nch * 512], BF16)
        bkv_sb = ph.sbuf("bkv_sb", [128, 4], F32)
        bq_sb = ph.sbuf("bq_sb", [128, 4], F32)
        wr_sb = ph.sbuf("wr_sb", [128, nch], BF16)
        ck_sb = ph.sbuf("ck_sb", [128, 4 * TL], BF16)
        cq_sb = ph.sbuf("cq_sb", [128, 4 * TL], BF16)
        lg_sb = ph.sbuf("lg_sb", [128, tiles], F32)
        lgT_sb = ph.sbuf("lgT_sb", [tiles, 128], F32)
        idf2_sb = ph.sbuf("idf2_sb", [128, 128], F32)
        pm = [ph.psum("pm%d" % b, [128, 512], F32) for b in range(2)]
        pl = ph.psum("pl", [128, 1], F32)
        plT = ph.psum("plT", [128, 128], F32)
        V, S, P, G, Y = ph.V, ph.S, ph.P, ph.G, ph.Y

        with nc.Block() as blk:
            @blk.vector
            def _(_v):
                G.dma_start(xnf[:].rearrange("p (c t) -> p c t", t=TL),
                            xn_fm.rearrange("(c p) t -> p c t", p=128))
                emit_weight_cast_load(ph, wdkv, wkv_sb, D, 512)
                emit_weight_cast_load(ph, wdq, wq_sb, D, 512)
                G.dma_start(bkv_sb[:], bdkv.rearrange("(c p) -> p c", p=128),
                            allow_slow_non_contiguous=True)
                G.dma_start(bq_sb[:], bdq.rearrange("(c p) -> p c", p=128),
                            allow_slow_non_contiguous=True)
                G.dma_start(wr_sb[:], wr.rearrange("(c p) -> p c", p=128),
                            allow_slow_non_contiguous=True)
                P.wait(G); S.wait(G); V.wait(G)
                for which, wsb, bsb, csb in ((0, wkv_sb, bkv_sb, ck_sb),
                                             (1, wq_sb, bq_sb, cq_sb)):
                    for cc in range(4):
                        for ts in range(TL // 512):
                            pb = pm[ts % 2]
                            P.wait(V)
                            for c in range(nch):
                                P.matmul(pb[:],
                                         wsb[:, c * 512 + cc * 128: c * 512 + (cc + 1) * 128],
                                         xnf[:, c * TL + ts * 512: c * TL + (ts + 1) * 512],
                                         start=(c == 0), stop=(c == nch - 1))
                            V.wait(P)
                            V.tensor_tensor(csb[:, cc * TL + ts * 512: cc * TL + (ts + 1) * 512],
                                            pb[:], bsb[:, cc:cc + 1].broadcast_to([128, 512]),
                                            op=ALU.add)
                for i in range(tiles):
                    P.wait(V)
                    for c in range(nch):
                        P.matmul(pl[:], xnf[:, c * TL + i * 128: c * TL + (i + 1) * 128],
                                 wr_sb[:, c:c + 1], start=(c == 0), stop=(c == nch - 1))
                    V.wait(P)
                    V.tensor_copy(lg_sb[:, i:i + 1], pl[:])
                nc.sync.wait_ge(S.sem, S.n)
                nc.sync.wait_ge(V.sem, V.n)
                Y.dma_start(ckv.rearrange("(c p) t -> p c t", p=128),
                            ck_sb[:].rearrange("p (c t) -> p c t", t=TL))
                Y.dma_start(cq.rearrange("(c p) t -> p c t", p=128),
                            cq_sb[:].rearrange("p (c t) -> p c t", t=TL))
                Y.dma_start(logits.rearrange("(f p) -> f p", p=128), lgT_sb[:])
                Y.wait(Y); G.wait(G)
        return ph


def build_l1flash(Ttot=T, name="l1flash", ph=None, src_blk=None):
    if ph is None:
        ph = Ph(name)
    nc = ph.nc
    NT = Ttot // 128
    if src_blk is None:
        src_blk = Ttot
    nsb = Ttot // src_blk
    ckv = ph.io("ckv", [nsb * 512, src_blk], BF16, "ExternalInput")
    cqf = ph.io("cqf", [nsb * 512, src_blk], BF16, "ExternalInput")
    xn_fm = ph.io("xn_fm", [nsb * D, src_blk], BF16, "ExternalInput")
    wuk = ph.io("wuk", [512, 192], F32, "ExternalInput")
    buk = ph.io("buk", [192], F32, "ExternalInput")
    wuv = ph.io("wuv", [512, 256], F32, "ExternalInput")
    buv = ph.io("buv", [256], F32, "ExternalInput")
    wuq = ph.io("wuq", [512, 192], F32, "ExternalInput")
    buq = ph.io("buq", [192], F32, "ExternalInput")
    wqr = ph.io("wqr", [512, 64], F32, "ExternalInput")
    bqr = ph.io("bqr", [64], F32, "ExternalInput")
    wkr = ph.io("wkr", [D, 64], F32, "ExternalInput")
    bkr = ph.io("bkr", [64], F32, "ExternalInput")
    o_tm = ph.io("o_tm", [Ttot, 256], BF16, "ExternalOutput")
    ident = ph.const_in("ident", _const_identity_bf16())
    identf = ph.const_in("identf", _const_identity_f32())
    cos_c, sin_c = _const_l1_rope()
    cosc = ph.const_in("cosc", cos_c[:Ttot])
    sinc = ph.const_in("sinc", sin_c[:Ttot])
    maskT = ph.const_in("maskT", _const_maskT())
    onesb = ph.const_in("onesb", np.ones((1, 128), NPBF))
    onescol = ph.const_in("onescol", np.ones((128, 1), NPBF))
    onesf = ph.const_in("onesf", np.ones((1, 128), np.float32))
    nch = D // 128
    SC = float(L1_SCALE)

    with ph.scope():
        kfull = [ph.sbuf("kfull%d" % h, [128, Ttot], BF16) for h in range(2)]
        qfull = [ph.sbuf("qfull%d" % h, [128, Ttot], BF16) for h in range(2)]
        vsb = [ph.sbuf("vsb%d" % h, [128, Ttot], BF16) for h in range(2)]
        ot_sb = ph.sbuf("ot_sb", [128, (Ttot // 128) * 256], BF16)
        cstr = ph.sbuf("cstr", [128, 2 * 8 * 128], BF16)
        xstr = ph.sbuf("xstr", [128, 2 * nch * 128], BF16)
        wuk_sb = ph.sbuf("wuk_sb", [128, 4 * 192], BF16)
        wuq_sb = ph.sbuf("wuq_sb", [128, 4 * 192], BF16)
        wuv_sb = ph.sbuf("wuv_sb", [128, 4 * 256], BF16)
        wqr_sb = ph.sbuf("wqr_sb", [128, 4 * 64], BF16)
        wkr_sb = ph.sbuf("wkr_sb", [128, nch * 64], BF16)
        buk_sb = ph.sbuf("buk_sb", [96, 2], F32)
        buq_sb = ph.sbuf("buq_sb", [96, 2], F32)
        buv_sb = ph.sbuf("buv_sb", [1, 256], BF16)
        bqr_sb = ph.sbuf("bqr_sb", [1, 64], BF16)
        bkr_sb = ph.sbuf("bkr_sb", [1, 64], BF16)
        cos_sb = ph.sbuf("cos_sb", [128, NT * 16], F32)
        sin_sb = ph.sbuf("sin_sb", [128, NT * 16], F32)
        rot_sb = ph.sbuf("rot_sb", [128, 64], BF16)
        rA = ph.sbuf("rA", [128, 16], F32)
        rB = ph.sbuf("rB", [128, 16], F32)
        id_sb = ph.sbuf("id_sb", [128, 128], BF16)
        idf_sb = ph.sbuf("idf_sb", [128, 128], F32)
        mT_sb = ph.sbuf("mT_sb", [128, 128], F32)
        on_sb = ph.sbuf("on_sb", [1, 128], BF16)
        onc_sb = ph.sbuf("onc_sb", [128, 1], BF16)
        onf_sb = ph.sbuf("onf_sb", [1, 128], F32)
        p_sb = [ph.sbuf("p_sb%d" % b, [128, 512], BF16) for b in range(2)]
        mrow = ph.sbuf("mrow", [128, 1], F32)
        m1 = ph.sbuf("m1", [1, 1], F32)
        mbc = ph.sbuf("mbc", [128, 1], F32)
        nbias = ph.sbuf("nbias", [128, 1], F32)
        lsb = ph.sbuf("lsb", [128, 1], F32)
        linv = ph.sbuf("linv", [128, 1], F32)
        otmp = ph.sbuf("otmp", [128, 128], BF16)
        pB = ph.psum("pB", [128, 512], F32)
        po = ph.psum("po", [128, 128], F32)
        pl = ph.psum("pl", [128, 1], F32)
        pmax = ph.psum("pmax", [128, 128], F32)
        tpA = ph.psum("tpA", [128, 128], BF16)
        V, S, P, G, Y = ph.V, ph.S, ph.P, ph.G, ph.Y

        with nc.Block() as blk:
            @blk.vector
            def _(_v):
                G.dma_start(id_sb[:], ident)
                G.dma_start(idf_sb[:], identf)
                G.dma_start(mT_sb[:], maskT)
                G.dma_start(on_sb[:], onesb)
                G.dma_start(onc_sb[:], onescol)
                G.dma_start(onf_sb[:], onesf)
                G.dma_start(cos_sb[:].rearrange("p (t f) -> p t f", f=16),
                            cosc.rearrange("(t p) f -> p t f", p=128))
                G.dma_start(sin_sb[:].rearrange("p (t f) -> p t f", f=16),
                            sinc.rearrange("(t p) f -> p t f", p=128))
                emit_weight_cast_load(ph, wuk, wuk_sb, 512, 192)
                emit_weight_cast_load(ph, wuq, wuq_sb, 512, 192)
                emit_weight_cast_load(ph, wuv, wuv_sb, 512, 256)
                emit_weight_cast_load(ph, wqr, wqr_sb, 512, 64)
                emit_weight_cast_load(ph, wkr, wkr_sb, D, 64)
                G.dma_start(buk_sb[:], buk.rearrange("(h p) -> p h", p=96),
                            allow_slow_non_contiguous=True)
                G.dma_start(buq_sb[:], buq.rearrange("(h p) -> p h", p=96),
                            allow_slow_non_contiguous=True)
                G.dma_start(buv_sb[:], buv[None, :])
                G.dma_start(bqr_sb[:], bqr[None, :])
                G.dma_start(bkr_sb[:], bkr[None, :])
                P.wait(G); S.wait(G); V.wait(G)

                def rotary_tm(src_ap, dst_ap, ti):
                    cs = cos_sb[:, ti * 16:(ti + 1) * 16]
                    sn = sin_sb[:, ti * 16:(ti + 1) * 16]
                    V.wait(V); V.wait(S); V.wait(P)
                    V.tensor_tensor(rA[:], src_ap[:, 0:16], cs, op=ALU.mult)
                    V.tensor_tensor(rB[:, 0:8], src_ap[:, 8:16], sn[:, 0:8], op=ALU.mult)
                    V.tensor_tensor(rB[:, 8:16], src_ap[:, 0:8], sn[:, 8:16], op=ALU.mult)
                    V.wait(V)
                    V.tensor_tensor(dst_ap[:, 0:8], rA[:, 0:8], rB[:, 0:8], op=ALU.subtract)
                    V.tensor_tensor(dst_ap[:, 8:16], rA[:, 8:16], rB[:, 8:16], op=ALU.add)
                    V.wait(V)

                for ti in range(NT):
                    sl = ti % 2
                    xb = xstr[:, sl * nch * 128:(sl + 1) * nch * 128]
                    if ti >= 2:
                        G.raw.wait_ge(P.sem, P.n)
                    rk, tl = (ti * 128) // src_blk, (ti * 128) % src_blk
                    G.dma_start(xb.rearrange("p (c t) -> p c t", t=128),
                                xn_fm.rearrange("(r c p) t -> p r c t", p=128,
                                                c=D // 128)
                                [:, rk, :, tl:tl + 128], ch=1 + sl)
                    P.wait_dch(G, 1 + sl)
                    P.wait(S); P.wait(V)
                    for c in range(nch):
                        P.matmul(pB[:, 0:64], xb[:, c * 128:(c + 1) * 128],
                                 wkr_sb[:, c * 64:(c + 1) * 64],
                                 start=(c == 0), stop=False)
                    P.matmul(pB[:, 0:64], on_sb[:], bkr_sb[:], start=False, stop=True)
                    V.wait(P)
                    V.tensor_copy(rot_sb[:], pB[:, 0:64])
                    for h in range(2):
                        rotary_tm(rot_sb[:, h * 32:(h + 1) * 32],
                                  rot_sb[:, h * 32:(h + 1) * 32], ti)
                    for h in range(2):
                        P.wait(V)
                        P.transpose(tpA[96:128, :], rot_sb[:, h * 32:(h + 1) * 32],
                                    id_sb[:], tile_position=(0, 96))
                        V.wait(P)
                        V.tensor_copy(kfull[h][96:128, ti * 128:(ti + 1) * 128],
                                      tpA[96:128, :])
                G.raw.wait_ge(P.sem, P.n)
                for ti in range(NT):
                    sl = ti % 2
                    cb = cstr[:, sl * 8 * 128:(sl + 1) * 8 * 128]
                    ck_t = cb[:, 0:4 * 128]
                    cq_t = cb[:, 4 * 128:8 * 128]
                    if ti >= 2:
                        G.raw.wait_ge(P.sem, P.n)
                    rk, tl = (ti * 128) // src_blk, (ti * 128) % src_blk
                    G.dma_start(ck_t.rearrange("p (c t) -> p c t", t=128),
                                ckv.rearrange("(r c p) t -> p r c t", p=128, c=4)
                                [:, rk, :, tl:tl + 128], ch=1 + sl)
                    G.dma_start(cq_t.rearrange("p (c t) -> p c t", t=128),
                                cqf.rearrange("(r c p) t -> p r c t", p=128, c=4)
                                [:, rk, :, tl:tl + 128], ch=1 + sl)
                    P.wait_dch(G, 1 + sl)
                    for h in range(2):
                        P.wait(S); P.wait(V)
                        for c in range(4):
                            P.matmul(pB[0:96, 0:128],
                                     wuk_sb[:, c * 192 + h * 96: c * 192 + (h + 1) * 96],
                                     ck_t[:, c * 128:(c + 1) * 128],
                                     start=(c == 0), stop=(c == 3))
                        V.wait(P)
                        V.tensor_tensor(kfull[h][0:96, ti * 128:(ti + 1) * 128],
                                        pB[0:96, 0:128],
                                        buk_sb[:, h:h + 1].broadcast_to([96, 128]),
                                        op=ALU.add)
                        for c in range(4):
                            P.matmul(pmax[0:96, 0:128],
                                     wuq_sb[:, c * 192 + h * 96: c * 192 + (h + 1) * 96],
                                     cq_t[:, c * 128:(c + 1) * 128],
                                     start=(c == 0), stop=(c == 3))
                        V.wait(P)
                        V.tensor_tensor(qfull[h][0:96, ti * 128:(ti + 1) * 128],
                                        pmax[0:96, 0:128],
                                        buq_sb[:, h:h + 1].broadcast_to([96, 128]),
                                        op=ALU.add)
                        P.wait(S); P.wait(V)
                        for c in range(4):
                            P.matmul(po[:], ck_t[:, c * 128:(c + 1) * 128],
                                     wuv_sb[:, c * 256 + h * 128: c * 256 + (h + 1) * 128],
                                     start=(c == 0), stop=False)
                        P.matmul(po[:], on_sb[:],
                                 buv_sb[0:1, h * 128:(h + 1) * 128],
                                 start=False, stop=True)
                        V.wait(P)
                        V.tensor_copy(vsb[h][:, ti * 128:(ti + 1) * 128], po[:])
                        for c in range(4):
                            P.matmul(pl[:] if False else pB[:, 256:288],
                                     cq_t[:, c * 128:(c + 1) * 128],
                                     wqr_sb[:, c * 64 + h * 32: c * 64 + (h + 1) * 32],
                                     start=(c == 0), stop=False)
                        P.matmul(pB[:, 256:288], on_sb[:],
                                 bqr_sb[0:1, h * 32:(h + 1) * 32],
                                 start=False, stop=True)
                        V.wait(P)
                        V.tensor_copy(rot_sb[:, 0:32], pB[:, 256:288])
                        rotary_tm(rot_sb[:, 0:32], rot_sb[:, 0:32], ti)
                        P.wait(V)
                        P.transpose(tpA[96:128, :], rot_sb[:, 0:32],
                                    id_sb[:], tile_position=(0, 96))
                        V.wait(P)
                        V.tensor_copy(qfull[h][96:128, ti * 128:(ti + 1) * 128],
                                      tpA[96:128, :])
                for h in range(2):
                    for qt in range(NT):
                        nf = qt // 4
                        dc = qt - 4 * nf + 1
                        qsl = slice(qt * 128, (qt + 1) * 128)
                        P.wait(S); P.wait(V)
                        for jj in range(dc):
                            tk = 4 * nf + jj
                            P.matmul(pB[:, jj * 128:(jj + 1) * 128],
                                     kfull[h][:, tk * 128:(tk + 1) * 128],
                                     qfull[h][:, qsl], start=True, stop=True)
                        V.wait(P)
                        V.tensor_tensor(pB[:, (dc - 1) * 128: dc * 128],
                                        pB[:, (dc - 1) * 128: dc * 128],
                                        mT_sb[:], op=ALU.add)
                        V.wait(V)
                        V.reduce_max(mrow[:], pB[:, 0:dc * 128], axis=AX.X)
                        P.wait(V)
                        P.transpose(pmax[0:1, :], mrow[:], idf_sb[:])
                        V.wait(P)
                        V.reduce_max(m1[:], pmax[0:1, 0:128], axis=AX.X)
                        V.wait(V)
                        P.wait(V)
                        P.matmul(pmax[:, 0:1], onf_sb[:], m1[:], start=True, stop=True)
                        V.wait(P)
                        V.tensor_copy(mbc[:], pmax[:, 0:1])
                        V.wait(V)
                        V.tensor_scalar(nbias[:], mbc[:], -SC, None, op0=ALU.mult)
                        S.wait(V)
                        S.activation(p_sb[0][:, 0:dc * 128], pB[:, 0:dc * 128],
                                     AF.Exp, bias=nbias[:], scale=SC)
                        P.wait(S)
                        for jj in range(dc):
                            tk = 4 * nf + jj
                            last = (nf == 0 and jj == dc - 1)
                            P.matmul(pl[:], p_sb[0][:, jj * 128:(jj + 1) * 128],
                                     onc_sb[:], start=(jj == 0), stop=last)
                            P.matmul(po[:], p_sb[0][:, jj * 128:(jj + 1) * 128],
                                     vsb[h][:, tk * 128:(tk + 1) * 128],
                                     start=(jj == 0), stop=last)
                        pev = [S.n, 0]
                        for kb in range(nf):
                            buf = kb % 2
                            P.wait_ev(S, pev[buf])
                            P.wait(S)
                            for jj in range(4):
                                tk = 4 * kb + jj
                                P.matmul(pB[:, jj * 128:(jj + 1) * 128],
                                         kfull[h][:, tk * 128:(tk + 1) * 128],
                                         qfull[h][:, qsl], start=True, stop=True)
                            S.wait(P)
                            S.activation(p_sb[buf][:], pB[:], AF.Exp,
                                         bias=nbias[:], scale=SC)
                            pev[buf] = S.n
                            P.wait(S)
                            for jj in range(4):
                                tk = 4 * kb + jj
                                last = (kb == nf - 1 and jj == 3)
                                P.matmul(pl[:], p_sb[buf][:, jj * 128:(jj + 1) * 128],
                                         onc_sb[:], start=False, stop=last)
                                P.matmul(po[:], p_sb[buf][:, jj * 128:(jj + 1) * 128],
                                         vsb[h][:, tk * 128:(tk + 1) * 128],
                                         start=False, stop=last)
                        V.wait(P)
                        V.tensor_copy(lsb[:], pl[:])
                        V.wait(V)
                        V.reciprocal(linv[:], lsb[:])
                        V.wait(V)
                        V.tensor_tensor(ot_sb[:, qt * 256 + h * 128:
                                              qt * 256 + (h + 1) * 128],
                                        po[:],
                                        linv[:].broadcast_to([128, 128]),
                                        op=ALU.mult)
                nc.sync.wait_ge(V.sem, V.n)
                Y.dma_start(o_tm.rearrange("(tt p) n -> p tt n", p=128),
                            ot_sb[:].rearrange("p (tt n) -> p tt n", n=256))
                Y.wait(Y); G.wait(G)
        return ph


def _const_su_ones():
    m = np.zeros((128, 128), np.float32)
    p = np.arange(128)[:, None]
    i = np.arange(128)[None, :]
    m[p < i] = 1.0
    return m.astype(NPBF)


def build_l1sel(name="l1sel", ph=None):
    if ph is None:
        ph = Ph(name)
    nc = ph.nc
    NF = T // 128
    lgi = ph.io("logits", [T], F32, "ExternalInput")
    xn1 = ph.io("xn1_tm", [T, D], BF16, "ExternalInput")
    iotaj = ph.io("iotaj", [128, 128], F32, "ExternalInput")
    fx = ph.io("fx", [128, D], BF16, "ExternalOutput")
    selidx = ph.io("selidx", [128], I32, "ExternalOutput")
    rw = ph.io("rw", [128], F32, "ExternalOutput")
    iota_c = ph.const_in("iota_c",
                         (np.arange(128)[:, None] * NF +
                          np.arange(NF)[None, :]).astype(np.float32))
    suones = ph.const_in("suones", _const_su_ones())
    onesf = ph.const_in("onesf", np.ones((1, 128), np.float32))
    onescf = ph.const_in("onescf", np.ones((128, 1), np.float32))

    with ph.scope():
        lg = ph.sbuf("lg", [128, NF], F32)
        iota_sb = ph.sbuf("iota_sb", [128, NF], F32)
        su_sb = ph.sbuf("su_sb", [128, 128], BF16)
        onf_sb = ph.sbuf("onf_sb", [1, 128], F32)
        oncf_sb = ph.sbuf("oncf_sb", [128, 1], F32)
        idf_sb = ph.sbuf("idf_sb", [128, 128], F32)
        identf = ph.const_in("identf", _const_identity_f32())
        lo = ph.sbuf("lo", [128, 1], F32)
        hi = ph.sbuf("hi", [128, 1], F32)
        mid = ph.sbuf("mid", [128, 1], F32)
        selv = ph.sbuf("selv", [128, 1], F32)
        d1 = ph.sbuf("d1", [128, 1], F32)
        rowc = ph.sbuf("rowc", [128, 1], F32)
        rowcb = ph.sbuf("rowcb", [128, 1], BF16)
        cnt1 = ph.sbuf("cnt1", [1, 1], F32)
        cntb = ph.sbuf("cntb", [128, 1], F32)
        gmx = ph.sbuf("gmx", [128, 1], F32)
        ngmx = ph.sbuf("ngmx", [128, 1], F32)
        cmp = ph.sbuf("cmp", [128, NF], F32)
        mask = ph.sbuf("mask", [128, NF], F32)
        incl = ph.sbuf("incl", [128, NF], F32)
        slots = ph.sbuf("slots", [128, NF], F32)
        sltmp = ph.sbuf("sltmp", [128, NF], F32)
        slot_i = ph.sbuf("slot_i", [128, NF], I32)
        rwe = ph.sbuf("rwe", [128, NF], F32)
        pair_sb = ph.sbuf("pair_sb", [128, 2 * NF], F32)
        base_sb = ph.sbuf("base_sb", [128, 1], F32)
        mypair = ph.sbuf("mypair", [128, 2], F32)
        myidx = ph.sbuf("myidx", [128, 1], I32)
        iotaj_sb = ph.sbuf("iotaj_sb", [128, 128], F32)
        Mf = ph.sbuf("Mf", [128, 128], F32)
        myrw = ph.sbuf("myrw", [128, 1], F32)
        fx_sb = ph.sbuf("fx_sb", [128, D], BF16)
        selp = ph.sbuf("selp", [128, SEL_K // 128], F32)
        seli = ph.sbuf("seli", [128, SEL_K // 128], I32)
        rwn = ph.sbuf("rwn", [128, SEL_K // 128], F32)
        den = ph.sbuf("den", [128, 1], F32)
        pq = ph.psum("pq", [128, 128], F32)
        V, S, P, G, Y = ph.V, ph.S, ph.P, ph.G, ph.Y

        def cross_part_bcast(src_col, op):
            P.wait(V)
            P.transpose(pq[0:1, :], src_col, idf_sb[:])
            V.wait(P)
            V.tensor_reduce(cnt1[:], pq[0:1, 0:128], axis=AX.X, op=op)
            V.wait(V)
            P.wait(V)
            P.matmul(pq[:, 0:1], onf_sb[:], cnt1[:], start=True, stop=True)
            V.wait(P)
            V.tensor_copy(cntb[:], pq[:, 0:1])
            V.wait(V)

        with nc.Block() as blk:
            @blk.vector
            def _(_v):
                G.dma_start(lg[:], lgi.rearrange("(p f) -> p f", f=NF))
                G.dma_start(iota_sb[:], iota_c)
                G.dma_start(su_sb[:], suones)
                G.dma_start(onf_sb[:], onesf)
                G.dma_start(oncf_sb[:], onescf)
                G.dma_start(idf_sb[:], identf)
                G.dma_start(iotaj_sb[:], iotaj)
                P.wait(G); S.wait(G); V.wait(G)
                V.reduce_max(rowc[:], lg[:], axis=AX.X)
                cross_part_bcast(rowc[:], ALU.max)
                V.tensor_copy(gmx[:], cntb[:])
                V.tensor_scalar(hi[:], cntb[:], 1.0, None, op0=ALU.add)
                V.tensor_scalar(ngmx[:], cntb[:], -1.0, None, op0=ALU.mult)
                V.wait(V)
                V.tensor_reduce(rowc[:], lg[:], axis=AX.X, op=ALU.min)
                cross_part_bcast(rowc[:], ALU.min)
                V.tensor_scalar(lo[:], cntb[:], -1.0, None, op0=ALU.add)
                V.wait(V)
                for _it in range(64):
                    V.tensor_tensor(mid[:], lo[:], hi[:], op=ALU.add)
                    V.wait(V)
                    V.tensor_scalar(mid[:], mid[:], 0.5, None, op0=ALU.mult)
                    V.wait(V)
                    V.tensor_scalar(cmp[:], lg[:], mid[:], None, op0=ALU.is_ge)
                    V.wait(V)
                    V.reduce_sum(rowc[:], cmp[:], axis=AX.X)
                    cross_part_bcast(rowc[:], ALU.add)
                    V.tensor_scalar(selv[:], cntb[:], float(SEL_K), None, op0=ALU.is_ge)
                    V.wait(V)
                    V.tensor_tensor(d1[:], mid[:], lo[:], op=ALU.subtract)
                    V.wait(V)
                    V.scalar_tensor_tensor(lo[:], d1[:], selv[:], lo[:],
                                           op0=ALU.mult, op1=ALU.add)
                    V.tensor_tensor(d1[:], hi[:], mid[:], op=ALU.subtract)
                    V.wait(V)
                    V.scalar_tensor_tensor(d1[:], d1[:], selv[:], mid[:],
                                           op0=ALU.mult, op1=ALU.add)
                    V.wait(V)
                    V.tensor_copy(hi[:], d1[:])
                    V.wait(V)
                V.tensor_scalar(mask[:], lg[:], lo[:], None, op0=ALU.is_ge)
                V.wait(V)
                V.tensor_tensor_scan(incl[:], mask[:], mask[:], 0.0,
                                     op0=ALU.add, op1=ALU.bypass)
                V.wait(V)
                V.tensor_tensor(slots[:], incl[:], mask[:], op=ALU.subtract)
                V.tensor_copy(rowcb[:], incl[:, NF - 1:NF])
                V.wait(V)
                P.wait(V)
                P.matmul(pq[:, 0:1], su_sb[:], rowcb[:], start=True, stop=True)
                V.wait(P)
                V.tensor_copy(base_sb[:], pq[:, 0:1])
                V.wait(V)
                V.tensor_scalar(slots[:], slots[:], base_sb[:], None, op0=ALU.add)
                V.wait(V)
                V.tensor_tensor(sltmp[:], slots[:], mask[:], op=ALU.mult)
                V.tensor_scalar(cmp[:], mask[:], -60000.0, 60000.0,
                                op0=ALU.mult, op1=ALU.add)
                V.wait(V)
                V.tensor_tensor(sltmp[:], sltmp[:], cmp[:], op=ALU.add)
                V.wait(V)
                V.tensor_copy(slot_i[:], sltmp[:])
                S.wait(V)
                S.activation(rwe[:], lg[:], AF.Exp, bias=ngmx[:])
                V.wait(S); V.wait(V)
                V.tensor_copy(pair_sb[:].rearrange("p (f c) -> p f c", c=2)[:, :, 0],
                              iota_sb[:])
                V.tensor_copy(pair_sb[:].rearrange("p (f c) -> p f c", c=2)[:, :, 1],
                              rwe[:])
                V.wait(V)
                for f in range(NF):
                    V.wait(P)
                    V.tensor_tensor(Mf[:], sltmp[:, f:f + 1].broadcast_to([128, 128]),
                                    iotaj_sb[:], op=ALU.is_equal)
                    P.wait(V)
                    P.matmul(pq[:, 2:4], Mf[:], pair_sb[:, 2 * f:2 * f + 2],
                             start=(f == 0), stop=(f == NF - 1))
                V.wait(P)
                V.tensor_copy(mypair[:], pq[:, 2:4])
                V.wait(V)
                V.tensor_copy(myidx[:], mypair[:, 0:1])
                V.tensor_tensor(cmp[:], rwe[:], mask[:], op=ALU.mult)
                V.wait(V)
                V.reduce_sum(rowc[:], cmp[:], axis=AX.X)
                cross_part_bcast(rowc[:], ALU.add)
                V.reciprocal(den[:], cntb[:])
                V.wait(V)
                V.tensor_scalar(myrw[:], mypair[:, 1:2], den[:], None, op0=ALU.mult)
                nc.gpsimd.wait_ge(V.sem, V.n)
                G.indirect_dma_start(
                    fx_sb[:], None,
                    xn1, IndirectOffsetOnAxis(myidx[:], 0),
                    bounds_check=T - 1, oob_is_err=False, ch=3)
                Y.wait_dch(G, 3)
                nc.sync.wait_ge(V.sem, V.n)
                Y.dma_start(fx, fx_sb[:], ch=0)
                Y.dma_start(selidx[:, None], myidx[:], ch=1)
                Y.dma_start(rw[:, None], myrw[:], ch=1)
                Y.wait(Y); G.wait(G)
        return ph


def build_mini1(name="mini1", ph=None):
    if ph is None:
        ph = Ph(name)
    nc = ph.nc
    MT = SEL_K // 128
    nch = D // 128
    fxd = ph.io("fx_full", [SEL_K, D], BF16, "ExternalInput")
    g1d = ph.io("g1", [D], F32, "ExternalInput")
    wqd = ph.io("wq", [D, 256], F32, "ExternalInput")
    wkd = ph.io("wk", [D, 256], F32, "ExternalInput")
    wvd = ph.io("wv", [D, 256], F32, "ExternalInput")
    wod = ph.io("wo_r", [256, D], F32, "ExternalInput")
    frd = ph.io("freqs", [SEL_K, 64, 2], F32, "ExternalInput")
    mkd = ph.io("mask", [SEL_K, SEL_K], F32, "ExternalInput")
    part1 = ph.io("part1", [SEL_K, D], F32, "ExternalOutput")
    ident = ph.const_in("ident", _const_identity_bf16())
    eps_ap = ph.fconst(RMS1_EPS)

    with ph.scope():
        fx_sb = ph.sbuf("fx_sb", [128, MT * D], BF16)
        h1f = ph.sbuf("h1f", [128, nch * SEL_K], BF16)
        g1_sb = ph.sbuf("g1_sb", [128, nch], F32)
        wq_sb = ph.sbuf("wq_sb", [128, nch * 256], BF16)
        wk_sb = ph.sbuf("wk_sb", [128, nch * 256], BF16)
        wv_sb = ph.sbuf("wv_sb", [128, nch * 256], BF16)
        wo_sb = ph.sbuf("wo_sb", [128, 2 * D], BF16)
        fr_sb = ph.sbuf("fr_sb", [128, MT * 128], F32)
        q_sb = ph.sbuf("q_sb", [128, MT * 256], BF16)
        k_sb = ph.sbuf("k_sb", [128, MT * 256], BF16)
        v_sb = ph.sbuf("v_sb", [128, MT * 256], BF16)
        qf_sb = ph.sbuf("qf_sb", [128, SEL_K], BF16)
        qf2 = [ph.sbuf("qf2_%d" % h, [128, SEL_K], BF16) for h in range(2)]
        kf2 = [ph.sbuf("kf2_%d" % h, [128, SEL_K], BF16) for h in range(2)]
        ptr_sb = ph.sbuf("ptr_sb", [128, 256], BF16)
        of_sb = ph.sbuf("of_sb", [128, 2 * SEL_K], BF16)
        mk_sb = ph.sbuf("mk_sb", [128, SEL_K], F32)
        p1_sb = ph.sbuf("p1_sb", [128, SEL_K], BF16)
        otm_sb = ph.sbuf("otm_sb", [128, 128], BF16)
        pt_sb = ph.sbuf("pt_sb", [128, D], F32)
        scr = ph.sbuf("scr", [128, D], BF16)
        rstd = ph.sbuf("rstd", [128, 1], F32)
        ssq = ph.sbuf("ssq", [128, 1], F32)
        mrow = ph.sbuf("mrow", [128, 1], F32)
        nbias = ph.sbuf("nbias", [128, 1], F32)
        lsum = ph.sbuf("lsum", [128, 1], F32)
        l2 = ph.sbuf("l2", [128, 1], F32)
        linv = ph.sbuf("linv", [128, 1], F32)
        id_sb = ph.sbuf("id_sb", [128, 128], BF16)
        rA = ph.sbuf("rA", [128, 64], F32)
        rB = ph.sbuf("rB", [128, 64], F32)
        pB = ph.psum("pB", [128, 512], F32)
        pC = ph.psum("pC", [128, 512], F32)
        po = ph.psum("po", [128, 128], F32)
        tpA = ph.psum("tpA", [128, 128], BF16)
        tpB = ph.psum("tpB", [128, 128], BF16)
        V, S, P, G, Y = ph.V, ph.S, ph.P, ph.G, ph.Y

        with nc.Block() as blk:
            @blk.vector
            def _(_v):
                G.dma_start(id_sb[:], ident)
                G.dma_start(fx_sb[:].rearrange("p (t d) -> p t d", d=D),
                            fxd.rearrange("(t p) d -> p t d", p=128))
                G.dma_start(g1_sb[:], g1d.rearrange("(c p) -> p c", p=128),
                            allow_slow_non_contiguous=True)
                emit_weight_cast_load(ph, wqd, wq_sb, D, 256)
                emit_weight_cast_load(ph, wkd, wk_sb, D, 256)
                emit_weight_cast_load(ph, wvd, wv_sb, D, 256)
                G.dma_start(wo_sb[:].rearrange("p (c n) -> p c n", n=D),
                            wod.rearrange("(c p) n -> p c n", p=128))
                G.dma_start(fr_sb[:].rearrange("p (t f) -> p t f", f=128),
                            frd.rearrange("(t p) f c -> p t (f c)", p=128))
                P.wait(G); S.wait(G); V.wait(G)
                for wsb in (wq_sb, wk_sb, wv_sb):
                    for c in range(nch):
                        V.tensor_scalar(wsb[:, c * 256:(c + 1) * 256],
                                        wsb[:, c * 256:(c + 1) * 256],
                                        g1_sb[:, c:c + 1], None, op0=ALU.mult)
                V.wait(V)
                em = {}
                for i in range(MT):
                    fxt = fx_sb[:, i * D:(i + 1) * D]
                    V.wait(V); S.wait(S); S.wait(V); S.wait(P); V.wait(P)
                    emit_ln_tile(ph, fxt, scr[:], None, rstd, scr, ssq, None,
                                 eps_ap, D, rms=True)
                    P.wait(V)
                    out_cols = [h1f[:, c * SEL_K + i * 128: c * SEL_K + (i + 1) * 128]
                                for c in range(nch)]
                    emit_transpose_set(ph, scr, out_cols, [tpA, tpB], id_sb, nch, em)
                for i in range(MT):
                    for wsb, dst in ((wq_sb, q_sb), (wk_sb, k_sb), (wv_sb, v_sb)):
                        P.wait(V)
                        for c in range(nch):
                            P.matmul(pB[:, 0:256],
                                     h1f[:, c * SEL_K + i * 128: c * SEL_K + (i + 1) * 128],
                                     wsb[:, c * 256:(c + 1) * 256],
                                     start=(c == 0), stop=(c == nch - 1))
                        V.wait(P)
                        V.tensor_copy(dst[:, i * 256:(i + 1) * 256], pB[:, 0:256])
                    fr = fr_sb[:, i * 128:(i + 1) * 128].rearrange(
                        "p (f c) -> p f c", c=2)
                    cs, sn = fr[:, :, 0], fr[:, :, 1]
                    for dst in (q_sb, k_sb):
                        for h in range(2):
                            b0 = i * 256 + h * 128
                            x1 = dst[:, b0:b0 + 64]
                            x2 = dst[:, b0 + 64:b0 + 128]
                            V.wait(V); V.wait(P)
                            V.tensor_tensor(rA[:], x1, cs, op=ALU.mult)
                            V.tensor_tensor(rB[:], x2, sn, op=ALU.mult)
                            V.wait(V)
                            V.tensor_tensor(rA[:], rA[:], rB[:], op=ALU.subtract)
                            V.wait(V)
                            V.tensor_tensor(rB[:], x1, sn, op=ALU.mult)
                            V.wait(V)
                            V.tensor_tensor(x2, x2, cs, op=ALU.mult)
                            V.wait(V)
                            V.tensor_tensor(x2, x2, rB[:], op=ALU.add)
                            V.tensor_copy(x1, rA[:])
                            V.wait(V)
                SC = float(L1_SCALE)
                for h in range(2):
                    em2 = {}
                    for i in range(MT):
                        P.wait(V)
                        emit_transpose_set(
                            ph, None,
                            [qf_sb[:, (h * MT + i) * 128:(h * MT + i + 1) * 128]
                             if False else qf2[h][:, i * 128:(i + 1) * 128]],
                            [tpA, tpB], id_sb, 1, em2,
                            in_slices=[q_sb[:, i * 256 + h * 128: i * 256 + (h + 1) * 128]])
                        emit_transpose_set(
                            ph, None,
                            [kf2[h][:, i * 128:(i + 1) * 128]], [tpA, tpB],
                            id_sb, 1, em2,
                            in_slices=[k_sb[:, i * 256 + h * 128: i * 256 + (h + 1) * 128]])
                for qt in range(MT):
                    G.raw.wait_ge(V.sem, V.n)
                    G.raw.wait_ge(S.sem, S.n)
                    G.dma_start(mk_sb[:], mkd[qt * 128:(qt + 1) * 128, :], ch=1)
                    V.wait_dch(G, 1)
                    for h in range(2):
                        P.wait(S); P.wait(V)
                        for half, ps in ((0, pB), (1, pC)):
                            P.matmul(ps[:], qf2[h][:, qt * 128:(qt + 1) * 128],
                                     kf2[h][:, half * 512:(half + 1) * 512],
                                     start=True, stop=True)
                        V.wait(P)
                        for half, ps in ((0, pB), (1, pC)):
                            V.scalar_tensor_tensor(
                                ps[:], ps[:], SC, mk_sb[:, half * 512:(half + 1) * 512],
                                op0=ALU.mult, op1=ALU.add)
                        V.wait(V)
                        V.reduce_max(mrow[:], pB[:], axis=AX.X)
                        V.reduce_max(l2[:], pC[:], axis=AX.X)
                        V.wait(V)
                        V.tensor_tensor(mrow[:], mrow[:], l2[:], op=ALU.max)
                        V.wait(V)
                        V.tensor_scalar(nbias[:], mrow[:], -1.0, None, op0=ALU.mult)
                        S.wait(V)
                        S.activation(p1_sb[:, 0:512], pB[:], AF.Exp,
                                     bias=nbias[:], accum_out=lsum[:])
                        S.wait(S)
                        S.activation(p1_sb[:, 512:1024], pC[:], AF.Exp,
                                     bias=nbias[:], accum_out=l2[:])
                        V.wait(S)
                        V.tensor_tensor(lsum[:], lsum[:], l2[:], op=ALU.add)
                        V.wait(V)
                        V.reciprocal(linv[:], lsum[:])
                        P.wait(S); P.wait(V)
                        emA = {}
                        for tk in range(MT):
                            buf = tk % 2
                            tp = (tpA, tpB)[buf]
                            if emA.get(buf):
                                P.wait_ev(V, emA[buf])
                            P.transpose(tp[:], p1_sb[:, tk * 128:(tk + 1) * 128],
                                        id_sb[:])
                            V.wait(P)
                            V.tensor_copy(ptr_sb[:, buf * 128:(buf + 1) * 128], tp[:])
                            emA[buf] = V.n
                            P.wait_ev(V, V.n)
                            P.matmul(po[:], ptr_sb[:, buf * 128:(buf + 1) * 128],
                                     v_sb[:, tk * 256 + h * 128: tk * 256 + (h + 1) * 128],
                                     start=(tk == 0), stop=(tk == MT - 1))
                        V.wait(P)
                        V.tensor_tensor(otm_sb[:], po[:],
                                        linv[:].broadcast_to([128, 128]), op=ALU.mult)
                        P.wait(V)
                        P.transpose(tpA[:], otm_sb[:], id_sb[:])
                        V.wait(P)
                        V.tensor_copy(of_sb[:, h * SEL_K + qt * 128:
                                           h * SEL_K + (qt + 1) * 128], tpA[:])
                        emA[0] = V.n
                for i in range(MT):
                    for nt in range(4):
                        ps = (pB, pC)[nt % 2]
                        P.wait(V)
                        for c in range(2):
                            P.matmul(ps[:],
                                     of_sb[:, c * SEL_K + i * 128: c * SEL_K + (i + 1) * 128],
                                     wo_sb[:, c * D + nt * 512: c * D + (nt + 1) * 512],
                                     start=(c == 0), stop=(c == 1))
                        V.wait(P)
                        V.tensor_copy(pt_sb[:, nt * 512:(nt + 1) * 512], ps[:])
                    V.wait(V)
                    nc.sync.wait_ge(V.sem, V.n)
                    Y.dma_start(part1[i * 128:(i + 1) * 128, :], pt_sb[:], ch=i % 2)
                Y.wait(Y); G.wait(G)
        return ph


def build_mini2(name="mini2", ph=None, pre_summed=False):
    if ph is None:
        ph = Ph(name)
    nc = ph.nc
    parts = ph.io("parts", [1, 128, D] if pre_summed else [NCORES, 128, D],
                  F32, "ExternalInput")
    fxr = ph.io("fx_rows", [128, D], BF16, "ExternalInput")
    xb = ph.io("xb", [128, D], F32, "ExternalOutput")
    h2 = ph.io("h2", [128, D], BF16, "ExternalOutput")
    eps_ap = ph.fconst(RMS1_EPS)
    with ph.scope():
        acc = ph.sbuf("acc", [128, D], F32)
        pin = ph.sbuf("pin", [128, 2 * D], F32)
        fx_sb = ph.sbuf("fx_sb", [128, D], BF16)
        h2_sb = ph.sbuf("h2_sb", [128, D], BF16)
        scr = ph.sbuf("scr", [128, D], BF16)
        rstd = ph.sbuf("rstd", [128, 1], F32)
        ssq = ph.sbuf("ssq", [128, 1], F32)
        V, S, P, G, Y = ph.V, ph.S, ph.P, ph.G, ph.Y
        with nc.Block() as blk:
            @blk.vector
            def _(_v):
                Y.dma_start(fx_sb[:], fxr, ch=2)
                Y.dma_start(pin[:, 0:D], parts[0], ch=0)
                V.wait_dch(Y, 2); V.wait_dch(Y, 0)
                V.tensor_tensor(acc[:], fx_sb[:], pin[:, 0:D], op=ALU.add)
                for j in range(1, 1 if pre_summed else NCORES):
                    b = j % 2
                    if j >= 2:
                        nc.sync.wait_ge(V.sem, V.n)
                    Y.dma_start(pin[:, b * D:(b + 1) * D], parts[j], ch=b)
                    V.wait_dch(Y, b)
                    V.wait(V)
                    V.tensor_tensor(acc[:], acc[:], pin[:, b * D:(b + 1) * D],
                                    op=ALU.add)
                V.wait(V)
                emit_ln_tile(ph, acc[:], h2_sb[:], None, rstd, scr, ssq, None,
                             eps_ap, D, rms=True)
                nc.sync.wait_ge(V.sem, V.n)
                Y.dma_start(xb, acc[:], ch=2)
                Y.dma_start(h2, h2_sb[:], ch=2)
                Y.wait(Y); G.wait(G)
        return ph


def build_mini3(name="mini3", ph=None):
    if ph is None:
        ph = Ph(name)
    nc = ph.nc
    MT = SEL_K // 128
    nch = D // 128
    FS = FFN // NCORES
    h2d = ph.io("h2_full", [SEL_K, D], BF16, "ExternalInput")
    g2d = ph.io("g2", [D], F32, "ExternalInput")
    w1d = ph.io("w1c", [D, FS], F32, "ExternalInput")
    w3d = ph.io("w3c", [D, FS], F32, "ExternalInput")
    w2d = ph.io("w2r", [FS, D], F32, "ExternalInput")
    part2 = ph.io("part2", [SEL_K, D], F32, "ExternalOutput")
    ident = ph.const_in("ident", _const_identity_bf16())

    with ph.scope():
        h2_sb = ph.sbuf("h2_sb", [128, MT * D], BF16)
        h2f = ph.sbuf("h2f", [128, nch * SEL_K], BF16)
        g2_sb = ph.sbuf("g2_sb", [128, nch], F32)
        w1_sb = ph.sbuf("w1_sb", [128, nch * FS], BF16)
        w3_sb = ph.sbuf("w3_sb", [128, nch * FS], BF16)
        w2_sb = ph.sbuf("w2_sb", [128, (FS // 128) * D], BF16)
        u_sb = ph.sbuf("u_sb", [128, MT * FS], BF16)
        uf_sb = ph.sbuf("uf_sb", [128, (FS // 128) * SEL_K], BF16)
        u1_sb = ph.sbuf("u1_sb", [128, 512], BF16)
        pt_sb = ph.sbuf("pt_sb", [128, D], F32)
        id_sb = ph.sbuf("id_sb", [128, 128], BF16)
        pB = ph.psum("pB", [128, 512], F32)
        pC = ph.psum("pC", [128, 512], F32)
        tpA = ph.psum("tpA", [128, 128], BF16)
        tpB = ph.psum("tpB", [128, 128], BF16)
        V, S, P, G, Y = ph.V, ph.S, ph.P, ph.G, ph.Y
        with nc.Block() as blk:
            @blk.vector
            def _(_v):
                G.dma_start(id_sb[:], ident)
                G.dma_start(h2_sb[:].rearrange("p (t d) -> p t d", d=D),
                            h2d.rearrange("(t p) d -> p t d", p=128))
                G.dma_start(g2_sb[:], g2d.rearrange("(c p) -> p c", p=128),
                            allow_slow_non_contiguous=True)
                emit_weight_cast_load(ph, w1d, w1_sb, D, FS)
                emit_weight_cast_load(ph, w3d, w3_sb, D, FS)
                emit_weight_cast_load(ph, w2d, w2_sb, FS, D)
                P.wait(G); S.wait(G); V.wait(G)
                for wsb in (w1_sb, w3_sb):
                    for c in range(nch):
                        V.tensor_scalar(wsb[:, c * FS:(c + 1) * FS],
                                        wsb[:, c * FS:(c + 1) * FS],
                                        g2_sb[:, c:c + 1], None, op0=ALU.mult)
                V.wait(V)
                em = {}
                for i in range(MT):
                    P.wait(V)
                    out_cols = [h2f[:, c * SEL_K + i * 128: c * SEL_K + (i + 1) * 128]
                                for c in range(nch)]
                    emit_transpose_set(ph, h2_sb[:, i * D:(i + 1) * D], out_cols,
                                       [tpA, tpB], id_sb, nch, em)
                for i in range(MT):
                    for st in range(FS // 512):
                        P.wait(S); P.wait(V)
                        for c in range(nch):
                            lt = h2f[:, c * SEL_K + i * 128: c * SEL_K + (i + 1) * 128]
                            P.matmul(pB[:], lt, w1_sb[:, c * FS + st * 512:
                                                      c * FS + (st + 1) * 512],
                                     start=(c == 0), stop=(c == nch - 1))
                        for c in range(nch):
                            lt = h2f[:, c * SEL_K + i * 128: c * SEL_K + (i + 1) * 128]
                            P.matmul(pC[:], lt, w3_sb[:, c * FS + st * 512:
                                                      c * FS + (st + 1) * 512],
                                     start=(c == 0), stop=(c == nch - 1))
                        S.wait(P)
                        S.activation(u1_sb[:], pB[:], AF.Silu)
                        V.wait(S)
                        V.tensor_tensor(u_sb[:, i * FS + st * 512:
                                             i * FS + (st + 1) * 512],
                                        u1_sb[:], pC[:], op=ALU.mult)
                V.wait(V)
                em2 = {}
                for i in range(MT):
                    P.wait(V)
                    out_cols = [uf_sb[:, c * SEL_K + i * 128: c * SEL_K + (i + 1) * 128]
                                for c in range(FS // 128)]
                    emit_transpose_set(ph, u_sb[:, i * FS:(i + 1) * FS], out_cols,
                                       [tpA, tpB], id_sb, FS // 128, em2)
                for i in range(MT):
                    for nt in range(4):
                        ps = (pB, pC)[nt % 2]
                        P.wait(V)
                        for c in range(FS // 128):
                            P.matmul(ps[:],
                                     uf_sb[:, c * SEL_K + i * 128: c * SEL_K + (i + 1) * 128],
                                     w2_sb[:, c * D + nt * 512: c * D + (nt + 1) * 512],
                                     start=(c == 0), stop=(c == FS // 128 - 1))
                        V.wait(P)
                        V.tensor_copy(pt_sb[:, nt * 512:(nt + 1) * 512], ps[:])
                    V.wait(V)
                    nc.sync.wait_ge(V.sem, V.n)
                    Y.dma_start(part2[i * 128:(i + 1) * 128, :], pt_sb[:], ch=i % 2)
                Y.wait(Y); G.wait(G)
        return ph


def build_mini4(name="mini4", ph=None, pre_summed=False):
    if ph is None:
        ph = Ph(name)
    nc = ph.nc
    parts = ph.io("parts2", [1, 128, D] if pre_summed else [NCORES, 128, D],
                  F32, "ExternalInput")
    xbr = ph.io("xb_rows", [128, D], F32, "ExternalInput")
    rwr = ph.io("rw_rows", [128], F32, "ExternalInput")
    xw = ph.io("xw", [128, D], BF16, "ExternalOutput")
    with ph.scope():
        acc = ph.sbuf("acc", [128, D], F32)
        pin = ph.sbuf("pin", [128, 2 * D], F32)
        rw_sb = ph.sbuf("rw_sb", [128, 1], F32)
        xw_sb = ph.sbuf("xw_sb", [128, D], BF16)
        V, S, P, G, Y = ph.V, ph.S, ph.P, ph.G, ph.Y
        with nc.Block() as blk:
            @blk.vector
            def _(_v):
                Y.dma_start(acc[:], xbr, ch=2)
                Y.dma_start(rw_sb[:], rwr[:, None], ch=2)
                V.wait_dch(Y, 2)
                for j in range(1 if pre_summed else NCORES):
                    b = j % 2
                    if j >= 2:
                        nc.sync.wait_ge(V.sem, V.n)
                    Y.dma_start(pin[:, b * D:(b + 1) * D], parts[j], ch=b)
                    V.wait_dch(Y, b)
                    V.wait(V)
                    V.tensor_tensor(acc[:], acc[:], pin[:, b * D:(b + 1) * D],
                                    op=ALU.add)
                V.wait(V)
                V.tensor_scalar(xw_sb[:], acc[:], rw_sb[:], None, op0=ALU.mult)
                nc.sync.wait_ge(V.sem, V.n)
                Y.dma_start(xw, xw_sb[:], ch=2)
                Y.wait(Y); G.wait(G)
        return ph


def build_final(tiles=PT // 128, name="final", ph=None):
    if ph is None:
        ph = Ph(name)
    nc = ph.nc
    TL = tiles * 128
    nch = D // 128
    x1d = ph.io("x1", [TL, D], F32, "ExternalInput")
    xn1d = ph.io("xn1_tm", [TL, D], BF16, "ExternalInput")
    ofd = ph.io("o_a2a", [NCORES * TL, 256], BF16, "ExternalInput")
    wod = ph.io("wout", [D, D], F32, "ExternalInput")
    bod = ph.io("bout", [D], F32, "ExternalInput")
    xwd = ph.io("xw_full", [SEL_K, D], BF16, "ExternalInput")
    seld = ph.io("selidx", [SEL_K], I32, "ExternalInput")
    cbased = ph.io("cbase", [1], F32, "ExternalInput")
    outd = ph.io("out", [TL, D], F32, "ExternalOutput")
    xadd = nc.dram_tensor("xadd", [TL, D], BF16)
    onesc = ph.const_in("onesc", np.ones((1, 128), NPBF))
    ident_fin = ph.const_in("ident", _const_identity_bf16())

    with ph.scope():
        oa_sb = ph.sbuf("oa_sb", [128, D], BF16)
        of_sb = ph.sbuf("of_sb", [128, D], BF16)
        tpA = ph.psum("tpA", [128, 128], BF16)
        tpB = ph.psum("tpB", [128, 128], BF16)
        id_sb = ph.sbuf("id_sb", [128, 128], BF16)
        wo_sb = ph.sbuf("wo_sb", [128, nch * D], BF16)
        bo_sb = ph.sbuf("bo_sb", [1, D], BF16)
        on_sb = ph.sbuf("on_sb", [1, 128], BF16)
        sel_sb = ph.sbuf("sel_sb", [128, SEL_K // 128], F32)
        seli_sb = ph.sbuf("seli_sb", [128, SEL_K // 128], I32)
        cb_sb = ph.sbuf("cb_sb", [1, 1], F32)
        cbb = ph.sbuf("cbb", [128, 1], F32)
        negm = ph.sbuf("negm", [128, SEL_K // 128], F32)
        onesf = ph.const_in("onesf", np.ones((1, 128), np.float32))
        onf_sb = ph.sbuf("onf_sb", [1, 128], F32)
        xw_sb = ph.sbuf("xw_sb", [128, D], BF16)
        zz_sb = ph.sbuf("zz_sb", [128, D], BF16)
        x1_sb = ph.sbuf("x1_sb", [128, 2 * D], F32)
        xn_sb = ph.sbuf("xn_sb", [128, 2 * D], BF16)
        xa_sb = ph.sbuf("xa_sb", [128, 2 * D], BF16)
        gv_sb = ph.sbuf("gv_sb", [128, D], BF16)
        t1_sb = ph.sbuf("t1_sb", [128, 512], F32)
        out_sb = ph.sbuf("out_sb", [128, 2 * D], F32)
        pB = ph.psum("pB", [128, 512], F32)
        pC = ph.psum("pC", [128, 512], F32)
        pq = ph.psum("pq", [128, 1], F32)
        V, S, P, G, Y = ph.V, ph.S, ph.P, ph.G, ph.Y
        with nc.Block() as blk:
            @blk.vector
            def _(_v):
                G.dma_start(id_sb[:], ident_fin)
                emit_weight_cast_load(ph, wod, wo_sb, D, D)
                G.dma_start(bo_sb[:], bod[None, :])
                G.dma_start(on_sb[:], onesc)
                G.dma_start(onf_sb[:], onesf)
                G.dma_start(seli_sb[:], seld.rearrange("(p f) -> p f", f=SEL_K // 128))
                G.dma_start(cb_sb[:], cbased[None, :])
                V.memset(zz_sb[:], 0.0)
                P.wait(G); V.wait(G); S.wait(G)
                for i in range(tiles):
                    Y.dma_start(xadd.ap()[i * 128:(i + 1) * 128, :], zz_sb[:], ch=3)
                P.matmul(pq[:], onf_sb[:], cb_sb[:], start=True, stop=True)
                V.wait(P)
                V.tensor_copy(cbb[:], pq[:])
                V.wait(V)
                V.tensor_copy(sel_sb[:], seli_sb[:])
                V.wait(V)
                V.tensor_scalar(sel_sb[:], sel_sb[:], cbb[:], None, op0=ALU.add)
                V.wait(V)
                V.tensor_scalar(negm[:], sel_sb[:], 0.0, 70000.0,
                                op0=ALU.is_lt, op1=ALU.mult)
                V.wait(V)
                V.tensor_tensor(sel_sb[:], sel_sb[:], negm[:], op=ALU.add)
                V.wait(V)
                V.tensor_copy(seli_sb[:], sel_sb[:])
                V.wait(V)
                G.raw.wait_ge(V.sem, V.n)
                G.wait_dch(Y, 3)
                for r in range(SEL_K // 128):
                    G.dma_start(xw_sb[:],
                                xwd.rearrange("(p r) d -> p r d", r=SEL_K // 128)
                                [:, r, :], ch=1)
                    G.wait_dch(G, 1)
                    G.indirect_dma_start(
                        xadd.ap(), IndirectOffsetOnAxis(seli_sb[:, r:r + 1], 0),
                        xw_sb[:], None,
                        bounds_check=TL - 1, oob_is_err=False, ch=1)
                    G.wait_dch(G, 1)
                st_done = [0, 0]
                for i in range(tiles):
                    b = i % 2
                    if i >= 2:
                        Y.wait_dch(Y, b, st_done[b])
                    Y.dma_start(x1_sb[:, b * D:(b + 1) * D],
                                x1d[i * 128:(i + 1) * 128, :], ch=b)
                    Y.dma_start(xn_sb[:, b * D:(b + 1) * D],
                                xn1d[i * 128:(i + 1) * 128, :], ch=b)
                    Y.dma_start(xa_sb[:, b * D:(b + 1) * D],
                                xadd.ap()[i * 128:(i + 1) * 128, :], ch=b)
                    V.wait_dch(Y, b); V.wait(V); V.wait(S)
                    V.tensor_tensor(gv_sb[:], xn_sb[:, b * D:(b + 1) * D],
                                    xa_sb[:, b * D:(b + 1) * D], op=ALU.add)
                    S.wait(V)
                    S.activation(gv_sb[:], gv_sb[:], AF.Silu)
                    G.raw.wait_ge(P.sem, P.n)
                    for r in range(NCORES):
                        G.dma_start(oa_sb[:, r * 256:(r + 1) * 256],
                                    ofd[r * TL + i * 128: r * TL + (i + 1) * 128, :],
                                    ch=2)
                    P.wait(G); V.wait(G)
                    emt = {}
                    emit_transpose_set(ph, oa_sb[:],
                                       [of_sb[:, c * 128:(c + 1) * 128]
                                        for c in range(nch)],
                                       [tpA, tpB], id_sb, nch, emt)
                    for nt in range(4):
                        ps = (pB, pC)[nt % 2]
                        P.wait(V); P.wait(S)
                        for c in range(nch):
                            P.matmul(ps[:],
                                     of_sb[:, c * 128:(c + 1) * 128],
                                     wo_sb[:, c * D + nt * 512: c * D + (nt + 1) * 512],
                                     start=(c == 0), stop=False)
                        P.matmul(ps[:], on_sb[:], bo_sb[0:1, nt * 512:(nt + 1) * 512],
                                 start=False, stop=True)
                        V.wait(P)
                        V.tensor_tensor(t1_sb[:], ps[:],
                                        gv_sb[:, nt * 512:(nt + 1) * 512], op=ALU.mult)
                        V.wait(V)
                        V.tensor_tensor(out_sb[:, b * D + nt * 512: b * D + (nt + 1) * 512],
                                        t1_sb[:],
                                        x1_sb[:, b * D + nt * 512: b * D + (nt + 1) * 512],
                                        op=ALU.add)
                    V.wait(V)
                    nc.sync.wait_ge(V.sem, V.n)
                    Y.dma_start(outd[i * 128:(i + 1) * 128, :],
                                out_sb[:, b * D:(b + 1) * D], ch=b)
                    st_done[b] = Y.dmark(b)
                Y.wait(Y); G.wait(G)
        return ph



def _shard_rows(a, n=NCORES):
    a = np.ascontiguousarray(a)
    return np.split(a, n, axis=0)


import os as _os
_TRACE = _os.environ.get("K_TRACE", "0") == "1"
_TIMES = []


def _run(ph, in_maps, **kw):
    for m in in_maps:
        for k, v in ph.consts.items():
            m[k] = v
    if _TRACE:
        kw.setdefault("trace", True)
    res = run_bass_kernel_spmd(ph.nc, in_maps, core_ids=list(range(len(in_maps))), **kw)
    if _TRACE:
        _TIMES.append((ph.name, res.exec_time_ns))
    ph.close()
    return res.results


def emit_coll(ph, kind, op, in_ap, out_ap, ch=3):
    nc = ph.nc
    from concourse.aot_env import allow_ici_all_to_all
    with nc.Block() as blk:
        @blk.vector
        def _(_v):
            with allow_ici_all_to_all():
                ph.G.collective_compute(kind, op,
                                        replica_groups=[list(range(NCORES))],
                                        ins=[in_ap], outs=[out_ap], ch=ch)
            ph.G.wait_dch(ph.G, ch)


def build_fused():
    ph = Ph("fused")
    nc = ph.nc

    def dram(name, shape, dtype, kind="Internal", shared=False):
        return nc.dram_tensor(name, list(shape), dtype, kind=kind,
                              addr_space="Shared" if shared else "Local").ap()

    def ein(name, shape, dtype):
        return dram(name, shape, dtype, "ExternalInput")

    x = ein("x", [PT, D], F32)
    wq0 = ein("wq0", [D, 256], F32)
    wk0 = ein("wk0", [D, 256], F32)
    wi0 = ein("wi0", [D, 256], F32)
    wo0 = ein("wo0", [D, D], F32)
    gw0 = ein("gw0", [D], F32)
    mrw = ein("mrw", [D, 8], F32)
    mrb = ein("mrb", [8], F32)
    mew = ein("mew", [8, D, D], F32)
    meb = ein("meb", [8, D], F32)
    mow = ein("mow", [D, D], F32)
    mob = ein("mob", [D], F32)
    wdkv = ein("wdkv", [D, 512], F32)
    bdkv = ein("bdkv", [512], F32)
    wdq = ein("wdq", [D, 512], F32)
    bdq = ein("bdq", [512], F32)
    wrl1 = ein("wrl1", [D], F32)
    wuk = ein("wuk", [512, 192], F32)
    buk = ein("buk", [192], F32)
    wuv = ein("wuv", [512, 256], F32)
    buv = ein("buv", [256], F32)
    wuq = ein("wuq", [512, 192], F32)
    buq = ein("buq", [192], F32)
    wqr = ein("wqr", [512, 64], F32)
    bqr = ein("bqr", [64], F32)
    wkr = ein("wkr", [D, 64], F32)
    bkr = ein("bkr", [64], F32)
    iotaj = ein("iotaj", [128, 128], F32)
    g1m = ein("g1m", [D], F32)
    wqm = ein("wqm", [D, 256], F32)
    wkm = ein("wkm", [D, 256], F32)
    wvm = ein("wvm", [D, 256], F32)
    wom = ein("wom", [256, D], F32)
    freqs = ein("freqs", [SEL_K, 64, 2], F32)
    maskm = ein("maskm", [SEL_K, SEL_K], F32)
    g2m = ein("g2m", [D], F32)
    w1c = ein("w1c", [D, FFN // NCORES], F32)
    w3c = ein("w3c", [D, FFN // NCORES], F32)
    w2r = ein("w2r", [FFN // NCORES, D], F32)
    wout = ein("wout", [D, D], F32)
    bout = ein("bout", [D], F32)
    cbase = ein("cbase", [1], F32)
    out = dram("out", [PT, D], F32, "ExternalOutput")

    xn_tm_l = dram("xn_tm_l", [PT, D], BF16)
    xn_fm_l = dram("xn_fm_l", [D, PT], BF16)
    ag_xnfm = dram("ag_xnfm", [NCORES * D, PT], BF16)
    o_tm_l = dram("o_tm_l", [T, 256], BF16)
    a2a_o = dram("a2a_o", [T, 256], BF16)
    s_tm_l = dram("s_tm_l", [PT, D], BF16)
    g_tm_l = dram("g_tm_l", [PT, D], BF16)
    x1_l = dram("x1_l", [PT, D], F32)
    xn1_tm_l = dram("xn1_tm_l", [PT, D], BF16)
    xn1_fm_l = dram("xn1_fm_l", [D, PT], BF16)
    ag_xn1fm = dram("ag_xn1fm", [NCORES * D, PT], BF16)
    ag_xn1tm = dram("ag_xn1tm", [T, D], BF16)
    ckv_l = dram("ckv_l", [512, PT], BF16)
    cq_l = dram("cq_l", [512, PT], BF16)
    ag_ckv = dram("ag_ckv", [NCORES * 512, PT], BF16)
    ag_cq = dram("ag_cq", [NCORES * 512, PT], BF16)
    lg_l = dram("lg_l", [PT], F32)
    ag_lg = dram("ag_lg", [T], F32)
    o1_tm_l = dram("o1_tm_l", [T, 256], BF16)
    a2a_o1 = dram("a2a_o1", [T, 256], BF16)
    fx_l = dram("fx_l", [128, D], BF16)
    ag_fx = dram("ag_fx", [SEL_K, D], BF16)
    sel_l = dram("sel_l", [128], I32)
    ag_sel = dram("ag_sel", [SEL_K], I32)
    rw_l = dram("rw_l", [128], F32)
    part1_l = dram("part1_l", [SEL_K, D], F32)
    p1s_l = dram("p1s_l", [1, 128, D], F32)
    xb_l = dram("xb_l", [128, D], F32)
    h2_l = dram("h2_l", [128, D], BF16)
    ag_h2 = dram("ag_h2", [SEL_K, D], BF16)
    part2_l = dram("part2_l", [SEL_K, D], F32)
    p2s_l = dram("p2s_l", [1, 128, D], F32)
    xw_l = dram("xw_l", [128, D], BF16)
    ag_xw = dram("ag_xw", [SEL_K, D], BF16)

    ph.fconst(LN_EPS)
    ph.fconst(RMS0_EPS)
    ph.fconst(RMS1_EPS)

    def phase(builder, ext, **kw):
        ph.ext = ext
        builder(ph=ph, **kw)
        ph.ext = None

    AG, RS, A2A = "AllGather", "ReduceScatter", "AllToAll"
    phase(build_ln0, {"x": x, "xn_tm": xn_tm_l, "xn_fm": xn_fm_l})
    emit_coll(ph, AG, ALU.bypass, xn_fm_l, ag_xnfm)
    phase(build_gla, {"xn_fm": ag_xnfm, "wq": wq0, "wk": wk0, "wi": wi0,
                      "o_tm": o_tm_l}, src_blk=PT)
    emit_coll(ph, A2A, ALU.bypass, o_tm_l, a2a_o)
    phase(build_glaep, {"o": a2a_o, "wo": wo0, "gw": gw0, "s_tm": s_tm_l},
          nrk=NCORES)
    phase(build_moe, {"xn_fm": xn_fm_l, "rw": mrw, "rb": mrb, "ew": mew,
                      "eb": meb, "ow": mow, "ob": mob, "g_tm": g_tm_l})
    phase(build_combine_ln, {"x": x, "s_in": s_tm_l, "g_in": g_tm_l,
                             "x1": x1_l, "xn_tm": xn1_tm_l, "xn_fm": xn1_fm_l})
    emit_coll(ph, AG, ALU.bypass, xn1_fm_l, ag_xn1fm)
    emit_coll(ph, AG, ALU.bypass, xn1_tm_l, ag_xn1tm)
    phase(build_l1proj, {"xn_fm": xn1_fm_l, "wdkv": wdkv, "bdkv": bdkv,
                         "wdq": wdq, "bdq": bdq, "wr": wrl1,
                         "ckv": ckv_l, "cq": cq_l, "logits": lg_l})
    emit_coll(ph, AG, ALU.bypass, ckv_l, ag_ckv)
    emit_coll(ph, AG, ALU.bypass, cq_l, ag_cq)
    emit_coll(ph, AG, ALU.bypass, lg_l, ag_lg)
    phase(build_l1flash, {"ckv": ag_ckv, "cqf": ag_cq, "xn_fm": ag_xn1fm,
                          "wuk": wuk, "buk": buk, "wuv": wuv, "buv": buv,
                          "wuq": wuq, "buq": buq, "wqr": wqr, "bqr": bqr,
                          "wkr": wkr, "bkr": bkr, "o_tm": o1_tm_l},
          src_blk=PT)
    emit_coll(ph, A2A, ALU.bypass, o1_tm_l, a2a_o1)
    phase(build_l1sel, {"logits": ag_lg, "xn1_tm": ag_xn1tm, "iotaj": iotaj,
                        "fx": fx_l, "selidx": sel_l, "rw": rw_l})
    emit_coll(ph, AG, ALU.bypass, fx_l, ag_fx)
    emit_coll(ph, AG, ALU.bypass, sel_l, ag_sel)
    phase(build_mini1, {"fx_full": ag_fx, "g1": g1m, "wq": wqm, "wk": wkm,
                        "wv": wvm, "wo_r": wom, "freqs": freqs, "mask": maskm,
                        "part1": part1_l})
    emit_coll(ph, RS, ALU.add, part1_l, p1s_l.rearrange("o p d -> (o p) d"))
    phase(build_mini2, {"parts": p1s_l, "fx_rows": fx_l, "xb": xb_l,
                        "h2": h2_l}, pre_summed=True)
    emit_coll(ph, AG, ALU.bypass, h2_l, ag_h2)
    phase(build_mini3, {"h2_full": ag_h2, "g2": g2m, "w1c": w1c, "w3c": w3c,
                        "w2r": w2r, "part2": part2_l})
    emit_coll(ph, RS, ALU.add, part2_l, p2s_l.rearrange("o p d -> (o p) d"))
    phase(build_mini4, {"parts2": p2s_l, "xb_rows": xb_l, "rw_rows": rw_l,
                        "xw": xw_l}, pre_summed=True)
    emit_coll(ph, AG, ALU.bypass, xw_l, ag_xw)
    phase(build_final, {"x1": x1_l, "xn1_tm": xn1_tm_l, "o_a2a": a2a_o1,
                        "wout": wout, "bout": bout, "xw_full": ag_xw,
                        "selidx": ag_sel, "cbase": cbase, "out": out})
    return ph, out


FUSED = False


def kernel(**inputs):
    if FUSED:
        return _kernel_fused(**inputs)
    return _kernel_unfused(**inputs)


def _kernel_fused(**inputs):
    x = np.ascontiguousarray(np.asarray(inputs["L0_attn_hidden_states"])[0])
    ph, _ = build_fused()
    A = lambda k: np.asarray(inputs[k])
    C = np.ascontiguousarray
    FS = FFN // NCORES
    in_maps = []
    for c in range(NCORES):
        h256 = slice(c * 256, (c + 1) * 256)
        h192 = slice(c * 192, (c + 1) * 192)
        h64 = slice(c * 64, (c + 1) * 64)
        m = {
            "x": C(x[c * PT:(c + 1) * PT]),
            "wq0": C(A("L0_attn_Wq")[:, h256]),
            "wk0": C(A("L0_attn_Wk")[:, h256]),
            "wi0": C(A("L0_attn_Wi")[:, h256]),
            "wo0": A("L0_attn_Wo"), "gw0": A("L0_attn_g_weight"),
            "mrw": A("L0_ffn_router_w"), "mrb": A("L0_ffn_router_b"),
            "mew": A("L0_ffn_expert_w"), "meb": A("L0_ffn_expert_b"),
            "mow": A("L0_ffn_out_w"), "mob": A("L0_ffn_out_b"),
            "wdkv": A("L1_attn_W_dkv"), "bdkv": A("L1_attn_b_dkv"),
            "wdq": A("L1_attn_W_dq"), "bdq": A("L1_attn_b_dq"),
            "wrl1": A("L1_ffn_w_router"),
            "wuk": C(A("L1_attn_W_uk")[:, h192]), "buk": C(A("L1_attn_b_uk")[h192]),
            "wuv": C(A("L1_attn_W_uv")[:, h256]), "buv": C(A("L1_attn_b_uv")[h256]),
            "wuq": C(A("L1_attn_W_uq")[:, h192]), "buq": C(A("L1_attn_b_uq")[h192]),
            "wqr": C(A("L1_attn_W_qr")[:, h64]), "bqr": C(A("L1_attn_b_qr")[h64]),
            "wkr": C(A("L1_attn_W_kr")[:, h64]), "bkr": C(A("L1_attn_b_kr")[h64]),
            "iotaj": (np.arange(128)[None, :] +
                      128.0 * c * np.ones((128, 1))).astype(np.float32),
            "g1m": A("L1_ffn_g1"),
            "wqm": C(A("L1_ffn_wq")[:, h256]), "wkm": C(A("L1_ffn_wk")[:, h256]),
            "wvm": C(A("L1_ffn_wv")[:, h256]), "wom": C(A("L1_ffn_wo")[h256, :]),
            "freqs": A("L1_ffn_freqs_cis"), "maskm": A("L1_ffn_mask"),
            "g2m": A("L1_ffn_g2"),
            "w1c": C(A("L1_ffn_w1")[:, c * FS:(c + 1) * FS]),
            "w3c": C(A("L1_ffn_w3")[:, c * FS:(c + 1) * FS]),
            "w2r": C(A("L1_ffn_w2")[c * FS:(c + 1) * FS, :]),
            "wout": A("L1_attn_W_out"), "bout": A("L1_attn_b_out"),
            "cbase": np.array([-PT * c], np.float32),
        }
        in_maps.append(m)
    res = _run(ph, in_maps)
    out = np.concatenate([np.asarray(r["out"], np.float32) for r in res], 0)
    return out[None]


def _kernel_unfused(**inputs):
    x = np.ascontiguousarray(np.asarray(inputs["L0_attn_hidden_states"])[0])
    f32 = np.float32

    def rows(a):
        return [np.ascontiguousarray(s) for s in np.split(a, NCORES, 0)]

    def cols(a, n=NCORES):
        return [np.ascontiguousarray(s) for s in np.split(a, n, 1)]

    ph = build_ln0()
    res = _run(ph, [{"x": s} for s in rows(x)])
    xn_tm = [r["xn_tm"] for r in res]
    xn_fm_loc = [r["xn_fm"] for r in res]
    ag_xn_fm = np.concatenate(xn_fm_loc, 0)

    ph = build_gla(src_blk=PT)
    Wq, Wk, Wi = (np.asarray(inputs["L0_attn_" + n]) for n in ("Wq", "Wk", "Wi"))
    in_maps = []
    for c in range(NCORES):
        hs = slice(c * 256, (c + 1) * 256)
        in_maps.append({"xn_fm": ag_xn_fm,
                        "wq": np.ascontiguousarray(Wq[:, hs]),
                        "wk": np.ascontiguousarray(Wk[:, hs]),
                        "wi": np.ascontiguousarray(Wi[:, hs])})
    res = _run(ph, in_maps)
    o_percore = [r["o_tm"] for r in res]

    ph = build_glaep(nrk=NCORES)
    res = _run(ph, [{"o": np.concatenate(
                        [o_percore[r][c * PT:(c + 1) * PT, :] for r in range(NCORES)], 0),
                     "wo": np.asarray(inputs["L0_attn_Wo"]),
                     "gw": np.asarray(inputs["L0_attn_g_weight"])}
                    for c in range(NCORES)])
    s_tm = [r["s_tm"] for r in res]

    ph = build_moe()
    res = _run(ph, [{"xn_fm": xn_fm_loc[c],
                     "rw": np.asarray(inputs["L0_ffn_router_w"]),
                     "rb": np.asarray(inputs["L0_ffn_router_b"]),
                     "ew": np.asarray(inputs["L0_ffn_expert_w"]),
                     "eb": np.asarray(inputs["L0_ffn_expert_b"]),
                     "ow": np.asarray(inputs["L0_ffn_out_w"]),
                     "ob": np.asarray(inputs["L0_ffn_out_b"])}
                    for c in range(NCORES)])
    g_tm = [r["g_tm"] for r in res]

    ph = build_combine_ln()
    res = _run(ph, [{"x": xs, "s_in": ss, "g_in": gs}
                    for xs, ss, gs in zip(rows(x), s_tm, g_tm)])
    x1 = [r["x1"] for r in res]
    xn1_tm = [r["xn_tm"] for r in res]
    xn1_fm_loc = [r["xn_fm"] for r in res]
    ag_xn1_fm = np.concatenate(xn1_fm_loc, 0)
    xn1_tm_full = np.concatenate(xn1_tm, 0)

    ph = build_l1proj()
    res = _run(ph, [{"xn_fm": xn1_fm_loc[c],
                     "wdkv": np.asarray(inputs["L1_attn_W_dkv"]),
                     "bdkv": np.asarray(inputs["L1_attn_b_dkv"]),
                     "wdq": np.asarray(inputs["L1_attn_W_dq"]),
                     "bdq": np.asarray(inputs["L1_attn_b_dq"]),
                     "wr": np.asarray(inputs["L1_ffn_w_router"])}
                    for c in range(NCORES)])
    ag_ckv = np.concatenate([r["ckv"] for r in res], 0)
    ag_cq = np.concatenate([r["cq"] for r in res], 0)
    logits_full = np.concatenate([np.asarray(r["logits"]) for r in res])

    ph = build_l1flash(src_blk=PT)
    in_maps = []
    for c in range(NCORES):
        h2 = slice(c * 192, (c + 1) * 192)
        hv = slice(c * 256, (c + 1) * 256)
        hr = slice(c * 64, (c + 1) * 64)
        in_maps.append({
            "ckv": ag_ckv, "cqf": ag_cq, "xn_fm": ag_xn1_fm,
            "wuk": np.ascontiguousarray(np.asarray(inputs["L1_attn_W_uk"])[:, h2]),
            "buk": np.ascontiguousarray(np.asarray(inputs["L1_attn_b_uk"])[h2]),
            "wuv": np.ascontiguousarray(np.asarray(inputs["L1_attn_W_uv"])[:, hv]),
            "buv": np.ascontiguousarray(np.asarray(inputs["L1_attn_b_uv"])[hv]),
            "wuq": np.ascontiguousarray(np.asarray(inputs["L1_attn_W_uq"])[:, h2]),
            "buq": np.ascontiguousarray(np.asarray(inputs["L1_attn_b_uq"])[h2]),
            "wqr": np.ascontiguousarray(np.asarray(inputs["L1_attn_W_qr"])[:, hr]),
            "bqr": np.ascontiguousarray(np.asarray(inputs["L1_attn_b_qr"])[hr]),
            "wkr": np.ascontiguousarray(np.asarray(inputs["L1_attn_W_kr"])[:, hr]),
            "bkr": np.ascontiguousarray(np.asarray(inputs["L1_attn_b_kr"])[hr]),
        })
    res = _run(ph, in_maps)
    o1_percore = [r["o_tm"] for r in res]

    ph = build_l1sel()
    res = _run(ph, [{"logits": logits_full, "xn1_tm": xn1_tm_full,
                     "iotaj": (np.arange(128)[None, :] +
                               128 * c * np.ones((128, 1))).astype(np.float32)
                     + 0.0}
                    for c in range(NCORES)])
    fx_full = np.concatenate([r["fx"] for r in res], 0)
    selidx = np.concatenate([r["selidx"] for r in res])
    rw_full = np.concatenate([r["rw"] for r in res])

    ph = build_mini1()
    in_maps = []
    for c in range(NCORES):
        hv = slice(c * 256, (c + 1) * 256)
        in_maps.append({
            "fx_full": fx_full, "g1": np.asarray(inputs["L1_ffn_g1"]),
            "wq": np.ascontiguousarray(np.asarray(inputs["L1_ffn_wq"])[:, hv]),
            "wk": np.ascontiguousarray(np.asarray(inputs["L1_ffn_wk"])[:, hv]),
            "wv": np.ascontiguousarray(np.asarray(inputs["L1_ffn_wv"])[:, hv]),
            "wo_r": np.ascontiguousarray(np.asarray(inputs["L1_ffn_wo"])[hv, :]),
            "freqs": np.asarray(inputs["L1_ffn_freqs_cis"]),
            "mask": np.asarray(inputs["L1_ffn_mask"]),
        })
    res = _run(ph, in_maps)
    part1 = np.stack([r["part1"] for r in res], 0)

    ph = build_mini2()
    res = _run(ph, [{"parts": np.ascontiguousarray(
                        part1[:, c * 128:(c + 1) * 128, :]),
                     "fx_rows": np.ascontiguousarray(
                        fx_full[c * 128:(c + 1) * 128])}
                    for c in range(NCORES)])
    xb_rows = [r["xb"] for r in res]
    h2_full = np.concatenate([r["h2"] for r in res], 0)

    ph = build_mini3()
    FS = FFN // NCORES
    res = _run(ph, [{"h2_full": h2_full, "g2": np.asarray(inputs["L1_ffn_g2"]),
                     "w1c": np.ascontiguousarray(
                        np.asarray(inputs["L1_ffn_w1"])[:, c * FS:(c + 1) * FS]),
                     "w3c": np.ascontiguousarray(
                        np.asarray(inputs["L1_ffn_w3"])[:, c * FS:(c + 1) * FS]),
                     "w2r": np.ascontiguousarray(
                        np.asarray(inputs["L1_ffn_w2"])[c * FS:(c + 1) * FS, :])}
                    for c in range(NCORES)])
    part2 = np.stack([r["part2"] for r in res], 0)

    ph = build_mini4()
    res = _run(ph, [{"parts2": np.ascontiguousarray(
                        part2[:, c * 128:(c + 1) * 128, :]),
                     "xb_rows": xb_rows[c],
                     "rw_rows": np.ascontiguousarray(
                        np.asarray(rw_full)[c * 128:(c + 1) * 128])}
                    for c in range(NCORES)])
    xw_full = np.concatenate([r["xw"] for r in res], 0)

    ph = build_final()
    res = _run(ph, [{"x1": x1[c], "xn1_tm": xn1_tm[c],
                     "o_a2a": np.concatenate(
                        [o1_percore[r][c * PT:(c + 1) * PT, :]
                         for r in range(NCORES)], 0),
                     "wout": np.asarray(inputs["L1_attn_W_out"]),
                     "bout": np.asarray(inputs["L1_attn_b_out"]),
                     "xw_full": xw_full, "selidx": selidx,
                     "cbase": np.array([-PT * c], np.float32)}
                    for c in range(NCORES)])
    out = np.concatenate([np.asarray(r["out"], f32) for r in res], 0)
    return out[None]


# version 04
# speedup vs baseline: 1.2622x; 1.1133x over previous
import numpy as np
import ml_dtypes
from contextlib import ExitStack

import concourse.bass as bass
import concourse.mybir as mybir
from concourse.bass import IndirectOffsetOnAxis
from concourse.bass_utils import run_bass_kernel_spmd

F32 = mybir.dt.float32
BF16 = mybir.dt.bfloat16
I32 = mybir.dt.int32
U8 = mybir.dt.uint8
AF = mybir.ActivationFunctionType
ALU = mybir.AluOpType
AX = mybir.AxisListType
NPBF = ml_dtypes.bfloat16

NCORES = 8
T = 8192
D = 2048
PT = T // NCORES
NH = 16
HPC = NH // NCORES
HD = 128
GLA_SCALE = float(HD) ** -0.5
L1_SCALE = 1.0 / float(np.sqrt(np.float32(128.0)))
CHUNK = 32
LN_EPS = 1e-5
RMS0_EPS = 1e-5
RMS1_EPS = 1e-6
SEL_K = 1024
MINI_PT = SEL_K // NCORES
FFN = 8192
C1_GELU = 0.7978845608028654
C2_GELU = 0.7978845608028654 * 0.044715

_DT = {F32: np.float32, BF16: NPBF, I32: np.int32}


class Eng:
    _NOINC = {"wait_ge", "wait_op", "nop", "sem_inc", "drain"}
    _DMA = {"dma_start", "indirect_dma_start", "dma_start_transpose"}

    def __init__(self, raw, sem, dma_sems=()):
        self.raw = raw
        self.sem = sem
        self.n = 0
        self.d = [[h, 0] for h in dma_sems]

    def __getattr__(self, name):
        f = getattr(self.raw, name)
        if not callable(f):
            return f
        if name in Eng._NOINC:
            return f
        if name in Eng._DMA or name == "collective_compute":
            def g(*a, ch=0, **kw):
                instr = f(*a, **kw)
                instr.then_inc(self.d[ch][0], 16)
                self.d[ch][1] += 16
                return instr
            return g

        def g(*a, **kw):
            instr = f(*a, **kw)
            instr.then_inc(self.sem, 1)
            self.n += 1
            return instr
        return g

    def wait(self, *producers):
        for o in producers:
            if o.n:
                self.raw.wait_ge(o.sem, o.n)
            for h, cnt in o.d:
                if cnt:
                    self.raw.wait_ge(h, cnt)

    def wait_ev(self, o, n):
        if n:
            self.raw.wait_ge(o.sem, n)

    def wait_dch(self, o, ch, n=None):
        h, cnt = o.d[ch]
        n = cnt if n is None else n
        if n:
            self.raw.wait_ge(h, n)

    def dmark(self, ch=0):
        return self.d[ch][1]


class Ph:
    def __init__(self, name):
        self.nc = bass.Bass("TRN2", target_bir_lowering=False)
        self.stack = ExitStack()
        self.scopes = [self.stack]
        self.name = name
        self.consts = {}
        self._fconsts = {}
        self._uid = 0
        self.ext = None
        es = self.stack.enter_context
        nc = self.nc
        self.V = Eng(nc.vector, es(nc.semaphore("sem_v")))
        self.S = Eng(nc.scalar, es(nc.semaphore("sem_s")))
        self.P = Eng(nc.tensor, es(nc.semaphore("sem_p")))
        self.G = Eng(nc.gpsimd, es(nc.semaphore("sem_g")),
                     [es(nc.semaphore("sem_gd%d" % i)) for i in range(4)])
        self.Y = Eng(nc.sync, es(nc.semaphore("sem_y")),
                     [es(nc.semaphore("sem_yd%d" % i)) for i in range(4)])
        for v in (LN_EPS, RMS1_EPS):
            self.fconst(v)

    def scope(self):
        from contextlib import contextmanager

        @contextmanager
        def _s():
            es = ExitStack()
            self.scopes.append(es)
            try:
                yield
            finally:
                es.close()
                self.scopes.pop()
        return _s()

    def io(self, name, shape, dtype, kind):
        if self.ext is not None and name in self.ext:
            return self.ext[name]
        return self.nc.dram_tensor(name, list(shape), dtype, kind=kind).ap()

    def sbuf(self, name, shape, dtype):
        self._uid += 1
        return self.scopes[-1].enter_context(
            self.nc.sbuf_tensor("%s_u%d" % (name, self._uid), list(shape), dtype))

    def psum(self, name, shape, dtype=F32):
        self._uid += 1
        return self.scopes[-1].enter_context(
            self.nc.psum_tensor("%s_u%d" % (name, self._uid), list(shape), dtype))

    def fconst(self, val, dtype=F32):
        key = (dtype, float(val))
        if key in self._fconsts:
            return self._fconsts[key]
        self._uid += 1
        t = self.stack.enter_context(
            self.nc.sbuf_tensor("fc_u%d" % self._uid, [128, 1], dtype))
        self.nc.vector.memset(t[:], float(val))
        self._fconsts[key] = t[:]
        try:
            self.nc.const_aps.register(dtype, float(val), t[:])
        except Exception:
            pass
        return t[:]

    def const_in(self, name, arr):
        if name in self.consts:
            return self._const_aps[name]
        arr = np.ascontiguousarray(arr)
        t = self.nc.dram_tensor(name, list(arr.shape),
                                mybir.dt.from_np(arr.dtype), kind="ExternalInput")
        self.consts[name] = arr
        if not hasattr(self, "_const_aps"):
            self._const_aps = {}
        self._const_aps[name] = t.ap()
        return t.ap()

    def close(self):
        self.stack.close()


def _const_identity_bf16():
    return np.eye(128, dtype=NPBF)


def _const_identity_f32():
    return np.eye(128, dtype=np.float32)


def _const_neg_u32_blockdiag():
    m = np.zeros((128, 128), np.float32)
    for c in range(4):
        for t in range(32):
            for t2 in range(t, 32):
                m[c * 32 + t, c * 32 + t2] = -1.0
    return m.astype(NPBF)


def _const_mask32():
    return np.triu(np.ones((32, 32), np.float32))


def _ap(x):
    return x[:] if hasattr(x, "shape") and not isinstance(x, bass.AP) else x


def emit_ln_tile(ph, x_ap, out_ap, mean, rstd, scr, ssq, s1, eps_ap, n_feat,
                 rms=False):
    V, S = ph.V, ph.S
    mean, rstd, scr, ssq, s1 = _ap(mean), _ap(rstd), _ap(scr), _ap(ssq), _ap(s1)
    S.wait(V)
    S.activation(scr, x_ap, AF.Square, accum_out=ssq)
    if not rms:
        V.reduce_sum(s1, x_ap, axis=AX.X)
        V.wait(V)
        V.tensor_scalar(mean, s1, 1.0 / n_feat, None, op0=ALU.mult)
        V.tensor_tensor(s1, s1, s1, op=ALU.mult)
        V.wait(V); V.wait(S)
        V.scalar_tensor_tensor(ssq, s1, -1.0 / n_feat, ssq, op0=ALU.mult, op1=ALU.add)
        S.wait(V)
    S.wait(S)
    S.activation(ssq, ssq, AF.Ln, bias=eps_ap, scale=1.0 / n_feat)
    S.wait(S)
    S.activation(rstd, ssq, AF.Exp, scale=-0.5)
    V.wait(S)
    if rms:
        V.tensor_scalar(out_ap, x_ap, rstd, None, op0=ALU.mult)
    else:
        V.tensor_scalar(out_ap, x_ap, mean, rstd, op0=ALU.subtract, op1=ALU.mult)


def emit_transpose_set(ph, src_tm, out_cols, tps, id_sb, nchunk, evac_marks,
                       in_slices=None):
    P, V = ph.P, ph.V
    for c in range(nchunk):
        buf = c % 2
        if evac_marks.get(buf):
            P.wait_ev(V, evac_marks[buf])
        src = in_slices[c] if in_slices is not None else \
            src_tm[:, c * 128:(c + 1) * 128]
        P.transpose(tps[buf][:], src, id_sb[:])
        V.wait(P)
        V.tensor_copy(out_cols[c], tps[buf][:])
        evac_marks[buf] = V.n


def build_ln0(tiles=8, n_feat=D, name="ln0", ph=None):
    if ph is None:
        ph = Ph(name)
    nc = ph.nc
    x = ph.io("x", [tiles * 128, n_feat], F32, "ExternalInput")
    xn_tm = ph.io("xn_tm", [tiles * 128, n_feat], BF16, "ExternalOutput")
    xn_fm = ph.io("xn_fm", [n_feat, tiles * 128], BF16, "ExternalOutput")
    ident = ph.const_in("ident", _const_identity_bf16())
    nchunk = n_feat // 128

    with ph.scope():
        x_sb = ph.sbuf("x_sb", [128, 2 * n_feat], F32)
        xn_sb = ph.sbuf("xn_sb", [128, 2 * n_feat], BF16)
        fm_sb = ph.sbuf("fm_sb", [128, nchunk * tiles * 128], BF16)
        id_sb = ph.sbuf("id_sb", [128, 128], BF16)
        scr = ph.sbuf("scr", [128, n_feat], BF16)
        mean = ph.sbuf("mean", [128, 1], F32)
        rstd = ph.sbuf("rstd", [128, 1], F32)
        ssq = ph.sbuf("ssq", [128, 1], F32)
        s1 = ph.sbuf("s1", [128, 1], F32)
        tp0 = ph.psum("tp0", [128, 128], BF16)
        tp1 = ph.psum("tp1", [128, 128], BF16)
        eps_ap = ph.fconst(LN_EPS)
        V, S, P, G, Y = ph.V, ph.S, ph.P, ph.G, ph.Y

        with nc.Block() as blk:
            @blk.vector
            def _(_v):
                G.dma_start(id_sb[:], ident)
                P.wait(G)
                evac_marks = {}
                st_done = [0, 0]
                pe_done = [0, 0]
                for i in range(tiles):
                    b = i % 2
                    xb = x_sb[:, b * n_feat:(b + 1) * n_feat]
                    xnb = xn_sb[:, b * n_feat:(b + 1) * n_feat]
                    if i >= 2:
                        Y.wait_dch(Y, 2 + b, st_done[b])
                    Y.dma_start(xb, x[i * 128:(i + 1) * 128, :], ch=b)
                    V.wait(Y); S.wait(Y)
                    V.wait(V)
                    if i >= 2:
                        V.wait_ev(P, pe_done[b])
                    emit_ln_tile(ph, xb, xnb, mean, rstd, scr, ssq, s1,
                                 eps_ap, n_feat)
                    nc.sync.wait_ge(V.sem, V.n)
                    Y.dma_start(xn_tm[i * 128:(i + 1) * 128, :], xnb, ch=2 + b)
                    st_done[b] = Y.dmark(2 + b)
                    P.wait(V)
                    out_cols = [fm_sb[:, c * tiles * 128 + i * 128:
                                      c * tiles * 128 + (i + 1) * 128]
                                for c in range(nchunk)]
                    emit_transpose_set(ph, xnb, out_cols, [tp0, tp1],
                                       id_sb, nchunk, evac_marks)
                    pe_done[b] = P.n
                nc.sync.wait_ge(V.sem, V.n)
                Y.dma_start(xn_fm.rearrange("(c p) t -> p c t", p=128),
                            fm_sb[:].rearrange("p (c t) -> p c t", t=tiles * 128),
                            ch=2)
                Y.wait(Y); G.wait(G)
        return ph


def emit_weight_cast_load(ph, w_dram, w_sb, nrows, ncols, eng=None):
    G = eng or ph.G
    nch = nrows // 128
    G.dma_start(
        w_sb[:].rearrange("p (c n) -> p c n", n=ncols) if nch > 1 else w_sb[:],
        w_dram.rearrange("(c p) n -> p c n", p=128) if nch > 1 else w_dram)


def emit_bias_row_load(ph, b_dram, b_sb, n):
    ph.G.dma_start(b_sb[0:1, :], b_dram[None, :])


def emit_mm_tm(ph, psum_ap, act_fm_cols, w_sb_cols, kchunks, first=True,
               bias_row=None, ones_col=None):
    P = ph.P
    n = len(kchunks)
    for j, k in enumerate(kchunks):
        P.matmul(psum_ap, act_fm_cols(k), w_sb_cols(k),
                 start=(first and j == 0), stop=False)
    if bias_row is not None:
        P.matmul(psum_ap, ones_col, bias_row, start=False, stop=True)
    else:
        pass


def build_glaep(tiles=PT // 128, name="glaep", ph=None, nrk=1):
    if ph is None:
        ph = Ph(name)
    nc = ph.nc
    o = ph.io("o", [nrk * tiles * 128, D // nrk], BF16, "ExternalInput")
    wo = ph.io("wo", [D, D], F32, "ExternalInput")
    gw = ph.io("gw", [D], F32, "ExternalInput")
    s_tm = ph.io("s_tm", [tiles * 128, D], BF16, "ExternalOutput")
    ident = ph.const_in("ident", _const_identity_bf16())
    eps_ap = ph.fconst(RMS0_EPS)
    V, S, P, G, Y = ph.V, ph.S, ph.P, ph.G, ph.Y
    nch = D // 128

    with ph.scope():
        wo_sb = ph.sbuf("wo_sb", [128, nch * D], BF16)
        gw_sb = ph.sbuf("gw_sb", [128, nch], F32)
        id_sb = ph.sbuf("id_sb", [128, 128], BF16)
        o_sb = ph.sbuf("o_sb", [128, 2 * D], BF16)
        of_sb = ph.sbuf("of_sb", [128, D], BF16)
        st_sb = ph.sbuf("st_sb", [128, 2 * D], BF16)
        scr = ph.sbuf("scr", [128, D], BF16)
        rstd = ph.sbuf("rstd", [128, 1], F32)
        ssq = ph.sbuf("ssq", [128, 1], F32)
        tp0 = ph.psum("tp0", [128, 128], BF16)
        tp1 = ph.psum("tp1", [128, 128], BF16)
        mm0 = ph.psum("mm0", [128, 512], F32)
        mm1 = ph.psum("mm1", [128, 512], F32)

        with nc.Block() as blk:
            @blk.vector
            def _(_v):
                G.dma_start(id_sb[:], ident)
                emit_weight_cast_load(ph, wo, wo_sb, D, D)
                G.dma_start(gw_sb[:], gw.rearrange("(c p) -> p c", p=128), allow_slow_non_contiguous=True)
                V.wait(G)
                for c in range(nch):
                    V.tensor_scalar(wo_sb[:, c * D:(c + 1) * D],
                                    wo_sb[:, c * D:(c + 1) * D],
                                    gw_sb[:, c:c + 1], None, op0=ALU.mult)
                P.wait(V); S.wait(V)
                evac_marks = {}
                st_done = [0, 0]
                mmev = {}
                for i in range(tiles):
                    b = i % 2
                    ob = o_sb[:, b * D:(b + 1) * D]
                    stb = st_sb[:, b * D:(b + 1) * D]
                    if i >= 2:
                        Y.wait_dch(Y, 2 + b, st_done[b])
                    if nrk == 1:
                        Y.dma_start(ob, o[i * 128:(i + 1) * 128, :], ch=b)
                    else:
                        for r in range(nrk):
                            Y.dma_start(ob[:, r * (D // nrk):(r + 1) * (D // nrk)],
                                        o[r * tiles * 128 + i * 128:
                                          r * tiles * 128 + (i + 1) * 128, :],
                                        ch=b)
                    V.wait(Y); S.wait(Y)
                    V.wait(V); S.wait(S)
                    emit_ln_tile(ph, ob, ob, None, rstd, scr, ssq, None,
                                 eps_ap, D, rms=True)
                    P.wait(V)
                    out_cols = [of_sb[:, c * 128:(c + 1) * 128] for c in range(nch)]
                    emit_transpose_set(ph, ob, out_cols, [tp0, tp1], id_sb, nch,
                                       evac_marks)
                    for nt in range(4):
                        ps = (mm0, mm1)[nt % 2]
                        if nt >= 2 or i > 0:
                            P.wait_ev(V, mmev.get(nt % 2, 0))
                        P.wait(V)
                        for kc in range(nch):
                            P.matmul(ps[:], of_sb[:, kc * 128:(kc + 1) * 128],
                                     wo_sb[:, kc * D + nt * 512: kc * D + (nt + 1) * 512],
                                     start=(kc == 0), stop=(kc == nch - 1))
                        V.wait(P)
                        V.tensor_copy(stb[:, nt * 512:(nt + 1) * 512], ps[:])
                        mmev[nt % 2] = V.n
                    V.wait(V)
                    nc.sync.wait_ge(V.sem, V.n)
                    Y.dma_start(s_tm[i * 128:(i + 1) * 128, :], stb, ch=2 + b)
                    st_done[b] = Y.dmark(2 + b)
                Y.wait(Y); G.wait(G)
        return ph


def build_combine_ln(tiles=PT // 128, eps=LN_EPS, name="combine", ph=None):
    if ph is None:
        ph = Ph(name)
    nc = ph.nc
    n_feat = D
    x = ph.io("x", [tiles * 128, n_feat], F32, "ExternalInput")
    s_in = ph.io("s_in", [tiles * 128, n_feat], BF16, "ExternalInput")
    g_in = ph.io("g_in", [tiles * 128, n_feat], BF16, "ExternalInput")
    x1 = ph.io("x1", [tiles * 128, n_feat], F32, "ExternalOutput")
    xn_tm = ph.io("xn_tm", [tiles * 128, n_feat], BF16, "ExternalOutput")
    xn_fm = ph.io("xn_fm", [n_feat, tiles * 128], BF16, "ExternalOutput")
    ident = ph.const_in("ident", _const_identity_bf16())
    nchunk = n_feat // 128

    with ph.scope():
        x_sb = ph.sbuf("x_sb", [128, 2 * n_feat], F32)
        sg_sb = ph.sbuf("sg_sb", [128, 2 * n_feat], BF16)
        gg_sb = ph.sbuf("gg_sb", [128, 2 * n_feat], BF16)
        xn_sb = ph.sbuf("xn_sb", [128, 2 * n_feat], BF16)
        fm_sb = ph.sbuf("fm_sb", [128, nchunk * tiles * 128], BF16)
        id_sb = ph.sbuf("id_sb", [128, 128], BF16)
        scr = ph.sbuf("scr", [128, n_feat], BF16)
        sgf = ph.sbuf("sgf", [128, n_feat], F32)
        mean = ph.sbuf("mean", [128, 1], F32)
        rstd = ph.sbuf("rstd", [128, 1], F32)
        ssq = ph.sbuf("ssq", [128, 1], F32)
        s1 = ph.sbuf("s1", [128, 1], F32)
        tp0 = ph.psum("tp0", [128, 128], BF16)
        tp1 = ph.psum("tp1", [128, 128], BF16)
        eps_ap = ph.fconst(eps)
        V, S, P, G, Y = ph.V, ph.S, ph.P, ph.G, ph.Y

        with nc.Block() as blk:
            @blk.vector
            def _(_v):
                G.dma_start(id_sb[:], ident)
                P.wait(G)
                evac_marks = {}
                st_done = [0, 0]
                pe_done = [0, 0]
                for i in range(tiles):
                    b = i % 2
                    xb = x_sb[:, b * n_feat:(b + 1) * n_feat]
                    sb = sg_sb[:, b * n_feat:(b + 1) * n_feat]
                    gb = gg_sb[:, b * n_feat:(b + 1) * n_feat]
                    xnb = xn_sb[:, b * n_feat:(b + 1) * n_feat]
                    if i >= 2:
                        Y.wait_dch(Y, 2 + b, st_done[b])
                    Y.dma_start(xb, x[i * 128:(i + 1) * 128, :], ch=b)
                    Y.dma_start(sb, s_in[i * 128:(i + 1) * 128, :], ch=b)
                    Y.dma_start(gb, g_in[i * 128:(i + 1) * 128, :], ch=b)
                    V.wait(Y)
                    V.wait(V)
                    if i >= 2:
                        V.wait_ev(P, pe_done[b])
                    V.tensor_tensor(sgf[:], sb, gb, op=ALU.mult)
                    V.wait(V)
                    V.tensor_tensor(xb, xb, sgf[:], op=ALU.add)
                    V.wait(V)
                    nc.sync.wait_ge(V.sem, V.n)
                    Y.dma_start(x1[i * 128:(i + 1) * 128, :], xb, ch=2 + b)
                    S.wait(V)
                    emit_ln_tile(ph, xb, xnb, mean, rstd, scr, ssq, s1,
                                 eps_ap, n_feat)
                    nc.sync.wait_ge(V.sem, V.n)
                    Y.dma_start(xn_tm[i * 128:(i + 1) * 128, :], xnb, ch=2 + b)
                    st_done[b] = Y.dmark(2 + b)
                    P.wait(V)
                    out_cols = [fm_sb[:, c * tiles * 128 + i * 128:
                                      c * tiles * 128 + (i + 1) * 128]
                                for c in range(nchunk)]
                    emit_transpose_set(ph, xnb, out_cols, [tp0, tp1],
                                       id_sb, nchunk, evac_marks)
                    pe_done[b] = P.n
                nc.sync.wait_ge(V.sem, V.n)
                Y.dma_start(xn_fm.rearrange("(c p) t -> p c t", p=128),
                            fm_sb[:].rearrange("p (c t) -> p c t", t=tiles * 128),
                            ch=2)
                Y.wait(Y); G.wait(G)
        return ph


def build_moe(tiles=PT // 128, name="moe", ph=None):
    if ph is None:
        ph = Ph(name)
    nc = ph.nc
    E = 8
    xn_fm = ph.io("xn_fm", [D, tiles * 128], BF16, "ExternalInput")
    rw = ph.io("rw", [D, E], F32, "ExternalInput")
    rb = ph.io("rb", [E], F32, "ExternalInput")
    ew = ph.io("ew", [E, D, D], F32, "ExternalInput")
    eb = ph.io("eb", [E, D], F32, "ExternalInput")
    ow = ph.io("ow", [D, D], F32, "ExternalInput")
    ob = ph.io("ob", [D], F32, "ExternalInput")
    g_tm = ph.io("g_tm", [tiles * 128, D], BF16, "ExternalOutput")
    ident = ph.const_in("ident", _const_identity_bf16())
    onesc = ph.const_in("onesc", np.ones((1, 128), NPBF))
    nch = D // 128
    TL = tiles * 128

    with ph.scope():
        xnf = ph.sbuf("xnf", [128, nch * TL], BF16)
        rw_sb = ph.sbuf("rw_sb", [128, nch * E], BF16)
        rb_sb = ph.sbuf("rb_sb", [1, E], BF16)
        eb_sb = ph.sbuf("eb_sb", [1, 2 * 1024], BF16)
        ob_sb = ph.sbuf("ob_sb", [1, 2 * 1024], BF16)
        wx = [ph.sbuf("wx0", [128, nch * 1024], BF16),
              ph.sbuf("wx1", [128, nch * 1024], BF16)]
        ones_sb = ph.sbuf("ones_sb", [1, 128], BF16)
        id_sb = ph.sbuf("id_sb", [128, 128], BF16)
        z_sb = ph.sbuf("z_sb", [128, E], F32)
        t8 = ph.sbuf("t8", [128, E], F32)
        nmax = ph.sbuf("nmax", [128, 1], F32)
        den = ph.sbuf("den", [128, 1], F32)
        selm = ph.sbuf("selm", [128, E], F32)
        hg = ph.sbuf("hg", [128, tiles * E], F32)
        comb = ph.sbuf("comb", [128, tiles * D], BF16)
        cfm = ph.sbuf("cfm", [128, D], BF16)
        gt = ph.sbuf("gt", [128, 2 * D], BF16)
        gel = [[ph.sbuf("gel%d_%d" % (b, j), [128, 512], F32) for j in range(3)]
               for b in range(4)]
        tp0 = ph.psum("tp0", [128, 128], BF16)
        tp1 = ph.psum("tp1", [128, 128], BF16)
        pr = ph.psum("pr", [128, 8], F32)
        pm = [ph.psum("pm%d" % b, [128, 512], F32) for b in range(4)]
        V, S, P, G, Y = ph.V, ph.S, ph.P, ph.G, ph.Y
        c1 = 2.0 * C1_GELU
        c2 = 2.0 * C2_GELU

        with nc.Block() as blk:
            @blk.vector
            def _(_v):
                G.dma_start(id_sb[:], ident)
                G.dma_start(ones_sb[:], onesc)
                G.dma_start(xnf[:].rearrange("p (c t) -> p c t", t=TL),
                            xn_fm.rearrange("(c p) t -> p c t", p=128))
                G.dma_start(rw_sb[:].rearrange("p (c n) -> p c n", n=E),
                            rw.rearrange("(c p) n -> p c n", p=128))
                G.dma_start(rb_sb[:], rb[None, :])
                V.memset(comb[:], 0.0)
                P.wait(G); V.wait(G); S.wait(G)
                for i in range(tiles):
                    P.wait(V)
                    V.wait(S)
                    for c in range(nch):
                        P.matmul(pr[:], xnf[:, c * TL + i * 128: c * TL + (i + 1) * 128],
                                 rw_sb[:, c * E:(c + 1) * E],
                                 start=(c == 0), stop=False)
                    P.matmul(pr[:], ones_sb[:], rb_sb[:], start=False, stop=True)
                    V.wait(P)
                    V.reduce_max(nmax[:], pr[:], axis=AX.X, negate=True)
                    V.wait(V); S.wait(V)
                    S.activation(z_sb[:], pr[:], AF.Exp, bias=nmax[:])
                    V.wait(S)
                    V.max(out=t8[:], in_=z_sb[:])
                    V.wait(V)
                    V.tensor_tensor(den[:], t8[:, 0:1], t8[:, 1:2], op=ALU.add)
                    V.wait(V)
                    V.reciprocal(den[:], den[:])
                    V.wait(V)
                    V.tensor_scalar(selm[:], z_sb[:], t8[:, 1:2], None, op0=ALU.is_ge)
                    V.wait(V)
                    V.tensor_tensor(selm[:], z_sb[:], selm[:], op=ALU.mult)
                    V.wait(V)
                    V.tensor_scalar(hg[:, i * E:(i + 1) * E], selm[:], den[:], None,
                                    op0=ALU.mult)
                wx_rd = [0, 0]
                ld_done = [0, 0]
                vmark = {}
                for ei in range(2 * E + 1):
                    if ei < 2 * E:
                        e, half = divmod(ei, 2)
                        buf = ei % 2
                        if ei >= 2:
                            G.raw.wait_ge(P.sem, wx_rd[buf])
                        G.dma_start(wx[buf][:].rearrange("p (c n) -> p c n", n=1024),
                                    ew[e].rearrange("(c p) n -> p c n", p=128)
                                    [:, :, half * 1024:(half + 1) * 1024], ch=1 + buf)
                        G.dma_start(eb_sb[0:1, buf * 1024:(buf + 1) * 1024],
                                    eb[e][None, half * 1024:(half + 1) * 1024], ch=1 + buf)
                        ld_done[buf] = G.dmark(1 + buf)
                    if ei >= 1:
                        e, half = divmod(ei - 1, 2)
                        buf = (ei - 1) % 2
                        P.wait_dch(G, 1 + buf, ld_done[buf])
                        for i in range(tiles):
                            for nt in range(2):
                                pb = pm[2 * (buf) + nt]
                                gl = gel[2 * buf + nt]
                                P.wait_ev(V, vmark.get(2 * buf + nt, 0))
                                for c in range(nch):
                                    P.matmul(pb[:],
                                             xnf[:, c * TL + i * 128: c * TL + (i + 1) * 128],
                                             wx[buf][:, c * 1024 + nt * 512:
                                                     c * 1024 + (nt + 1) * 512],
                                             start=(c == 0), stop=False)
                                P.matmul(pb[:], ones_sb[:],
                                         eb_sb[0:1, buf * 1024 + nt * 512:
                                               buf * 1024 + (nt + 1) * 512],
                                         start=False, stop=True)
                                S.wait(P); S.wait(V)
                                S.activation(gl[0][:], pb[:], AF.Square)
                                V.wait(S)
                                V.tensor_scalar(gl[1][:], gl[0][:], c2, c1,
                                                op0=ALU.mult, op1=ALU.add)
                                V.wait(V)
                                V.tensor_tensor(gl[1][:], gl[1][:], pb[:], op=ALU.mult)
                                S.wait(V)
                                S.activation(gl[2][:], gl[1][:], AF.Sigmoid)
                                V.wait(S)
                                V.scalar_tensor_tensor(gl[2][:], gl[2][:],
                                                       hg[:, i * E + e: i * E + e + 1],
                                                       pb[:], op0=ALU.mult, op1=ALU.mult)
                                vmark[2 * buf + nt] = V.n
                                V.wait(V)
                                cs = comb[:, i * D + half * 1024 + nt * 512:
                                          i * D + half * 1024 + (nt + 1) * 512]
                                V.tensor_tensor(cs, cs, gl[2][:], op=ALU.add)
                        wx_rd[buf] = P.n
                G.raw.wait_ge(P.sem, P.n)
                G.dma_start(wx[0][:].rearrange("p (c n) -> p c n", n=1024),
                            ow.rearrange("(c p) n -> p c n", p=128)[:, :, 0:1024], ch=1)
                G.dma_start(wx[1][:].rearrange("p (c n) -> p c n", n=1024),
                            ow.rearrange("(c p) n -> p c n", p=128)[:, :, 1024:2048], ch=2)
                G.dma_start(ob_sb[0:1, 0:1024], ob[None, 0:1024], ch=1)
                G.dma_start(ob_sb[0:1, 1024:2048], ob[None, 1024:2048], ch=2)
                P.wait(G); V.wait(G)
                evac_marks = {}
                st_done = [0, 0]
                mmev = {}
                for i in range(tiles):
                    b = i % 2
                    gb = gt[:, b * D:(b + 1) * D]
                    if i >= 2:
                        S.wait_dch(Y, b, st_done[b])
                    V.wait(V)
                    out_cols = [cfm[:, c * 128:(c + 1) * 128] for c in range(nch)]
                    if i > 0:
                        P.wait(V)
                    in_sl = [comb[:, i * D + c * 128: i * D + (c + 1) * 128]
                             for c in range(nch)]
                    emit_transpose_set(ph, None, out_cols, [tp0, tp1], id_sb, nch,
                                       evac_marks, in_slices=in_sl)
                    for nt in range(4):
                        pb = pm[nt % 2]
                        P.wait_ev(S, mmev.get(nt % 2, 0))
                        P.wait(V)
                        wxh = wx[nt // 2]
                        off = (nt % 2) * 512
                        for c in range(nch):
                            P.matmul(pb[:], cfm[:, c * 128:(c + 1) * 128],
                                     wxh[:, c * 1024 + off: c * 1024 + off + 512],
                                     start=(c == 0), stop=False)
                        P.matmul(pb[:], ones_sb[:],
                                 ob_sb[0:1, nt * 512:(nt + 1) * 512],
                                 start=False, stop=True)
                        S.wait(P)
                        S.activation(gb[:, nt * 512:(nt + 1) * 512], pb[:], AF.Silu)
                        mmev[nt % 2] = S.n
                    nc.sync.wait_ge(S.sem, S.n)
                    Y.dma_start(g_tm[i * 128:(i + 1) * 128, :], gb, ch=b)
                    st_done[b] = Y.dmark(b)
                Y.wait(Y); G.wait(G)
        return ph


def build_gla(TQ=2048, Ttot=T, name="gla", stage=3, ph=None, src_blk=None):
    if ph is None:
        ph = Ph(name)
    nc = ph.nc
    NQ = Ttot // TQ
    NG = TQ // 128
    if src_blk is None:
        src_blk = Ttot
    nsb = Ttot // src_blk
    xn_fm = ph.io("xn_fm", [nsb * D, src_blk], BF16, "ExternalInput")
    wq = ph.io("wq", [D, 256], F32, "ExternalInput")
    wk = ph.io("wk", [D, 256], F32, "ExternalInput")
    wi = ph.io("wi", [D, 256], F32, "ExternalInput")
    o_tm = ph.io("o_tm", [Ttot, 256], BF16, "ExternalOutput")
    ident = ph.const_in("ident", _const_identity_bf16())
    negu = ph.const_in("negu", _const_neg_u32_blockdiag())
    mask32 = ph.const_in("mask32", _const_mask32())
    nch = D // 128

    with ph.scope():
        xnf = ph.sbuf("xnf", [128, nch * TQ], BF16)
        wq_sb = ph.sbuf("wq_sb", [128, nch * 256], BF16)
        wk_sb = ph.sbuf("wk_sb", [128, nch * 256], BF16)
        wi_sb = ph.sbuf("wi_sb", [128, nch * 256], BF16)
        id_sb = ph.sbuf("id_sb", [128, 128], BF16)
        negu_sb = ph.sbuf("negu_sb", [128, 128], BF16)
        m32_sb = ph.sbuf("m32_sb", [32, 32], F32)
        qf = ph.sbuf("qf", [128, 2 * TQ], BF16)
        kf = ph.sbuf("kf", [128, 2 * TQ], BF16)
        spf = ph.sbuf("spf", [128, 2 * TQ], BF16)
        vt = ph.sbuf("vt", [128, NG * 256], BF16)
        sa1 = ph.sbuf("sa1", [128, 512], F32)
        sr1 = ph.sbuf("sr1", [128, 512], F32)
        sp_tm = ph.sbuf("sp_tm", [128, 256], BF16)
        eG = ph.sbuf("eG", [128, 2 * 128], F32)
        qt_sb = ph.sbuf("qt_sb", [128, 256], BF16)
        kc_sb = ph.sbuf("kc_sb", [128, 256], BF16)
        khf_sb = ph.sbuf("khf_sb", [128, 256], BF16)
        kht_sb = ph.sbuf("kht_sb", [128, 256], BF16)
        Pt_sb = ph.sbuf("Pt_sb", [128, 2 * 32], BF16)
        S32 = ph.sbuf("S32", [128, 2 * 128], F32)
        Sbf = ph.sbuf("Sbf", [128, 2 * 128], BF16)
        o_q = ph.sbuf("o_q", [128, NG * 256], BF16)
        pmA = ph.psum("pmA", [128, 512], F32)
        pmB = ph.psum("pmB", [128, 512], F32)
        tpA = ph.psum("tpA", [128, 128], BF16)
        tpB = ph.psum("tpB", [128, 128], BF16)
        pP = ph.psum("pP", [128, 32], F32)
        po2 = ph.psum("po2", [128, 128], F32)
        po = ph.psum("po", [128, 128], F32)
        pS = ph.psum("pS", [128, 128], F32)
        V, S, P, G, Y = ph.V, ph.S, ph.P, ph.G, ph.Y

        with nc.Block() as blk:
            @blk.vector
            def _(_v):
                G.dma_start(id_sb[:], ident)
                G.dma_start(negu_sb[:], negu)
                G.dma_start(m32_sb[:], mask32)
                emit_weight_cast_load(ph, wq, wq_sb, D, 256)
                emit_weight_cast_load(ph, wk, wk_sb, D, 256)
                emit_weight_cast_load(ph, wi, wi_sb, D, 256)
                V.memset(S32[:], 0.0)
                V.memset(Sbf[:], 0.0)
                P.wait(G); V.wait(G); S.wait(G)

                for qi in range(NQ):
                    V.wait(P)
                    G.raw.wait_ge(P.sem, P.n)
                    xnfv = xn_fm.rearrange("(r c p) t -> p r c t", p=128,
                                           c=D // 128)
                    for off in range(0, TQ, min(TQ, src_blk)):
                        gt = qi * TQ + off
                        r, tl = gt // src_blk, gt % src_blk
                        w = min(TQ, src_blk)
                        G.dma_start(xnf[:].rearrange("p (c t) -> p c t", t=TQ)
                                    [:, :, off:off + w],
                                    xnfv[:, r, :, tl:tl + w], ch=1)
                    P.wait(G); S.wait(G); V.wait(G)
                    S.wait(Y)
                    PW = min(512, TQ)
                    for h in range(2):
                        for tt in range(TQ // PW):
                            sl = slice(tt * PW, (tt + 1) * PW)
                            pb = (pmA, pmB)[tt % 2]
                            P.wait(S); P.wait(V)
                            for c in range(nch):
                                P.matmul(pb[:, 0:PW], wq_sb[:, c * 256 + h * 128: c * 256 + (h + 1) * 128],
                                         xnf[:, c * TQ + tt * PW: c * TQ + (tt + 1) * PW],
                                         start=(c == 0), stop=(c == nch - 1))
                            S.wait(P)
                            S.activation(qf[:, h * TQ:(h + 1) * TQ][:, sl], pb[:, 0:PW],
                                         AF.Relu, scale=GLA_SCALE)
                            P.wait(S)
                            for c in range(nch):
                                P.matmul(pb[:, 0:PW], wk_sb[:, c * 256 + h * 128: c * 256 + (h + 1) * 128],
                                         xnf[:, c * TQ + tt * PW: c * TQ + (tt + 1) * PW],
                                         start=(c == 0), stop=(c == nch - 1))
                            S.wait(P); V.wait(P)
                            S.activation(kf[:, h * TQ:(h + 1) * TQ][:, sl], pb[:, 0:PW], AF.Sigmoid)
                            S.wait(S)
                            S.activation(sa1[:, 0:PW], pb[:, 0:PW], AF.Abs)
                            S.activation(sr1[:, 0:PW], pb[:, 0:PW], AF.Relu)
                            S.wait(S)
                            S.activation(sa1[:, 0:PW], sa1[:, 0:PW], AF.Exp, scale=-1.0)
                            S.wait(S)
                            S.activation(sa1[:, 0:PW], sa1[:, 0:PW], AF.Ln, bias=1.0)
                            V.wait(S); V.wait(V)
                            V.tensor_tensor(spf[:, h * TQ:(h + 1) * TQ][:, sl],
                                            sr1[:, 0:PW], sa1[:, 0:PW], op=ALU.add)
                            V.wait(V)
                    vev = {}
                    for tt in range(NG):
                        pb, pbi = (pmA, 0) if (tt % 2 == 0) else (pmB, 1)
                        P.wait_ev(V, vev.get(pbi, 0))
                        P.wait(S); P.wait(V)
                        for c in range(nch):
                            P.matmul(pb[:, 0:256],
                                     xnf[:, c * TQ + tt * 128: c * TQ + (tt + 1) * 128],
                                     wi_sb[:, c * 256:(c + 1) * 256],
                                     start=(c == 0), stop=(c == nch - 1))
                        V.wait(P)
                        V.tensor_copy(vt[:, tt * 256:(tt + 1) * 256], pb[:, 0:256])
                        vev[pbi] = V.n
                    pPread = {0: 0, 1: 0}
                    zmark = {0: 0, 1: 0}
                    pSread = {0: 0, 1: 0}
                    osmark = {0: 0, 1: 0}
                    for gi in range(NG if stage >= 1 else 0):
                        gsl = slice(gi * 128, (gi + 1) * 128)
                        P.wait(V)
                        em = {}
                        emit_transpose_set(ph, None,
                                           [sp_tm[:, 0:128], sp_tm[:, 128:256]],
                                           [tpA, tpB], id_sb, 2, em,
                                           in_slices=[spf[:, h * TQ + gi * 128:
                                                          h * TQ + (gi + 1) * 128]
                                                      for h in range(2)])
                        for h in range(2 if stage >= 2 else 0):
                            pg = (pmA, pmB)[h]
                            P.wait(V); P.wait(S)
                            P.matmul(pg[:, 0:128], sp_tm[:, h * 128:(h + 1) * 128],
                                     negu_sb[:], start=True, stop=True)
                            S.wait(P); S.wait(V)
                            S.activation(eG[:, h * 128:(h + 1) * 128], pg[:, 0:128], AF.Exp)
                            S.wait(S)
                            S.activation(sr1[:, 0:128], pg[:, 0:128], AF.Exp, scale=-1.0)
                            V.wait(S)
                            V.tensor_tensor(qt_sb[:, h * 128:(h + 1) * 128],
                                            qf[:, h * TQ:(h + 1) * TQ][:, gsl],
                                            eG[:, h * 128:(h + 1) * 128], op=ALU.mult)
                            V.tensor_tensor(kc_sb[:, h * 128:(h + 1) * 128],
                                            kf[:, h * TQ:(h + 1) * TQ][:, gsl],
                                            sr1[:, 0:128], op=ALU.mult)
                            V.wait(V)
                            eslice = eG[:, h * 128:(h + 1) * 128][:, 31::32]
                            V.tensor_tensor(
                                khf_sb[:, h * 128:(h + 1) * 128]
                                .rearrange("p (c i) -> p c i", i=32),
                                kc_sb[:, h * 128:(h + 1) * 128]
                                .rearrange("p (c i) -> p c i", i=32),
                                eslice[:, :, None].broadcast_to([128, 4, 32]),
                                op=ALU.mult)
                            V.wait(V)
                            em2 = {}
                            P.wait(V)
                            emit_transpose_set(ph, None,
                                               [kht_sb[:, h * 128:(h + 1) * 128]],
                                               [tpA, tpB], id_sb, 1, em2,
                                               in_slices=[khf_sb[:, h * 128:(h + 1) * 128]])
                            pass
                        sprep = S.n
                        vprep = V.n
                        for ci in range(4 if stage >= 3 else 0):
                            rs = slice(32 * ci, 32 * ci + 32)
                            for h in range(2):
                                pP_h = pP if h == 0 else pmB
                                po_h = po if h == 0 else po2
                                pS_h = pS if h == 0 else pmA
                                P.wait_ev(V, pPread[h])
                                P.wait_ev(S, sprep)
                                P.matmul(pP_h[rs, 0:32],
                                         kc_sb[:, h * 128 + 32 * ci: h * 128 + 32 * ci + 32],
                                         qt_sb[:, h * 128 + 32 * ci: h * 128 + 32 * ci + 32],
                                         start=True, stop=True,
                                         tile_position=(0, 32 * ci))
                                ptm = P.n
                                V.wait_ev(P, ptm)
                                V.tensor_tensor(Pt_sb[rs, h * 32:(h + 1) * 32],
                                                pP_h[rs, 0:32], m32_sb[:], op=ALU.mult)
                                pPread[h] = V.n
                                P.wait_ev(V, pPread[h])
                                P.wait_ev(V, zmark[h])
                                P.wait_ev(V, pSread[h])
                                P.wait_ev(S, osmark[h])
                                P.matmul(po_h[rs, 0:128],
                                         qt_sb[:, h * 128 + 32 * ci: h * 128 + 32 * ci + 32],
                                         Sbf[:, h * 128:(h + 1) * 128],
                                         start=True, stop=False, tile_position=(0, 32 * ci))
                                P.matmul(po_h[rs, 0:128],
                                         Pt_sb[rs, h * 32:(h + 1) * 32],
                                         vt[rs, gi * 256 + h * 128: gi * 256 + (h + 1) * 128],
                                         start=False, stop=True,
                                         tile_position=(32 * ci, 32 * ci))
                                P.matmul(pS_h[:, 0:128],
                                         kht_sb[rs, h * 128:(h + 1) * 128],
                                         vt[rs, gi * 256 + h * 128: gi * 256 + (h + 1) * 128],
                                         start=True, stop=True,
                                         tile_position=(32 * ci, 0))
                                pchunk = P.n
                                S.wait_ev(P, pchunk)
                                S.activation(o_q[rs, gi * 256 + h * 128:
                                                 gi * 256 + (h + 1) * 128],
                                             po_h[rs, 0:128], AF.Copy)
                                osmark[h] = S.n
                                V.wait_ev(P, pchunk)
                                V.wait_ev(V, zmark[h])
                                V.scalar_tensor_tensor(
                                    S32[:, h * 128:(h + 1) * 128],
                                    S32[:, h * 128:(h + 1) * 128],
                                    eG[:, h * 128 + 32 * ci + 31: h * 128 + 32 * ci + 32],
                                    pS_h[:, 0:128], op0=ALU.mult, op1=ALU.add)
                                pSread[h] = V.n
                                V.wait_ev(V, pSread[h])
                                V.tensor_copy(Sbf[:, h * 128:(h + 1) * 128],
                                              S32[:, h * 128:(h + 1) * 128])
                                zmark[h] = V.n
                    nc.sync.wait_ge(S.sem, S.n)
                    Y.dma_start(o_tm.rearrange("(tt p) n -> p tt n", p=128)
                                [:, qi * NG:(qi + 1) * NG, :],
                                o_q[:].rearrange("p (tt n) -> p tt n", n=256))
                Y.wait(Y); G.wait(G)
        return ph


def _const_l1_rope():
    half = 16
    inv_freq = 1.0 / 10000.0 ** (np.arange(0, half, 2, dtype=np.float32) / half)
    t = np.arange(T, dtype=np.float32) / 40.0
    freqs = t[:, None] * inv_freq[None]
    emb = np.concatenate([freqs, freqs], -1)
    return np.cos(emb).astype(np.float32), np.sin(emb).astype(np.float32)


def _const_maskT():
    m = np.zeros((128, 128), np.float32)
    tk = np.arange(128)[:, None]
    tq = np.arange(128)[None, :]
    m[tk > tq] = -1e30
    return m


def build_l1proj(tiles=PT // 128, name="l1proj", ph=None):
    if ph is None:
        ph = Ph(name)
    nc = ph.nc
    TL = tiles * 128
    nch = D // 128
    xn_fm = ph.io("xn_fm", [D, TL], BF16, "ExternalInput")
    wdkv = ph.io("wdkv", [D, 512], F32, "ExternalInput")
    bdkv = ph.io("bdkv", [512], F32, "ExternalInput")
    wdq = ph.io("wdq", [D, 512], F32, "ExternalInput")
    bdq = ph.io("bdq", [512], F32, "ExternalInput")
    wr = ph.io("wr", [D], F32, "ExternalInput")
    ckv = ph.io("ckv", [512, TL], BF16, "ExternalOutput")
    cq = ph.io("cq", [512, TL], BF16, "ExternalOutput")
    identf_l1p = ph.const_in("identf", _const_identity_f32())
    logits = ph.io("logits", [tiles * 128], F32, "ExternalOutput")

    with ph.scope():
        xnf = ph.sbuf("xnf", [128, nch * TL], BF16)
        wkv_sb = ph.sbuf("wkv_sb", [128, nch * 512], BF16)
        wq_sb = ph.sbuf("wq_sb", [128, nch * 512], BF16)
        bkv_sb = ph.sbuf("bkv_sb", [128, 4], F32)
        bq_sb = ph.sbuf("bq_sb", [128, 4], F32)
        wr_sb = ph.sbuf("wr_sb", [128, nch], BF16)
        ck_sb = ph.sbuf("ck_sb", [128, 4 * TL], BF16)
        cq_sb = ph.sbuf("cq_sb", [128, 4 * TL], BF16)
        lg_sb = ph.sbuf("lg_sb", [128, tiles], F32)
        lgT_sb = ph.sbuf("lgT_sb", [tiles, 128], F32)
        idf2_sb = ph.sbuf("idf2_sb", [128, 128], F32)
        pm = [ph.psum("pm%d" % b, [128, 512], F32) for b in range(2)]
        pl = ph.psum("pl", [128, 1], F32)
        plT = ph.psum("plT", [128, 128], F32)
        V, S, P, G, Y = ph.V, ph.S, ph.P, ph.G, ph.Y

        with nc.Block() as blk:
            @blk.vector
            def _(_v):
                G.dma_start(xnf[:].rearrange("p (c t) -> p c t", t=TL),
                            xn_fm.rearrange("(c p) t -> p c t", p=128))
                emit_weight_cast_load(ph, wdkv, wkv_sb, D, 512)
                emit_weight_cast_load(ph, wdq, wq_sb, D, 512)
                G.dma_start(bkv_sb[:], bdkv.rearrange("(c p) -> p c", p=128),
                            allow_slow_non_contiguous=True)
                G.dma_start(bq_sb[:], bdq.rearrange("(c p) -> p c", p=128),
                            allow_slow_non_contiguous=True)
                G.dma_start(wr_sb[:], wr.rearrange("(c p) -> p c", p=128),
                            allow_slow_non_contiguous=True)
                P.wait(G); S.wait(G); V.wait(G)
                for which, wsb, bsb, csb in ((0, wkv_sb, bkv_sb, ck_sb),
                                             (1, wq_sb, bq_sb, cq_sb)):
                    for cc in range(4):
                        for ts in range(TL // 512):
                            pb = pm[ts % 2]
                            P.wait(V)
                            for c in range(nch):
                                P.matmul(pb[:],
                                         wsb[:, c * 512 + cc * 128: c * 512 + (cc + 1) * 128],
                                         xnf[:, c * TL + ts * 512: c * TL + (ts + 1) * 512],
                                         start=(c == 0), stop=(c == nch - 1))
                            V.wait(P)
                            V.tensor_tensor(csb[:, cc * TL + ts * 512: cc * TL + (ts + 1) * 512],
                                            pb[:], bsb[:, cc:cc + 1].broadcast_to([128, 512]),
                                            op=ALU.add)
                for i in range(tiles):
                    P.wait(V)
                    for c in range(nch):
                        P.matmul(pl[:], xnf[:, c * TL + i * 128: c * TL + (i + 1) * 128],
                                 wr_sb[:, c:c + 1], start=(c == 0), stop=(c == nch - 1))
                    V.wait(P)
                    V.tensor_copy(lg_sb[:, i:i + 1], pl[:])
                nc.sync.wait_ge(S.sem, S.n)
                nc.sync.wait_ge(V.sem, V.n)
                Y.dma_start(ckv.rearrange("(c p) t -> p c t", p=128),
                            ck_sb[:].rearrange("p (c t) -> p c t", t=TL))
                Y.dma_start(cq.rearrange("(c p) t -> p c t", p=128),
                            cq_sb[:].rearrange("p (c t) -> p c t", t=TL))
                Y.dma_start(logits.rearrange("(f p) -> f p", p=128), lgT_sb[:])
                Y.wait(Y); G.wait(G)
        return ph


def build_l1flash(Ttot=T, name="l1flash", ph=None, src_blk=None):
    if ph is None:
        ph = Ph(name)
    nc = ph.nc
    NT = Ttot // 128
    if src_blk is None:
        src_blk = Ttot
    nsb = Ttot // src_blk
    ckv = ph.io("ckv", [nsb * 512, src_blk], BF16, "ExternalInput")
    cqf = ph.io("cqf", [nsb * 512, src_blk], BF16, "ExternalInput")
    xn_fm = ph.io("xn_fm", [nsb * D, src_blk], BF16, "ExternalInput")
    wuk = ph.io("wuk", [512, 192], F32, "ExternalInput")
    buk = ph.io("buk", [192], F32, "ExternalInput")
    wuv = ph.io("wuv", [512, 256], F32, "ExternalInput")
    buv = ph.io("buv", [256], F32, "ExternalInput")
    wuq = ph.io("wuq", [512, 192], F32, "ExternalInput")
    buq = ph.io("buq", [192], F32, "ExternalInput")
    wqr = ph.io("wqr", [512, 64], F32, "ExternalInput")
    bqr = ph.io("bqr", [64], F32, "ExternalInput")
    wkr = ph.io("wkr", [D, 64], F32, "ExternalInput")
    bkr = ph.io("bkr", [64], F32, "ExternalInput")
    o_tm = ph.io("o_tm", [Ttot, 256], BF16, "ExternalOutput")
    ident = ph.const_in("ident", _const_identity_bf16())
    identf = ph.const_in("identf", _const_identity_f32())
    cos_c, sin_c = _const_l1_rope()
    cosc = ph.const_in("cosc", cos_c[:Ttot])
    sinc = ph.const_in("sinc", sin_c[:Ttot])
    maskT = ph.const_in("maskT", _const_maskT())
    onesb = ph.const_in("onesb", np.ones((1, 128), NPBF))
    onescol = ph.const_in("onescol", np.ones((128, 1), NPBF))
    onesf = ph.const_in("onesf", np.ones((1, 128), np.float32))
    nch = D // 128
    SC = float(L1_SCALE)

    with ph.scope():
        kfull = [ph.sbuf("kfull%d" % h, [128, Ttot], BF16) for h in range(2)]
        qfull = [ph.sbuf("qfull%d" % h, [128, Ttot], BF16) for h in range(2)]
        vsb = [ph.sbuf("vsb%d" % h, [128, Ttot], BF16) for h in range(2)]
        ot_sb = ph.sbuf("ot_sb", [128, (Ttot // 128) * 256], BF16)
        cstr = ph.sbuf("cstr", [128, 2 * 8 * 128], BF16)
        xstr = ph.sbuf("xstr", [128, 2 * nch * 128], BF16)
        wuk_sb = ph.sbuf("wuk_sb", [128, 4 * 192], BF16)
        wuq_sb = ph.sbuf("wuq_sb", [128, 4 * 192], BF16)
        wuv_sb = ph.sbuf("wuv_sb", [128, 4 * 256], BF16)
        wqr_sb = ph.sbuf("wqr_sb", [128, 4 * 64], BF16)
        wkr_sb = ph.sbuf("wkr_sb", [128, nch * 64], BF16)
        buk_sb = ph.sbuf("buk_sb", [96, 2], F32)
        buq_sb = ph.sbuf("buq_sb", [96, 2], F32)
        buv_sb = ph.sbuf("buv_sb", [1, 256], BF16)
        bqr_sb = ph.sbuf("bqr_sb", [1, 64], BF16)
        bkr_sb = ph.sbuf("bkr_sb", [1, 64], BF16)
        cos_sb = ph.sbuf("cos_sb", [128, NT * 16], F32)
        sin_sb = ph.sbuf("sin_sb", [128, NT * 16], F32)
        rot_sb = ph.sbuf("rot_sb", [128, 64], BF16)
        rA = ph.sbuf("rA", [128, 16], F32)
        rB = ph.sbuf("rB", [128, 16], F32)
        id_sb = ph.sbuf("id_sb", [128, 128], BF16)
        idf_sb = ph.sbuf("idf_sb", [128, 128], F32)
        mT_sb = ph.sbuf("mT_sb", [128, 128], F32)
        on_sb = ph.sbuf("on_sb", [1, 128], BF16)
        onc_sb = ph.sbuf("onc_sb", [128, 1], BF16)
        onf_sb = ph.sbuf("onf_sb", [1, 128], F32)
        p_sb = [ph.sbuf("p_sb%d" % b, [128, 512], BF16) for b in range(2)]
        mrow = ph.sbuf("mrow", [128, 1], F32)
        m1 = ph.sbuf("m1", [1, 1], F32)
        mbc = ph.sbuf("mbc", [128, 1], F32)
        nbias = ph.sbuf("nbias", [128, 1], F32)
        lsb = ph.sbuf("lsb", [128, 1], F32)
        linv = ph.sbuf("linv", [128, 1], F32)
        otmp = ph.sbuf("otmp", [128, 128], BF16)
        pB = ph.psum("pB", [128, 512], F32)
        pB2 = ph.psum("pB2", [128, 512], F32)
        po = ph.psum("po", [128, 128], F32)
        pl = ph.psum("pl", [128, 1], F32)
        pmax = ph.psum("pmax", [128, 128], F32)
        tpA = ph.psum("tpA", [128, 128], BF16)
        V, S, P, G, Y = ph.V, ph.S, ph.P, ph.G, ph.Y

        with nc.Block() as blk:
            @blk.vector
            def _(_v):
                G.dma_start(id_sb[:], ident)
                G.dma_start(idf_sb[:], identf)
                G.dma_start(mT_sb[:], maskT)
                G.dma_start(on_sb[:], onesb)
                G.dma_start(onc_sb[:], onescol)
                G.dma_start(onf_sb[:], onesf)
                G.dma_start(cos_sb[:].rearrange("p (t f) -> p t f", f=16),
                            cosc.rearrange("(t p) f -> p t f", p=128))
                G.dma_start(sin_sb[:].rearrange("p (t f) -> p t f", f=16),
                            sinc.rearrange("(t p) f -> p t f", p=128))
                emit_weight_cast_load(ph, wuk, wuk_sb, 512, 192)
                emit_weight_cast_load(ph, wuq, wuq_sb, 512, 192)
                emit_weight_cast_load(ph, wuv, wuv_sb, 512, 256)
                emit_weight_cast_load(ph, wqr, wqr_sb, 512, 64)
                emit_weight_cast_load(ph, wkr, wkr_sb, D, 64)
                G.dma_start(buk_sb[:], buk.rearrange("(h p) -> p h", p=96),
                            allow_slow_non_contiguous=True)
                G.dma_start(buq_sb[:], buq.rearrange("(h p) -> p h", p=96),
                            allow_slow_non_contiguous=True)
                G.dma_start(buv_sb[:], buv[None, :])
                G.dma_start(bqr_sb[:], bqr[None, :])
                G.dma_start(bkr_sb[:], bkr[None, :])
                P.wait(G); S.wait(G); V.wait(G)

                def rotary_tm(src_ap, dst_ap, ti):
                    cs = cos_sb[:, ti * 16:(ti + 1) * 16]
                    sn = sin_sb[:, ti * 16:(ti + 1) * 16]
                    V.wait(V); V.wait(S); V.wait(P)
                    V.tensor_tensor(rA[:], src_ap[:, 0:16], cs, op=ALU.mult)
                    V.tensor_tensor(rB[:, 0:8], src_ap[:, 8:16], sn[:, 0:8], op=ALU.mult)
                    V.tensor_tensor(rB[:, 8:16], src_ap[:, 0:8], sn[:, 8:16], op=ALU.mult)
                    V.wait(V)
                    V.tensor_tensor(dst_ap[:, 0:8], rA[:, 0:8], rB[:, 0:8], op=ALU.subtract)
                    V.tensor_tensor(dst_ap[:, 8:16], rA[:, 8:16], rB[:, 8:16], op=ALU.add)
                    V.wait(V)

                for ti in range(NT):
                    sl = ti % 2
                    xb = xstr[:, sl * nch * 128:(sl + 1) * nch * 128]
                    if ti >= 2:
                        G.raw.wait_ge(P.sem, P.n)
                    rk, tl = (ti * 128) // src_blk, (ti * 128) % src_blk
                    G.dma_start(xb.rearrange("p (c t) -> p c t", t=128),
                                xn_fm.rearrange("(r c p) t -> p r c t", p=128,
                                                c=D // 128)
                                [:, rk, :, tl:tl + 128], ch=1 + sl)
                    P.wait_dch(G, 1 + sl)
                    P.wait(S); P.wait(V)
                    for c in range(nch):
                        P.matmul(pB[:, 0:64], xb[:, c * 128:(c + 1) * 128],
                                 wkr_sb[:, c * 64:(c + 1) * 64],
                                 start=(c == 0), stop=False)
                    P.matmul(pB[:, 0:64], on_sb[:], bkr_sb[:], start=False, stop=True)
                    V.wait(P)
                    V.tensor_copy(rot_sb[:], pB[:, 0:64])
                    for h in range(2):
                        rotary_tm(rot_sb[:, h * 32:(h + 1) * 32],
                                  rot_sb[:, h * 32:(h + 1) * 32], ti)
                    for h in range(2):
                        P.wait(V)
                        P.transpose(tpA[96:128, :], rot_sb[:, h * 32:(h + 1) * 32],
                                    id_sb[:], tile_position=(0, 96))
                        V.wait(P)
                        V.tensor_copy(kfull[h][96:128, ti * 128:(ti + 1) * 128],
                                      tpA[96:128, :])
                G.raw.wait_ge(P.sem, P.n)
                for ti in range(NT):
                    sl = ti % 2
                    cb = cstr[:, sl * 8 * 128:(sl + 1) * 8 * 128]
                    ck_t = cb[:, 0:4 * 128]
                    cq_t = cb[:, 4 * 128:8 * 128]
                    if ti >= 2:
                        G.raw.wait_ge(P.sem, P.n)
                    rk, tl = (ti * 128) // src_blk, (ti * 128) % src_blk
                    G.dma_start(ck_t.rearrange("p (c t) -> p c t", t=128),
                                ckv.rearrange("(r c p) t -> p r c t", p=128, c=4)
                                [:, rk, :, tl:tl + 128], ch=1 + sl)
                    G.dma_start(cq_t.rearrange("p (c t) -> p c t", t=128),
                                cqf.rearrange("(r c p) t -> p r c t", p=128, c=4)
                                [:, rk, :, tl:tl + 128], ch=1 + sl)
                    P.wait_dch(G, 1 + sl)
                    for h in range(2):
                        P.wait(S); P.wait(V)
                        for c in range(4):
                            P.matmul(pB[0:96, 0:128],
                                     wuk_sb[:, c * 192 + h * 96: c * 192 + (h + 1) * 96],
                                     ck_t[:, c * 128:(c + 1) * 128],
                                     start=(c == 0), stop=(c == 3))
                        V.wait(P)
                        V.tensor_tensor(kfull[h][0:96, ti * 128:(ti + 1) * 128],
                                        pB[0:96, 0:128],
                                        buk_sb[:, h:h + 1].broadcast_to([96, 128]),
                                        op=ALU.add)
                        for c in range(4):
                            P.matmul(pmax[0:96, 0:128],
                                     wuq_sb[:, c * 192 + h * 96: c * 192 + (h + 1) * 96],
                                     cq_t[:, c * 128:(c + 1) * 128],
                                     start=(c == 0), stop=(c == 3))
                        V.wait(P)
                        V.tensor_tensor(qfull[h][0:96, ti * 128:(ti + 1) * 128],
                                        pmax[0:96, 0:128],
                                        buq_sb[:, h:h + 1].broadcast_to([96, 128]),
                                        op=ALU.add)
                        P.wait(S); P.wait(V)
                        for c in range(4):
                            P.matmul(po[:], ck_t[:, c * 128:(c + 1) * 128],
                                     wuv_sb[:, c * 256 + h * 128: c * 256 + (h + 1) * 128],
                                     start=(c == 0), stop=False)
                        P.matmul(po[:], on_sb[:],
                                 buv_sb[0:1, h * 128:(h + 1) * 128],
                                 start=False, stop=True)
                        V.wait(P)
                        V.tensor_copy(vsb[h][:, ti * 128:(ti + 1) * 128], po[:])
                        for c in range(4):
                            P.matmul(pl[:] if False else pB[:, 256:288],
                                     cq_t[:, c * 128:(c + 1) * 128],
                                     wqr_sb[:, c * 64 + h * 32: c * 64 + (h + 1) * 32],
                                     start=(c == 0), stop=False)
                        P.matmul(pB[:, 256:288], on_sb[:],
                                 bqr_sb[0:1, h * 32:(h + 1) * 32],
                                 start=False, stop=True)
                        V.wait(P)
                        V.tensor_copy(rot_sb[:, 0:32], pB[:, 256:288])
                        rotary_tm(rot_sb[:, 0:32], rot_sb[:, 0:32], ti)
                        P.wait(V)
                        P.transpose(tpA[96:128, :], rot_sb[:, 0:32],
                                    id_sb[:], tile_position=(0, 96))
                        V.wait(P)
                        V.tensor_copy(qfull[h][96:128, ti * 128:(ti + 1) * 128],
                                      tpA[96:128, :])
                smark = {0: 0, 1: 0}
                pmark = {0: 0, 1: 0}
                pvmark = {0: 0, 1: 0}
                vread1 = 0
                for h in range(2):
                    for qt in range(NT):
                        nf = qt // 4
                        dc = qt - 4 * nf + 1
                        qsl = slice(qt * 128, (qt + 1) * 128)
                        banks = (pB, pB2)
                        V.wait(S)
                        P.wait_ev(S, smark[1])
                        P.wait_ev(V, vread1)
                        for jj in range(dc):
                            tk = 4 * nf + jj
                            P.matmul(pB2[:, jj * 128:(jj + 1) * 128],
                                     kfull[h][:, tk * 128:(tk + 1) * 128],
                                     qfull[h][:, qsl], start=True, stop=True)
                        pmark[1] = P.n
                        V.wait_ev(P, pmark[1])
                        V.tensor_tensor(pB2[:, (dc - 1) * 128: dc * 128],
                                        pB2[:, (dc - 1) * 128: dc * 128],
                                        mT_sb[:], op=ALU.add)
                        V.wait(V)
                        V.reduce_max(mrow[:], pB2[:, 0:dc * 128], axis=AX.X)
                        P.wait(V)
                        P.transpose(pmax[0:1, :], mrow[:], idf_sb[:])
                        V.wait(P)
                        V.reduce_max(m1[:], pmax[0:1, 0:128], axis=AX.X)
                        V.wait(V)
                        P.wait(V)
                        P.matmul(pmax[:, 0:1], onf_sb[:], m1[:], start=True, stop=True)
                        V.wait(P)
                        V.tensor_copy(mbc[:], pmax[:, 0:1])
                        V.wait(V)
                        V.tensor_scalar(nbias[:], mbc[:], -SC, None, op0=ALU.mult)
                        vread1 = V.n
                        pending = (dc, 4 * nf, 1, True)
                        for kb in range(nf + 1):
                            if kb < nf:
                                idx = kb % 2
                                P.wait_ev(S, smark[idx])
                                for jj in range(4):
                                    tk = 4 * kb + jj
                                    P.matmul(banks[idx][:, jj * 128:(jj + 1) * 128],
                                             kfull[h][:, tk * 128:(tk + 1) * 128],
                                             qfull[h][:, qsl], start=True, stop=True)
                                pmark[idx] = P.n
                            w, tkb, pidx, first = pending
                            S.wait_ev(P, pmark[pidx])
                            S.wait_ev(P, pvmark[pidx])
                            S.wait_ev(V, vread1)
                            S.activation(p_sb[pidx][:, 0:w * 128],
                                         banks[pidx][:, 0:w * 128],
                                         AF.Exp, bias=nbias[:], scale=SC)
                            smark[pidx] = S.n
                            if first:
                                P.wait(V)
                            P.wait_ev(S, smark[pidx])
                            for jj in range(w):
                                tk = tkb + jj
                                last = (kb == nf and jj == w - 1)
                                P.matmul(pl[:], p_sb[pidx][:, jj * 128:(jj + 1) * 128],
                                         onc_sb[:], start=first and jj == 0, stop=last)
                                P.matmul(po[:], p_sb[pidx][:, jj * 128:(jj + 1) * 128],
                                         vsb[h][:, tk * 128:(tk + 1) * 128],
                                         start=first and jj == 0, stop=last)
                            pvmark[pidx] = P.n
                            pending = (4, 4 * kb, kb % 2, False)
                        V.wait(P)
                        V.tensor_copy(lsb[:], pl[:])
                        V.wait(V)
                        V.reciprocal(linv[:], lsb[:])
                        V.wait(V)
                        V.tensor_tensor(ot_sb[:, qt * 256 + h * 128:
                                              qt * 256 + (h + 1) * 128],
                                        po[:],
                                        linv[:].broadcast_to([128, 128]),
                                        op=ALU.mult)
                nc.sync.wait_ge(V.sem, V.n)
                Y.dma_start(o_tm.rearrange("(tt p) n -> p tt n", p=128),
                            ot_sb[:].rearrange("p (tt n) -> p tt n", n=256))
                Y.wait(Y); G.wait(G)
        return ph


def _const_su_ones():
    m = np.zeros((128, 128), np.float32)
    p = np.arange(128)[:, None]
    i = np.arange(128)[None, :]
    m[p < i] = 1.0
    return m.astype(NPBF)


def build_l1sel(name="l1sel", ph=None):
    if ph is None:
        ph = Ph(name)
    nc = ph.nc
    NF = T // 128
    lgi = ph.io("logits", [T], F32, "ExternalInput")
    xn1 = ph.io("xn1_tm", [T, D], BF16, "ExternalInput")
    iotaj = ph.io("iotaj", [128, 128], F32, "ExternalInput")
    fx = ph.io("fx", [128, D], BF16, "ExternalOutput")
    selidx = ph.io("selidx", [128], I32, "ExternalOutput")
    rw = ph.io("rw", [128], F32, "ExternalOutput")
    iota_c = ph.const_in("iota_c",
                         (np.arange(128)[:, None] * NF +
                          np.arange(NF)[None, :]).astype(np.float32))
    suones = ph.const_in("suones", _const_su_ones())
    onesf = ph.const_in("onesf", np.ones((1, 128), np.float32))
    onescf = ph.const_in("onescf", np.ones((128, 1), np.float32))

    with ph.scope():
        lg = ph.sbuf("lg", [128, NF], F32)
        iota_sb = ph.sbuf("iota_sb", [128, NF], F32)
        su_sb = ph.sbuf("su_sb", [128, 128], BF16)
        onf_sb = ph.sbuf("onf_sb", [1, 128], F32)
        oncf_sb = ph.sbuf("oncf_sb", [128, 1], F32)
        idf_sb = ph.sbuf("idf_sb", [128, 128], F32)
        identf = ph.const_in("identf", _const_identity_f32())
        lo = ph.sbuf("lo", [128, 1], F32)
        hi = ph.sbuf("hi", [128, 1], F32)
        mid = ph.sbuf("mid", [128, 1], F32)
        selv = ph.sbuf("selv", [128, 1], F32)
        d1 = ph.sbuf("d1", [128, 1], F32)
        rowc = ph.sbuf("rowc", [128, 1], F32)
        rowcb = ph.sbuf("rowcb", [128, 1], BF16)
        cnt1 = ph.sbuf("cnt1", [1, 1], F32)
        cntb = ph.sbuf("cntb", [128, 1], F32)
        gmx = ph.sbuf("gmx", [128, 1], F32)
        ngmx = ph.sbuf("ngmx", [128, 1], F32)
        cmp = ph.sbuf("cmp", [128, NF], F32)
        mask = ph.sbuf("mask", [128, NF], F32)
        incl = ph.sbuf("incl", [128, NF], F32)
        slots = ph.sbuf("slots", [128, NF], F32)
        sltmp = ph.sbuf("sltmp", [128, NF], F32)
        slot_i = ph.sbuf("slot_i", [128, NF], I32)
        rwe = ph.sbuf("rwe", [128, NF], F32)
        pair_sb = ph.sbuf("pair_sb", [128, 2 * NF], F32)
        base_sb = ph.sbuf("base_sb", [128, 1], F32)
        mypair = ph.sbuf("mypair", [128, 2], F32)
        myidx = ph.sbuf("myidx", [128, 1], I32)
        iotaj_sb = ph.sbuf("iotaj_sb", [128, 128], F32)
        Mf = ph.sbuf("Mf", [128, 128], F32)
        myrw = ph.sbuf("myrw", [128, 1], F32)
        fx_sb = ph.sbuf("fx_sb", [128, D], BF16)
        selp = ph.sbuf("selp", [128, SEL_K // 128], F32)
        seli = ph.sbuf("seli", [128, SEL_K // 128], I32)
        rwn = ph.sbuf("rwn", [128, SEL_K // 128], F32)
        den = ph.sbuf("den", [128, 1], F32)
        pq = ph.psum("pq", [128, 128], F32)
        V, S, P, G, Y = ph.V, ph.S, ph.P, ph.G, ph.Y

        def cross_part_bcast(src_col, op):
            P.wait(V)
            P.transpose(pq[0:1, :], src_col, idf_sb[:])
            V.wait(P)
            V.tensor_reduce(cnt1[:], pq[0:1, 0:128], axis=AX.X, op=op)
            V.wait(V)
            P.wait(V)
            P.matmul(pq[:, 0:1], onf_sb[:], cnt1[:], start=True, stop=True)
            V.wait(P)
            V.tensor_copy(cntb[:], pq[:, 0:1])
            V.wait(V)

        with nc.Block() as blk:
            @blk.vector
            def _(_v):
                G.dma_start(lg[:], lgi.rearrange("(p f) -> p f", f=NF))
                G.dma_start(iota_sb[:], iota_c)
                G.dma_start(su_sb[:], suones)
                G.dma_start(onf_sb[:], onesf)
                G.dma_start(oncf_sb[:], onescf)
                G.dma_start(idf_sb[:], identf)
                G.dma_start(iotaj_sb[:], iotaj)
                P.wait(G); S.wait(G); V.wait(G)
                V.reduce_max(rowc[:], lg[:], axis=AX.X)
                cross_part_bcast(rowc[:], ALU.max)
                V.tensor_copy(gmx[:], cntb[:])
                V.tensor_scalar(hi[:], cntb[:], 1.0, None, op0=ALU.add)
                V.tensor_scalar(ngmx[:], cntb[:], -1.0, None, op0=ALU.mult)
                V.wait(V)
                V.tensor_reduce(rowc[:], lg[:], axis=AX.X, op=ALU.min)
                cross_part_bcast(rowc[:], ALU.min)
                V.tensor_scalar(lo[:], cntb[:], -1.0, None, op0=ALU.add)
                V.wait(V)
                for _it in range(64):
                    V.tensor_tensor(mid[:], lo[:], hi[:], op=ALU.add)
                    V.wait(V)
                    V.tensor_scalar(mid[:], mid[:], 0.5, None, op0=ALU.mult)
                    V.wait(V)
                    V.tensor_scalar(cmp[:], lg[:], mid[:], None, op0=ALU.is_ge)
                    V.wait(V)
                    V.reduce_sum(rowc[:], cmp[:], axis=AX.X)
                    cross_part_bcast(rowc[:], ALU.add)
                    V.tensor_scalar(selv[:], cntb[:], float(SEL_K), None, op0=ALU.is_ge)
                    V.wait(V)
                    V.tensor_tensor(d1[:], mid[:], lo[:], op=ALU.subtract)
                    V.wait(V)
                    V.scalar_tensor_tensor(lo[:], d1[:], selv[:], lo[:],
                                           op0=ALU.mult, op1=ALU.add)
                    V.tensor_tensor(d1[:], hi[:], mid[:], op=ALU.subtract)
                    V.wait(V)
                    V.scalar_tensor_tensor(d1[:], d1[:], selv[:], mid[:],
                                           op0=ALU.mult, op1=ALU.add)
                    V.wait(V)
                    V.tensor_copy(hi[:], d1[:])
                    V.wait(V)
                V.tensor_scalar(mask[:], lg[:], lo[:], None, op0=ALU.is_ge)
                V.wait(V)
                V.tensor_tensor_scan(incl[:], mask[:], mask[:], 0.0,
                                     op0=ALU.add, op1=ALU.bypass)
                V.wait(V)
                V.tensor_tensor(slots[:], incl[:], mask[:], op=ALU.subtract)
                V.tensor_copy(rowcb[:], incl[:, NF - 1:NF])
                V.wait(V)
                P.wait(V)
                P.matmul(pq[:, 0:1], su_sb[:], rowcb[:], start=True, stop=True)
                V.wait(P)
                V.tensor_copy(base_sb[:], pq[:, 0:1])
                V.wait(V)
                V.tensor_scalar(slots[:], slots[:], base_sb[:], None, op0=ALU.add)
                V.wait(V)
                V.tensor_tensor(sltmp[:], slots[:], mask[:], op=ALU.mult)
                V.tensor_scalar(cmp[:], mask[:], -60000.0, 60000.0,
                                op0=ALU.mult, op1=ALU.add)
                V.wait(V)
                V.tensor_tensor(sltmp[:], sltmp[:], cmp[:], op=ALU.add)
                V.wait(V)
                V.tensor_copy(slot_i[:], sltmp[:])
                S.wait(V)
                S.activation(rwe[:], lg[:], AF.Exp, bias=ngmx[:])
                V.wait(S); V.wait(V)
                V.tensor_copy(pair_sb[:].rearrange("p (f c) -> p f c", c=2)[:, :, 0],
                              iota_sb[:])
                V.tensor_copy(pair_sb[:].rearrange("p (f c) -> p f c", c=2)[:, :, 1],
                              rwe[:])
                V.wait(V)
                for f in range(NF):
                    V.wait(P)
                    V.tensor_tensor(Mf[:], sltmp[:, f:f + 1].broadcast_to([128, 128]),
                                    iotaj_sb[:], op=ALU.is_equal)
                    P.wait(V)
                    P.matmul(pq[:, 2:4], Mf[:], pair_sb[:, 2 * f:2 * f + 2],
                             start=(f == 0), stop=(f == NF - 1))
                V.wait(P)
                V.tensor_copy(mypair[:], pq[:, 2:4])
                V.wait(V)
                V.tensor_copy(myidx[:], mypair[:, 0:1])
                V.tensor_tensor(cmp[:], rwe[:], mask[:], op=ALU.mult)
                V.wait(V)
                V.reduce_sum(rowc[:], cmp[:], axis=AX.X)
                cross_part_bcast(rowc[:], ALU.add)
                V.reciprocal(den[:], cntb[:])
                V.wait(V)
                V.tensor_scalar(myrw[:], mypair[:, 1:2], den[:], None, op0=ALU.mult)
                nc.gpsimd.wait_ge(V.sem, V.n)
                G.indirect_dma_start(
                    fx_sb[:], None,
                    xn1, IndirectOffsetOnAxis(myidx[:], 0),
                    bounds_check=T - 1, oob_is_err=False, ch=3)
                Y.wait_dch(G, 3)
                nc.sync.wait_ge(V.sem, V.n)
                Y.dma_start(fx, fx_sb[:], ch=0)
                Y.dma_start(selidx[:, None], myidx[:], ch=1)
                Y.dma_start(rw[:, None], myrw[:], ch=1)
                Y.wait(Y); G.wait(G)
        return ph


def build_mini1(name="mini1", ph=None):
    if ph is None:
        ph = Ph(name)
    nc = ph.nc
    MT = SEL_K // 128
    nch = D // 128
    fxd = ph.io("fx_full", [SEL_K, D], BF16, "ExternalInput")
    g1d = ph.io("g1", [D], F32, "ExternalInput")
    wqd = ph.io("wq", [D, 256], F32, "ExternalInput")
    wkd = ph.io("wk", [D, 256], F32, "ExternalInput")
    wvd = ph.io("wv", [D, 256], F32, "ExternalInput")
    wod = ph.io("wo_r", [256, D], F32, "ExternalInput")
    frd = ph.io("freqs", [SEL_K, 64, 2], F32, "ExternalInput")
    mkd = ph.io("mask", [SEL_K, SEL_K], F32, "ExternalInput")
    part1 = ph.io("part1", [SEL_K, D], F32, "ExternalOutput")
    ident = ph.const_in("ident", _const_identity_bf16())
    eps_ap = ph.fconst(RMS1_EPS)

    with ph.scope():
        fx_sb = ph.sbuf("fx_sb", [128, MT * D], BF16)
        h1f = ph.sbuf("h1f", [128, nch * SEL_K], BF16)
        g1_sb = ph.sbuf("g1_sb", [128, nch], F32)
        wq_sb = ph.sbuf("wq_sb", [128, nch * 256], BF16)
        wk_sb = ph.sbuf("wk_sb", [128, nch * 256], BF16)
        wv_sb = ph.sbuf("wv_sb", [128, nch * 256], BF16)
        wo_sb = ph.sbuf("wo_sb", [128, 2 * D], BF16)
        fr_sb = ph.sbuf("fr_sb", [128, MT * 128], F32)
        q_sb = ph.sbuf("q_sb", [128, MT * 256], BF16)
        k_sb = ph.sbuf("k_sb", [128, MT * 256], BF16)
        v_sb = ph.sbuf("v_sb", [128, MT * 256], BF16)
        qf_sb = ph.sbuf("qf_sb", [128, SEL_K], BF16)
        qf2 = [ph.sbuf("qf2_%d" % h, [128, SEL_K], BF16) for h in range(2)]
        kf2 = [ph.sbuf("kf2_%d" % h, [128, SEL_K], BF16) for h in range(2)]
        ptr_sb = ph.sbuf("ptr_sb", [128, 256], BF16)
        of_sb = ph.sbuf("of_sb", [128, 2 * SEL_K], BF16)
        mk_sb = ph.sbuf("mk_sb", [128, SEL_K], F32)
        p1_sb = ph.sbuf("p1_sb", [128, SEL_K], BF16)
        otm_sb = ph.sbuf("otm_sb", [128, 128], BF16)
        pt_sb = ph.sbuf("pt_sb", [128, D], F32)
        scr = ph.sbuf("scr", [128, D], BF16)
        rstd = ph.sbuf("rstd", [128, 1], F32)
        ssq = ph.sbuf("ssq", [128, 1], F32)
        mrow = ph.sbuf("mrow", [128, 1], F32)
        nbias = ph.sbuf("nbias", [128, 1], F32)
        lsum = ph.sbuf("lsum", [128, 1], F32)
        l2 = ph.sbuf("l2", [128, 1], F32)
        linv = ph.sbuf("linv", [128, 1], F32)
        id_sb = ph.sbuf("id_sb", [128, 128], BF16)
        rA = ph.sbuf("rA", [128, 64], F32)
        rB = ph.sbuf("rB", [128, 64], F32)
        pB = ph.psum("pB", [128, 512], F32)
        pC = ph.psum("pC", [128, 512], F32)
        po = ph.psum("po", [128, 128], F32)
        tpA = ph.psum("tpA", [128, 128], BF16)
        tpB = ph.psum("tpB", [128, 128], BF16)
        V, S, P, G, Y = ph.V, ph.S, ph.P, ph.G, ph.Y

        with nc.Block() as blk:
            @blk.vector
            def _(_v):
                G.dma_start(id_sb[:], ident)
                G.dma_start(fx_sb[:].rearrange("p (t d) -> p t d", d=D),
                            fxd.rearrange("(t p) d -> p t d", p=128))
                G.dma_start(g1_sb[:], g1d.rearrange("(c p) -> p c", p=128),
                            allow_slow_non_contiguous=True)
                emit_weight_cast_load(ph, wqd, wq_sb, D, 256)
                emit_weight_cast_load(ph, wkd, wk_sb, D, 256)
                emit_weight_cast_load(ph, wvd, wv_sb, D, 256)
                G.dma_start(wo_sb[:].rearrange("p (c n) -> p c n", n=D),
                            wod.rearrange("(c p) n -> p c n", p=128))
                G.dma_start(fr_sb[:].rearrange("p (t f) -> p t f", f=128),
                            frd.rearrange("(t p) f c -> p t (f c)", p=128))
                P.wait(G); S.wait(G); V.wait(G)
                for wsb in (wq_sb, wk_sb, wv_sb):
                    for c in range(nch):
                        V.tensor_scalar(wsb[:, c * 256:(c + 1) * 256],
                                        wsb[:, c * 256:(c + 1) * 256],
                                        g1_sb[:, c:c + 1], None, op0=ALU.mult)
                V.wait(V)
                em = {}
                for i in range(MT):
                    fxt = fx_sb[:, i * D:(i + 1) * D]
                    V.wait(V); S.wait(S); S.wait(V); S.wait(P); V.wait(P)
                    emit_ln_tile(ph, fxt, scr[:], None, rstd, scr, ssq, None,
                                 eps_ap, D, rms=True)
                    P.wait(V)
                    out_cols = [h1f[:, c * SEL_K + i * 128: c * SEL_K + (i + 1) * 128]
                                for c in range(nch)]
                    emit_transpose_set(ph, scr, out_cols, [tpA, tpB], id_sb, nch, em)
                for i in range(MT):
                    for wsb, dst in ((wq_sb, q_sb), (wk_sb, k_sb), (wv_sb, v_sb)):
                        P.wait(V)
                        for c in range(nch):
                            P.matmul(pB[:, 0:256],
                                     h1f[:, c * SEL_K + i * 128: c * SEL_K + (i + 1) * 128],
                                     wsb[:, c * 256:(c + 1) * 256],
                                     start=(c == 0), stop=(c == nch - 1))
                        V.wait(P)
                        V.tensor_copy(dst[:, i * 256:(i + 1) * 256], pB[:, 0:256])
                    fr = fr_sb[:, i * 128:(i + 1) * 128].rearrange(
                        "p (f c) -> p f c", c=2)
                    cs, sn = fr[:, :, 0], fr[:, :, 1]
                    for dst in (q_sb, k_sb):
                        for h in range(2):
                            b0 = i * 256 + h * 128
                            x1 = dst[:, b0:b0 + 64]
                            x2 = dst[:, b0 + 64:b0 + 128]
                            V.wait(V); V.wait(P)
                            V.tensor_tensor(rA[:], x1, cs, op=ALU.mult)
                            V.tensor_tensor(rB[:], x2, sn, op=ALU.mult)
                            V.wait(V)
                            V.tensor_tensor(rA[:], rA[:], rB[:], op=ALU.subtract)
                            V.wait(V)
                            V.tensor_tensor(rB[:], x1, sn, op=ALU.mult)
                            V.wait(V)
                            V.tensor_tensor(x2, x2, cs, op=ALU.mult)
                            V.wait(V)
                            V.tensor_tensor(x2, x2, rB[:], op=ALU.add)
                            V.tensor_copy(x1, rA[:])
                            V.wait(V)
                SC = float(L1_SCALE)
                for h in range(2):
                    em2 = {}
                    for i in range(MT):
                        P.wait(V)
                        emit_transpose_set(
                            ph, None,
                            [qf_sb[:, (h * MT + i) * 128:(h * MT + i + 1) * 128]
                             if False else qf2[h][:, i * 128:(i + 1) * 128]],
                            [tpA, tpB], id_sb, 1, em2,
                            in_slices=[q_sb[:, i * 256 + h * 128: i * 256 + (h + 1) * 128]])
                        emit_transpose_set(
                            ph, None,
                            [kf2[h][:, i * 128:(i + 1) * 128]], [tpA, tpB],
                            id_sb, 1, em2,
                            in_slices=[k_sb[:, i * 256 + h * 128: i * 256 + (h + 1) * 128]])
                for qt in range(MT):
                    G.raw.wait_ge(V.sem, V.n)
                    G.raw.wait_ge(S.sem, S.n)
                    G.dma_start(mk_sb[:], mkd[qt * 128:(qt + 1) * 128, :], ch=1)
                    V.wait_dch(G, 1)
                    for h in range(2):
                        P.wait(S); P.wait(V)
                        for half, ps in ((0, pB), (1, pC)):
                            P.matmul(ps[:], qf2[h][:, qt * 128:(qt + 1) * 128],
                                     kf2[h][:, half * 512:(half + 1) * 512],
                                     start=True, stop=True)
                        V.wait(P)
                        for half, ps in ((0, pB), (1, pC)):
                            V.scalar_tensor_tensor(
                                ps[:], ps[:], SC, mk_sb[:, half * 512:(half + 1) * 512],
                                op0=ALU.mult, op1=ALU.add)
                        V.wait(V)
                        V.reduce_max(mrow[:], pB[:], axis=AX.X)
                        V.reduce_max(l2[:], pC[:], axis=AX.X)
                        V.wait(V)
                        V.tensor_tensor(mrow[:], mrow[:], l2[:], op=ALU.max)
                        V.wait(V)
                        V.tensor_scalar(nbias[:], mrow[:], -1.0, None, op0=ALU.mult)
                        S.wait(V)
                        S.activation(p1_sb[:, 0:512], pB[:], AF.Exp,
                                     bias=nbias[:], accum_out=lsum[:])
                        S.wait(S)
                        S.activation(p1_sb[:, 512:1024], pC[:], AF.Exp,
                                     bias=nbias[:], accum_out=l2[:])
                        V.wait(S)
                        V.tensor_tensor(lsum[:], lsum[:], l2[:], op=ALU.add)
                        V.wait(V)
                        V.reciprocal(linv[:], lsum[:])
                        P.wait(S); P.wait(V)
                        emA = {}
                        for tk in range(MT):
                            buf = tk % 2
                            tp = (tpA, tpB)[buf]
                            if emA.get(buf):
                                P.wait_ev(V, emA[buf])
                            P.transpose(tp[:], p1_sb[:, tk * 128:(tk + 1) * 128],
                                        id_sb[:])
                            V.wait(P)
                            V.tensor_copy(ptr_sb[:, buf * 128:(buf + 1) * 128], tp[:])
                            emA[buf] = V.n
                            P.wait_ev(V, V.n)
                            P.matmul(po[:], ptr_sb[:, buf * 128:(buf + 1) * 128],
                                     v_sb[:, tk * 256 + h * 128: tk * 256 + (h + 1) * 128],
                                     start=(tk == 0), stop=(tk == MT - 1))
                        V.wait(P)
                        V.tensor_tensor(otm_sb[:], po[:],
                                        linv[:].broadcast_to([128, 128]), op=ALU.mult)
                        P.wait(V)
                        P.transpose(tpA[:], otm_sb[:], id_sb[:])
                        V.wait(P)
                        V.tensor_copy(of_sb[:, h * SEL_K + qt * 128:
                                           h * SEL_K + (qt + 1) * 128], tpA[:])
                        emA[0] = V.n
                for i in range(MT):
                    for nt in range(4):
                        ps = (pB, pC)[nt % 2]
                        P.wait(V)
                        for c in range(2):
                            P.matmul(ps[:],
                                     of_sb[:, c * SEL_K + i * 128: c * SEL_K + (i + 1) * 128],
                                     wo_sb[:, c * D + nt * 512: c * D + (nt + 1) * 512],
                                     start=(c == 0), stop=(c == 1))
                        V.wait(P)
                        V.tensor_copy(pt_sb[:, nt * 512:(nt + 1) * 512], ps[:])
                    V.wait(V)
                    nc.sync.wait_ge(V.sem, V.n)
                    Y.dma_start(part1[i * 128:(i + 1) * 128, :], pt_sb[:], ch=i % 2)
                Y.wait(Y); G.wait(G)
        return ph


def build_mini2(name="mini2", ph=None, pre_summed=False):
    if ph is None:
        ph = Ph(name)
    nc = ph.nc
    parts = ph.io("parts", [1, 128, D] if pre_summed else [NCORES, 128, D],
                  F32, "ExternalInput")
    fxr = ph.io("fx_rows", [128, D], BF16, "ExternalInput")
    xb = ph.io("xb", [128, D], F32, "ExternalOutput")
    h2 = ph.io("h2", [128, D], BF16, "ExternalOutput")
    eps_ap = ph.fconst(RMS1_EPS)
    with ph.scope():
        acc = ph.sbuf("acc", [128, D], F32)
        pin = ph.sbuf("pin", [128, 2 * D], F32)
        fx_sb = ph.sbuf("fx_sb", [128, D], BF16)
        h2_sb = ph.sbuf("h2_sb", [128, D], BF16)
        scr = ph.sbuf("scr", [128, D], BF16)
        rstd = ph.sbuf("rstd", [128, 1], F32)
        ssq = ph.sbuf("ssq", [128, 1], F32)
        V, S, P, G, Y = ph.V, ph.S, ph.P, ph.G, ph.Y
        with nc.Block() as blk:
            @blk.vector
            def _(_v):
                Y.dma_start(fx_sb[:], fxr, ch=2)
                Y.dma_start(pin[:, 0:D], parts[0], ch=0)
                V.wait_dch(Y, 2); V.wait_dch(Y, 0)
                V.tensor_tensor(acc[:], fx_sb[:], pin[:, 0:D], op=ALU.add)
                for j in range(1, 1 if pre_summed else NCORES):
                    b = j % 2
                    if j >= 2:
                        nc.sync.wait_ge(V.sem, V.n)
                    Y.dma_start(pin[:, b * D:(b + 1) * D], parts[j], ch=b)
                    V.wait_dch(Y, b)
                    V.wait(V)
                    V.tensor_tensor(acc[:], acc[:], pin[:, b * D:(b + 1) * D],
                                    op=ALU.add)
                V.wait(V)
                emit_ln_tile(ph, acc[:], h2_sb[:], None, rstd, scr, ssq, None,
                             eps_ap, D, rms=True)
                nc.sync.wait_ge(V.sem, V.n)
                Y.dma_start(xb, acc[:], ch=2)
                Y.dma_start(h2, h2_sb[:], ch=2)
                Y.wait(Y); G.wait(G)
        return ph


def build_mini3(name="mini3", ph=None):
    if ph is None:
        ph = Ph(name)
    nc = ph.nc
    MT = SEL_K // 128
    nch = D // 128
    FS = FFN // NCORES
    h2d = ph.io("h2_full", [SEL_K, D], BF16, "ExternalInput")
    g2d = ph.io("g2", [D], F32, "ExternalInput")
    w1d = ph.io("w1c", [D, FS], F32, "ExternalInput")
    w3d = ph.io("w3c", [D, FS], F32, "ExternalInput")
    w2d = ph.io("w2r", [FS, D], F32, "ExternalInput")
    part2 = ph.io("part2", [SEL_K, D], F32, "ExternalOutput")
    ident = ph.const_in("ident", _const_identity_bf16())

    with ph.scope():
        h2_sb = ph.sbuf("h2_sb", [128, MT * D], BF16)
        h2f = ph.sbuf("h2f", [128, nch * SEL_K], BF16)
        g2_sb = ph.sbuf("g2_sb", [128, nch], F32)
        w1_sb = ph.sbuf("w1_sb", [128, nch * FS], BF16)
        w3_sb = ph.sbuf("w3_sb", [128, nch * FS], BF16)
        w2_sb = ph.sbuf("w2_sb", [128, (FS // 128) * D], BF16)
        u_sb = ph.sbuf("u_sb", [128, MT * FS], BF16)
        uf_sb = ph.sbuf("uf_sb", [128, (FS // 128) * SEL_K], BF16)
        u1_sb = ph.sbuf("u1_sb", [128, 512], BF16)
        pt_sb = ph.sbuf("pt_sb", [128, D], F32)
        id_sb = ph.sbuf("id_sb", [128, 128], BF16)
        pB = ph.psum("pB", [128, 512], F32)
        pC = ph.psum("pC", [128, 512], F32)
        tpA = ph.psum("tpA", [128, 128], BF16)
        tpB = ph.psum("tpB", [128, 128], BF16)
        V, S, P, G, Y = ph.V, ph.S, ph.P, ph.G, ph.Y
        with nc.Block() as blk:
            @blk.vector
            def _(_v):
                G.dma_start(id_sb[:], ident)
                G.dma_start(h2_sb[:].rearrange("p (t d) -> p t d", d=D),
                            h2d.rearrange("(t p) d -> p t d", p=128))
                G.dma_start(g2_sb[:], g2d.rearrange("(c p) -> p c", p=128),
                            allow_slow_non_contiguous=True)
                emit_weight_cast_load(ph, w1d, w1_sb, D, FS)
                emit_weight_cast_load(ph, w3d, w3_sb, D, FS)
                emit_weight_cast_load(ph, w2d, w2_sb, FS, D)
                P.wait(G); S.wait(G); V.wait(G)
                for wsb in (w1_sb, w3_sb):
                    for c in range(nch):
                        V.tensor_scalar(wsb[:, c * FS:(c + 1) * FS],
                                        wsb[:, c * FS:(c + 1) * FS],
                                        g2_sb[:, c:c + 1], None, op0=ALU.mult)
                V.wait(V)
                em = {}
                for i in range(MT):
                    P.wait(V)
                    out_cols = [h2f[:, c * SEL_K + i * 128: c * SEL_K + (i + 1) * 128]
                                for c in range(nch)]
                    emit_transpose_set(ph, h2_sb[:, i * D:(i + 1) * D], out_cols,
                                       [tpA, tpB], id_sb, nch, em)
                for i in range(MT):
                    for st in range(FS // 512):
                        P.wait(S); P.wait(V)
                        for c in range(nch):
                            lt = h2f[:, c * SEL_K + i * 128: c * SEL_K + (i + 1) * 128]
                            P.matmul(pB[:], lt, w1_sb[:, c * FS + st * 512:
                                                      c * FS + (st + 1) * 512],
                                     start=(c == 0), stop=(c == nch - 1))
                        for c in range(nch):
                            lt = h2f[:, c * SEL_K + i * 128: c * SEL_K + (i + 1) * 128]
                            P.matmul(pC[:], lt, w3_sb[:, c * FS + st * 512:
                                                      c * FS + (st + 1) * 512],
                                     start=(c == 0), stop=(c == nch - 1))
                        S.wait(P)
                        S.activation(u1_sb[:], pB[:], AF.Silu)
                        V.wait(S)
                        V.tensor_tensor(u_sb[:, i * FS + st * 512:
                                             i * FS + (st + 1) * 512],
                                        u1_sb[:], pC[:], op=ALU.mult)
                V.wait(V)
                em2 = {}
                for i in range(MT):
                    P.wait(V)
                    out_cols = [uf_sb[:, c * SEL_K + i * 128: c * SEL_K + (i + 1) * 128]
                                for c in range(FS // 128)]
                    emit_transpose_set(ph, u_sb[:, i * FS:(i + 1) * FS], out_cols,
                                       [tpA, tpB], id_sb, FS // 128, em2)
                for i in range(MT):
                    for nt in range(4):
                        ps = (pB, pC)[nt % 2]
                        P.wait(V)
                        for c in range(FS // 128):
                            P.matmul(ps[:],
                                     uf_sb[:, c * SEL_K + i * 128: c * SEL_K + (i + 1) * 128],
                                     w2_sb[:, c * D + nt * 512: c * D + (nt + 1) * 512],
                                     start=(c == 0), stop=(c == FS // 128 - 1))
                        V.wait(P)
                        V.tensor_copy(pt_sb[:, nt * 512:(nt + 1) * 512], ps[:])
                    V.wait(V)
                    nc.sync.wait_ge(V.sem, V.n)
                    Y.dma_start(part2[i * 128:(i + 1) * 128, :], pt_sb[:], ch=i % 2)
                Y.wait(Y); G.wait(G)
        return ph


def build_mini4(name="mini4", ph=None, pre_summed=False):
    if ph is None:
        ph = Ph(name)
    nc = ph.nc
    parts = ph.io("parts2", [1, 128, D] if pre_summed else [NCORES, 128, D],
                  F32, "ExternalInput")
    xbr = ph.io("xb_rows", [128, D], F32, "ExternalInput")
    rwr = ph.io("rw_rows", [128], F32, "ExternalInput")
    xw = ph.io("xw", [128, D], BF16, "ExternalOutput")
    with ph.scope():
        acc = ph.sbuf("acc", [128, D], F32)
        pin = ph.sbuf("pin", [128, 2 * D], F32)
        rw_sb = ph.sbuf("rw_sb", [128, 1], F32)
        xw_sb = ph.sbuf("xw_sb", [128, D], BF16)
        V, S, P, G, Y = ph.V, ph.S, ph.P, ph.G, ph.Y
        with nc.Block() as blk:
            @blk.vector
            def _(_v):
                Y.dma_start(acc[:], xbr, ch=2)
                Y.dma_start(rw_sb[:], rwr[:, None], ch=2)
                V.wait_dch(Y, 2)
                for j in range(1 if pre_summed else NCORES):
                    b = j % 2
                    if j >= 2:
                        nc.sync.wait_ge(V.sem, V.n)
                    Y.dma_start(pin[:, b * D:(b + 1) * D], parts[j], ch=b)
                    V.wait_dch(Y, b)
                    V.wait(V)
                    V.tensor_tensor(acc[:], acc[:], pin[:, b * D:(b + 1) * D],
                                    op=ALU.add)
                V.wait(V)
                V.tensor_scalar(xw_sb[:], acc[:], rw_sb[:], None, op0=ALU.mult)
                nc.sync.wait_ge(V.sem, V.n)
                Y.dma_start(xw, xw_sb[:], ch=2)
                Y.wait(Y); G.wait(G)
        return ph


def build_final(tiles=PT // 128, name="final", ph=None):
    if ph is None:
        ph = Ph(name)
    nc = ph.nc
    TL = tiles * 128
    nch = D // 128
    x1d = ph.io("x1", [TL, D], F32, "ExternalInput")
    xn1d = ph.io("xn1_tm", [TL, D], BF16, "ExternalInput")
    ofd = ph.io("o_a2a", [NCORES * TL, 256], BF16, "ExternalInput")
    wod = ph.io("wout", [D, D], F32, "ExternalInput")
    bod = ph.io("bout", [D], F32, "ExternalInput")
    xwd = ph.io("xw_full", [SEL_K, D], BF16, "ExternalInput")
    seld = ph.io("selidx", [SEL_K], I32, "ExternalInput")
    cbased = ph.io("cbase", [1], F32, "ExternalInput")
    outd = ph.io("out", [TL, D], F32, "ExternalOutput")
    xadd = nc.dram_tensor("xadd", [TL, D], BF16)
    onesc = ph.const_in("onesc", np.ones((1, 128), NPBF))
    ident_fin = ph.const_in("ident", _const_identity_bf16())

    with ph.scope():
        oa_sb = ph.sbuf("oa_sb", [128, D], BF16)
        of_sb = ph.sbuf("of_sb", [128, D], BF16)
        tpA = ph.psum("tpA", [128, 128], BF16)
        tpB = ph.psum("tpB", [128, 128], BF16)
        id_sb = ph.sbuf("id_sb", [128, 128], BF16)
        wo_sb = ph.sbuf("wo_sb", [128, nch * D], BF16)
        bo_sb = ph.sbuf("bo_sb", [1, D], BF16)
        on_sb = ph.sbuf("on_sb", [1, 128], BF16)
        sel_sb = ph.sbuf("sel_sb", [128, SEL_K // 128], F32)
        seli_sb = ph.sbuf("seli_sb", [128, SEL_K // 128], I32)
        cb_sb = ph.sbuf("cb_sb", [1, 1], F32)
        cbb = ph.sbuf("cbb", [128, 1], F32)
        negm = ph.sbuf("negm", [128, SEL_K // 128], F32)
        onesf = ph.const_in("onesf", np.ones((1, 128), np.float32))
        onf_sb = ph.sbuf("onf_sb", [1, 128], F32)
        xw_sb = ph.sbuf("xw_sb", [128, D], BF16)
        zz_sb = ph.sbuf("zz_sb", [128, D], BF16)
        x1_sb = ph.sbuf("x1_sb", [128, 2 * D], F32)
        xn_sb = ph.sbuf("xn_sb", [128, 2 * D], BF16)
        xa_sb = ph.sbuf("xa_sb", [128, 2 * D], BF16)
        gv_sb = ph.sbuf("gv_sb", [128, D], BF16)
        t1_sb = ph.sbuf("t1_sb", [128, 512], F32)
        out_sb = ph.sbuf("out_sb", [128, 2 * D], F32)
        pB = ph.psum("pB", [128, 512], F32)
        pC = ph.psum("pC", [128, 512], F32)
        pq = ph.psum("pq", [128, 1], F32)
        V, S, P, G, Y = ph.V, ph.S, ph.P, ph.G, ph.Y
        with nc.Block() as blk:
            @blk.vector
            def _(_v):
                G.dma_start(id_sb[:], ident_fin)
                emit_weight_cast_load(ph, wod, wo_sb, D, D)
                G.dma_start(bo_sb[:], bod[None, :])
                G.dma_start(on_sb[:], onesc)
                G.dma_start(onf_sb[:], onesf)
                G.dma_start(seli_sb[:], seld.rearrange("(p f) -> p f", f=SEL_K // 128))
                G.dma_start(cb_sb[:], cbased[None, :])
                V.memset(zz_sb[:], 0.0)
                P.wait(G); V.wait(G); S.wait(G)
                for i in range(tiles):
                    Y.dma_start(xadd.ap()[i * 128:(i + 1) * 128, :], zz_sb[:], ch=3)
                P.matmul(pq[:], onf_sb[:], cb_sb[:], start=True, stop=True)
                V.wait(P)
                V.tensor_copy(cbb[:], pq[:])
                V.wait(V)
                V.tensor_copy(sel_sb[:], seli_sb[:])
                V.wait(V)
                V.tensor_scalar(sel_sb[:], sel_sb[:], cbb[:], None, op0=ALU.add)
                V.wait(V)
                V.tensor_scalar(negm[:], sel_sb[:], 0.0, 70000.0,
                                op0=ALU.is_lt, op1=ALU.mult)
                V.wait(V)
                V.tensor_tensor(sel_sb[:], sel_sb[:], negm[:], op=ALU.add)
                V.wait(V)
                V.tensor_copy(seli_sb[:], sel_sb[:])
                V.wait(V)
                G.raw.wait_ge(V.sem, V.n)
                G.wait_dch(Y, 3)
                for r in range(SEL_K // 128):
                    G.dma_start(xw_sb[:],
                                xwd.rearrange("(p r) d -> p r d", r=SEL_K // 128)
                                [:, r, :], ch=1)
                    G.wait_dch(G, 1)
                    G.indirect_dma_start(
                        xadd.ap(), IndirectOffsetOnAxis(seli_sb[:, r:r + 1], 0),
                        xw_sb[:], None,
                        bounds_check=TL - 1, oob_is_err=False, ch=1)
                    G.wait_dch(G, 1)
                st_done = [0, 0]
                for i in range(tiles):
                    b = i % 2
                    if i >= 2:
                        Y.wait_dch(Y, b, st_done[b])
                    Y.dma_start(x1_sb[:, b * D:(b + 1) * D],
                                x1d[i * 128:(i + 1) * 128, :], ch=b)
                    Y.dma_start(xn_sb[:, b * D:(b + 1) * D],
                                xn1d[i * 128:(i + 1) * 128, :], ch=b)
                    Y.dma_start(xa_sb[:, b * D:(b + 1) * D],
                                xadd.ap()[i * 128:(i + 1) * 128, :], ch=b)
                    V.wait_dch(Y, b); V.wait(V); V.wait(S)
                    V.tensor_tensor(gv_sb[:], xn_sb[:, b * D:(b + 1) * D],
                                    xa_sb[:, b * D:(b + 1) * D], op=ALU.add)
                    S.wait(V)
                    S.activation(gv_sb[:], gv_sb[:], AF.Silu)
                    G.raw.wait_ge(P.sem, P.n)
                    for r in range(NCORES):
                        G.dma_start(oa_sb[:, r * 256:(r + 1) * 256],
                                    ofd[r * TL + i * 128: r * TL + (i + 1) * 128, :],
                                    ch=2)
                    P.wait(G); V.wait(G)
                    emt = {}
                    emit_transpose_set(ph, oa_sb[:],
                                       [of_sb[:, c * 128:(c + 1) * 128]
                                        for c in range(nch)],
                                       [tpA, tpB], id_sb, nch, emt)
                    for nt in range(4):
                        ps = (pB, pC)[nt % 2]
                        P.wait(V); P.wait(S)
                        for c in range(nch):
                            P.matmul(ps[:],
                                     of_sb[:, c * 128:(c + 1) * 128],
                                     wo_sb[:, c * D + nt * 512: c * D + (nt + 1) * 512],
                                     start=(c == 0), stop=False)
                        P.matmul(ps[:], on_sb[:], bo_sb[0:1, nt * 512:(nt + 1) * 512],
                                 start=False, stop=True)
                        V.wait(P)
                        V.tensor_tensor(t1_sb[:], ps[:],
                                        gv_sb[:, nt * 512:(nt + 1) * 512], op=ALU.mult)
                        V.wait(V)
                        V.tensor_tensor(out_sb[:, b * D + nt * 512: b * D + (nt + 1) * 512],
                                        t1_sb[:],
                                        x1_sb[:, b * D + nt * 512: b * D + (nt + 1) * 512],
                                        op=ALU.add)
                    V.wait(V)
                    nc.sync.wait_ge(V.sem, V.n)
                    Y.dma_start(outd[i * 128:(i + 1) * 128, :],
                                out_sb[:, b * D:(b + 1) * D], ch=b)
                    st_done[b] = Y.dmark(b)
                Y.wait(Y); G.wait(G)
        return ph



def _shard_rows(a, n=NCORES):
    a = np.ascontiguousarray(a)
    return np.split(a, n, axis=0)


import os as _os
_TRACE = _os.environ.get("K_TRACE", "0") == "1"
_TIMES = []


def _run(ph, in_maps, **kw):
    for m in in_maps:
        for k, v in ph.consts.items():
            m[k] = v
    if _TRACE:
        kw.setdefault("trace", True)
    res = run_bass_kernel_spmd(ph.nc, in_maps, core_ids=list(range(len(in_maps))), **kw)
    if _TRACE:
        _TIMES.append((ph.name, res.exec_time_ns))
    ph.close()
    return res.results


def emit_coll(ph, kind, op, in_ap, out_ap, ch=3):
    nc = ph.nc
    from concourse.aot_env import allow_ici_all_to_all
    with nc.Block() as blk:
        @blk.vector
        def _(_v):
            with allow_ici_all_to_all():
                ph.G.collective_compute(kind, op,
                                        replica_groups=[list(range(NCORES))],
                                        ins=[in_ap], outs=[out_ap], ch=ch)
            ph.G.wait_dch(ph.G, ch)


def build_fused():
    ph = Ph("fused")
    nc = ph.nc

    def dram(name, shape, dtype, kind="Internal", shared=False):
        return nc.dram_tensor(name, list(shape), dtype, kind=kind,
                              addr_space="Shared" if shared else "Local").ap()

    def ein(name, shape, dtype):
        return dram(name, shape, dtype, "ExternalInput")

    x = ein("x", [PT, D], F32)
    wq0 = ein("wq0", [D, 256], F32)
    wk0 = ein("wk0", [D, 256], F32)
    wi0 = ein("wi0", [D, 256], F32)
    wo0 = ein("wo0", [D, D], F32)
    gw0 = ein("gw0", [D], F32)
    mrw = ein("mrw", [D, 8], F32)
    mrb = ein("mrb", [8], F32)
    mew = ein("mew", [8, D, D], F32)
    meb = ein("meb", [8, D], F32)
    mow = ein("mow", [D, D], F32)
    mob = ein("mob", [D], F32)
    wdkv = ein("wdkv", [D, 512], F32)
    bdkv = ein("bdkv", [512], F32)
    wdq = ein("wdq", [D, 512], F32)
    bdq = ein("bdq", [512], F32)
    wrl1 = ein("wrl1", [D], F32)
    wuk = ein("wuk", [512, 192], F32)
    buk = ein("buk", [192], F32)
    wuv = ein("wuv", [512, 256], F32)
    buv = ein("buv", [256], F32)
    wuq = ein("wuq", [512, 192], F32)
    buq = ein("buq", [192], F32)
    wqr = ein("wqr", [512, 64], F32)
    bqr = ein("bqr", [64], F32)
    wkr = ein("wkr", [D, 64], F32)
    bkr = ein("bkr", [64], F32)
    iotaj = ein("iotaj", [128, 128], F32)
    g1m = ein("g1m", [D], F32)
    wqm = ein("wqm", [D, 256], F32)
    wkm = ein("wkm", [D, 256], F32)
    wvm = ein("wvm", [D, 256], F32)
    wom = ein("wom", [256, D], F32)
    freqs = ein("freqs", [SEL_K, 64, 2], F32)
    maskm = ein("maskm", [SEL_K, SEL_K], F32)
    g2m = ein("g2m", [D], F32)
    w1c = ein("w1c", [D, FFN // NCORES], F32)
    w3c = ein("w3c", [D, FFN // NCORES], F32)
    w2r = ein("w2r", [FFN // NCORES, D], F32)
    wout = ein("wout", [D, D], F32)
    bout = ein("bout", [D], F32)
    cbase = ein("cbase", [1], F32)
    out = dram("out", [PT, D], F32, "ExternalOutput")

    xn_tm_l = dram("xn_tm_l", [PT, D], BF16)
    xn_fm_l = dram("xn_fm_l", [D, PT], BF16)
    ag_xnfm = dram("ag_xnfm", [NCORES * D, PT], BF16)
    o_tm_l = dram("o_tm_l", [T, 256], BF16)
    a2a_o = dram("a2a_o", [T, 256], BF16)
    s_tm_l = dram("s_tm_l", [PT, D], BF16)
    g_tm_l = dram("g_tm_l", [PT, D], BF16)
    x1_l = dram("x1_l", [PT, D], F32)
    xn1_tm_l = dram("xn1_tm_l", [PT, D], BF16)
    xn1_fm_l = dram("xn1_fm_l", [D, PT], BF16)
    ag_xn1fm = dram("ag_xn1fm", [NCORES * D, PT], BF16)
    ag_xn1tm = dram("ag_xn1tm", [T, D], BF16)
    ckv_l = dram("ckv_l", [512, PT], BF16)
    cq_l = dram("cq_l", [512, PT], BF16)
    ag_ckv = dram("ag_ckv", [NCORES * 512, PT], BF16)
    ag_cq = dram("ag_cq", [NCORES * 512, PT], BF16)
    lg_l = dram("lg_l", [PT], F32)
    ag_lg = dram("ag_lg", [T], F32)
    o1_tm_l = dram("o1_tm_l", [T, 256], BF16)
    a2a_o1 = dram("a2a_o1", [T, 256], BF16)
    fx_l = dram("fx_l", [128, D], BF16)
    ag_fx = dram("ag_fx", [SEL_K, D], BF16)
    sel_l = dram("sel_l", [128], I32)
    ag_sel = dram("ag_sel", [SEL_K], I32)
    rw_l = dram("rw_l", [128], F32)
    part1_l = dram("part1_l", [SEL_K, D], F32)
    p1s_l = dram("p1s_l", [1, 128, D], F32)
    xb_l = dram("xb_l", [128, D], F32)
    h2_l = dram("h2_l", [128, D], BF16)
    ag_h2 = dram("ag_h2", [SEL_K, D], BF16)
    part2_l = dram("part2_l", [SEL_K, D], F32)
    p2s_l = dram("p2s_l", [1, 128, D], F32)
    xw_l = dram("xw_l", [128, D], BF16)
    ag_xw = dram("ag_xw", [SEL_K, D], BF16)

    ph.fconst(LN_EPS)
    ph.fconst(RMS0_EPS)
    ph.fconst(RMS1_EPS)

    def phase(builder, ext, **kw):
        ph.ext = ext
        builder(ph=ph, **kw)
        ph.ext = None

    AG, RS, A2A = "AllGather", "ReduceScatter", "AllToAll"
    phase(build_ln0, {"x": x, "xn_tm": xn_tm_l, "xn_fm": xn_fm_l})
    emit_coll(ph, AG, ALU.bypass, xn_fm_l, ag_xnfm)
    phase(build_gla, {"xn_fm": ag_xnfm, "wq": wq0, "wk": wk0, "wi": wi0,
                      "o_tm": o_tm_l}, src_blk=PT)
    emit_coll(ph, A2A, ALU.bypass, o_tm_l, a2a_o)
    phase(build_glaep, {"o": a2a_o, "wo": wo0, "gw": gw0, "s_tm": s_tm_l},
          nrk=NCORES)
    phase(build_moe, {"xn_fm": xn_fm_l, "rw": mrw, "rb": mrb, "ew": mew,
                      "eb": meb, "ow": mow, "ob": mob, "g_tm": g_tm_l})
    phase(build_combine_ln, {"x": x, "s_in": s_tm_l, "g_in": g_tm_l,
                             "x1": x1_l, "xn_tm": xn1_tm_l, "xn_fm": xn1_fm_l})
    emit_coll(ph, AG, ALU.bypass, xn1_fm_l, ag_xn1fm)
    emit_coll(ph, AG, ALU.bypass, xn1_tm_l, ag_xn1tm)
    phase(build_l1proj, {"xn_fm": xn1_fm_l, "wdkv": wdkv, "bdkv": bdkv,
                         "wdq": wdq, "bdq": bdq, "wr": wrl1,
                         "ckv": ckv_l, "cq": cq_l, "logits": lg_l})
    emit_coll(ph, AG, ALU.bypass, ckv_l, ag_ckv)
    emit_coll(ph, AG, ALU.bypass, cq_l, ag_cq)
    emit_coll(ph, AG, ALU.bypass, lg_l, ag_lg)
    phase(build_l1flash, {"ckv": ag_ckv, "cqf": ag_cq, "xn_fm": ag_xn1fm,
                          "wuk": wuk, "buk": buk, "wuv": wuv, "buv": buv,
                          "wuq": wuq, "buq": buq, "wqr": wqr, "bqr": bqr,
                          "wkr": wkr, "bkr": bkr, "o_tm": o1_tm_l},
          src_blk=PT)
    emit_coll(ph, A2A, ALU.bypass, o1_tm_l, a2a_o1)
    phase(build_l1sel, {"logits": ag_lg, "xn1_tm": ag_xn1tm, "iotaj": iotaj,
                        "fx": fx_l, "selidx": sel_l, "rw": rw_l})
    emit_coll(ph, AG, ALU.bypass, fx_l, ag_fx)
    emit_coll(ph, AG, ALU.bypass, sel_l, ag_sel)
    phase(build_mini1, {"fx_full": ag_fx, "g1": g1m, "wq": wqm, "wk": wkm,
                        "wv": wvm, "wo_r": wom, "freqs": freqs, "mask": maskm,
                        "part1": part1_l})
    emit_coll(ph, RS, ALU.add, part1_l, p1s_l.rearrange("o p d -> (o p) d"))
    phase(build_mini2, {"parts": p1s_l, "fx_rows": fx_l, "xb": xb_l,
                        "h2": h2_l}, pre_summed=True)
    emit_coll(ph, AG, ALU.bypass, h2_l, ag_h2)
    phase(build_mini3, {"h2_full": ag_h2, "g2": g2m, "w1c": w1c, "w3c": w3c,
                        "w2r": w2r, "part2": part2_l})
    emit_coll(ph, RS, ALU.add, part2_l, p2s_l.rearrange("o p d -> (o p) d"))
    phase(build_mini4, {"parts2": p2s_l, "xb_rows": xb_l, "rw_rows": rw_l,
                        "xw": xw_l}, pre_summed=True)
    emit_coll(ph, AG, ALU.bypass, xw_l, ag_xw)
    phase(build_final, {"x1": x1_l, "xn1_tm": xn1_tm_l, "o_a2a": a2a_o1,
                        "wout": wout, "bout": bout, "xw_full": ag_xw,
                        "selidx": ag_sel, "cbase": cbase, "out": out})
    return ph, out


FUSED = False


def kernel(**inputs):
    if FUSED:
        return _kernel_fused(**inputs)
    return _kernel_unfused(**inputs)


def _kernel_fused(**inputs):
    x = np.ascontiguousarray(np.asarray(inputs["L0_attn_hidden_states"])[0])
    ph, _ = build_fused()
    A = lambda k: np.asarray(inputs[k])
    C = np.ascontiguousarray
    FS = FFN // NCORES
    in_maps = []
    for c in range(NCORES):
        h256 = slice(c * 256, (c + 1) * 256)
        h192 = slice(c * 192, (c + 1) * 192)
        h64 = slice(c * 64, (c + 1) * 64)
        m = {
            "x": C(x[c * PT:(c + 1) * PT]),
            "wq0": C(A("L0_attn_Wq")[:, h256]),
            "wk0": C(A("L0_attn_Wk")[:, h256]),
            "wi0": C(A("L0_attn_Wi")[:, h256]),
            "wo0": A("L0_attn_Wo"), "gw0": A("L0_attn_g_weight"),
            "mrw": A("L0_ffn_router_w"), "mrb": A("L0_ffn_router_b"),
            "mew": A("L0_ffn_expert_w"), "meb": A("L0_ffn_expert_b"),
            "mow": A("L0_ffn_out_w"), "mob": A("L0_ffn_out_b"),
            "wdkv": A("L1_attn_W_dkv"), "bdkv": A("L1_attn_b_dkv"),
            "wdq": A("L1_attn_W_dq"), "bdq": A("L1_attn_b_dq"),
            "wrl1": A("L1_ffn_w_router"),
            "wuk": C(A("L1_attn_W_uk")[:, h192]), "buk": C(A("L1_attn_b_uk")[h192]),
            "wuv": C(A("L1_attn_W_uv")[:, h256]), "buv": C(A("L1_attn_b_uv")[h256]),
            "wuq": C(A("L1_attn_W_uq")[:, h192]), "buq": C(A("L1_attn_b_uq")[h192]),
            "wqr": C(A("L1_attn_W_qr")[:, h64]), "bqr": C(A("L1_attn_b_qr")[h64]),
            "wkr": C(A("L1_attn_W_kr")[:, h64]), "bkr": C(A("L1_attn_b_kr")[h64]),
            "iotaj": (np.arange(128)[None, :] +
                      128.0 * c * np.ones((128, 1))).astype(np.float32),
            "g1m": A("L1_ffn_g1"),
            "wqm": C(A("L1_ffn_wq")[:, h256]), "wkm": C(A("L1_ffn_wk")[:, h256]),
            "wvm": C(A("L1_ffn_wv")[:, h256]), "wom": C(A("L1_ffn_wo")[h256, :]),
            "freqs": A("L1_ffn_freqs_cis"), "maskm": A("L1_ffn_mask"),
            "g2m": A("L1_ffn_g2"),
            "w1c": C(A("L1_ffn_w1")[:, c * FS:(c + 1) * FS]),
            "w3c": C(A("L1_ffn_w3")[:, c * FS:(c + 1) * FS]),
            "w2r": C(A("L1_ffn_w2")[c * FS:(c + 1) * FS, :]),
            "wout": A("L1_attn_W_out"), "bout": A("L1_attn_b_out"),
            "cbase": np.array([-PT * c], np.float32),
        }
        in_maps.append(m)
    res = _run(ph, in_maps)
    out = np.concatenate([np.asarray(r["out"], np.float32) for r in res], 0)
    return out[None]


def _kernel_unfused(**inputs):
    x = np.ascontiguousarray(np.asarray(inputs["L0_attn_hidden_states"])[0])
    f32 = np.float32

    def rows(a):
        return [np.ascontiguousarray(s) for s in np.split(a, NCORES, 0)]

    def cols(a, n=NCORES):
        return [np.ascontiguousarray(s) for s in np.split(a, n, 1)]

    ph = build_ln0()
    res = _run(ph, [{"x": s} for s in rows(x)])
    xn_tm = [r["xn_tm"] for r in res]
    xn_fm_loc = [r["xn_fm"] for r in res]
    ag_xn_fm = np.concatenate(xn_fm_loc, 0)

    ph = build_gla(src_blk=PT)
    Wq, Wk, Wi = (np.asarray(inputs["L0_attn_" + n]) for n in ("Wq", "Wk", "Wi"))
    in_maps = []
    for c in range(NCORES):
        hs = slice(c * 256, (c + 1) * 256)
        in_maps.append({"xn_fm": ag_xn_fm,
                        "wq": np.ascontiguousarray(Wq[:, hs]),
                        "wk": np.ascontiguousarray(Wk[:, hs]),
                        "wi": np.ascontiguousarray(Wi[:, hs])})
    res = _run(ph, in_maps)
    o_percore = [r["o_tm"] for r in res]

    ph = build_glaep(nrk=NCORES)
    res = _run(ph, [{"o": np.concatenate(
                        [o_percore[r][c * PT:(c + 1) * PT, :] for r in range(NCORES)], 0),
                     "wo": np.asarray(inputs["L0_attn_Wo"]),
                     "gw": np.asarray(inputs["L0_attn_g_weight"])}
                    for c in range(NCORES)])
    s_tm = [r["s_tm"] for r in res]

    ph = build_moe()
    res = _run(ph, [{"xn_fm": xn_fm_loc[c],
                     "rw": np.asarray(inputs["L0_ffn_router_w"]),
                     "rb": np.asarray(inputs["L0_ffn_router_b"]),
                     "ew": np.asarray(inputs["L0_ffn_expert_w"]),
                     "eb": np.asarray(inputs["L0_ffn_expert_b"]),
                     "ow": np.asarray(inputs["L0_ffn_out_w"]),
                     "ob": np.asarray(inputs["L0_ffn_out_b"])}
                    for c in range(NCORES)])
    g_tm = [r["g_tm"] for r in res]

    ph = build_combine_ln()
    res = _run(ph, [{"x": xs, "s_in": ss, "g_in": gs}
                    for xs, ss, gs in zip(rows(x), s_tm, g_tm)])
    x1 = [r["x1"] for r in res]
    xn1_tm = [r["xn_tm"] for r in res]
    xn1_fm_loc = [r["xn_fm"] for r in res]
    ag_xn1_fm = np.concatenate(xn1_fm_loc, 0)
    xn1_tm_full = np.concatenate(xn1_tm, 0)

    ph = build_l1proj()
    res = _run(ph, [{"xn_fm": xn1_fm_loc[c],
                     "wdkv": np.asarray(inputs["L1_attn_W_dkv"]),
                     "bdkv": np.asarray(inputs["L1_attn_b_dkv"]),
                     "wdq": np.asarray(inputs["L1_attn_W_dq"]),
                     "bdq": np.asarray(inputs["L1_attn_b_dq"]),
                     "wr": np.asarray(inputs["L1_ffn_w_router"])}
                    for c in range(NCORES)])
    ag_ckv = np.concatenate([r["ckv"] for r in res], 0)
    ag_cq = np.concatenate([r["cq"] for r in res], 0)
    logits_full = np.concatenate([np.asarray(r["logits"]) for r in res])

    ph = build_l1flash(src_blk=PT)
    in_maps = []
    for c in range(NCORES):
        h2 = slice(c * 192, (c + 1) * 192)
        hv = slice(c * 256, (c + 1) * 256)
        hr = slice(c * 64, (c + 1) * 64)
        in_maps.append({
            "ckv": ag_ckv, "cqf": ag_cq, "xn_fm": ag_xn1_fm,
            "wuk": np.ascontiguousarray(np.asarray(inputs["L1_attn_W_uk"])[:, h2]),
            "buk": np.ascontiguousarray(np.asarray(inputs["L1_attn_b_uk"])[h2]),
            "wuv": np.ascontiguousarray(np.asarray(inputs["L1_attn_W_uv"])[:, hv]),
            "buv": np.ascontiguousarray(np.asarray(inputs["L1_attn_b_uv"])[hv]),
            "wuq": np.ascontiguousarray(np.asarray(inputs["L1_attn_W_uq"])[:, h2]),
            "buq": np.ascontiguousarray(np.asarray(inputs["L1_attn_b_uq"])[h2]),
            "wqr": np.ascontiguousarray(np.asarray(inputs["L1_attn_W_qr"])[:, hr]),
            "bqr": np.ascontiguousarray(np.asarray(inputs["L1_attn_b_qr"])[hr]),
            "wkr": np.ascontiguousarray(np.asarray(inputs["L1_attn_W_kr"])[:, hr]),
            "bkr": np.ascontiguousarray(np.asarray(inputs["L1_attn_b_kr"])[hr]),
        })
    res = _run(ph, in_maps)
    o1_percore = [r["o_tm"] for r in res]

    ph = build_l1sel()
    res = _run(ph, [{"logits": logits_full, "xn1_tm": xn1_tm_full,
                     "iotaj": (np.arange(128)[None, :] +
                               128 * c * np.ones((128, 1))).astype(np.float32)
                     + 0.0}
                    for c in range(NCORES)])
    fx_full = np.concatenate([r["fx"] for r in res], 0)
    selidx = np.concatenate([r["selidx"] for r in res])
    rw_full = np.concatenate([r["rw"] for r in res])

    ph = build_mini1()
    in_maps = []
    for c in range(NCORES):
        hv = slice(c * 256, (c + 1) * 256)
        in_maps.append({
            "fx_full": fx_full, "g1": np.asarray(inputs["L1_ffn_g1"]),
            "wq": np.ascontiguousarray(np.asarray(inputs["L1_ffn_wq"])[:, hv]),
            "wk": np.ascontiguousarray(np.asarray(inputs["L1_ffn_wk"])[:, hv]),
            "wv": np.ascontiguousarray(np.asarray(inputs["L1_ffn_wv"])[:, hv]),
            "wo_r": np.ascontiguousarray(np.asarray(inputs["L1_ffn_wo"])[hv, :]),
            "freqs": np.asarray(inputs["L1_ffn_freqs_cis"]),
            "mask": np.asarray(inputs["L1_ffn_mask"]),
        })
    res = _run(ph, in_maps)
    part1 = np.stack([r["part1"] for r in res], 0)

    ph = build_mini2()
    res = _run(ph, [{"parts": np.ascontiguousarray(
                        part1[:, c * 128:(c + 1) * 128, :]),
                     "fx_rows": np.ascontiguousarray(
                        fx_full[c * 128:(c + 1) * 128])}
                    for c in range(NCORES)])
    xb_rows = [r["xb"] for r in res]
    h2_full = np.concatenate([r["h2"] for r in res], 0)

    ph = build_mini3()
    FS = FFN // NCORES
    res = _run(ph, [{"h2_full": h2_full, "g2": np.asarray(inputs["L1_ffn_g2"]),
                     "w1c": np.ascontiguousarray(
                        np.asarray(inputs["L1_ffn_w1"])[:, c * FS:(c + 1) * FS]),
                     "w3c": np.ascontiguousarray(
                        np.asarray(inputs["L1_ffn_w3"])[:, c * FS:(c + 1) * FS]),
                     "w2r": np.ascontiguousarray(
                        np.asarray(inputs["L1_ffn_w2"])[c * FS:(c + 1) * FS, :])}
                    for c in range(NCORES)])
    part2 = np.stack([r["part2"] for r in res], 0)

    ph = build_mini4()
    res = _run(ph, [{"parts2": np.ascontiguousarray(
                        part2[:, c * 128:(c + 1) * 128, :]),
                     "xb_rows": xb_rows[c],
                     "rw_rows": np.ascontiguousarray(
                        np.asarray(rw_full)[c * 128:(c + 1) * 128])}
                    for c in range(NCORES)])
    xw_full = np.concatenate([r["xw"] for r in res], 0)

    ph = build_final()
    res = _run(ph, [{"x1": x1[c], "xn1_tm": xn1_tm[c],
                     "o_a2a": np.concatenate(
                        [o1_percore[r][c * PT:(c + 1) * PT, :]
                         for r in range(NCORES)], 0),
                     "wout": np.asarray(inputs["L1_attn_W_out"]),
                     "bout": np.asarray(inputs["L1_attn_b_out"]),
                     "xw_full": xw_full, "selidx": selidx,
                     "cbase": np.array([-PT * c], np.float32)}
                    for c in range(NCORES)])
    out = np.concatenate([np.asarray(r["out"], f32) for r in res], 0)
    return out[None]
